# version 03
# baseline (speedup 1.0000x reference)
import numpy as np
import concourse.bass as bass
import concourse.mybir as mybir
from concourse.bass_utils import run_bass_kernel_spmd

F32 = mybir.dt.float32
BF16 = mybir.dt.bfloat16
AF = mybir.ActivationFunctionType
ALU = mybir.AluOpType
AX = mybir.AxisListType.X

CBIT = 4992.0
EXP_BIAS = -(CBIT + 5.0)
LN_EPS = 1e-5

ENGS = ("pe", "act", "ve", "sp", "gp")


class CFG:
    def __init__(self, L=8192, NH=16, HF=2048):
        self.L = L
        self.NH = NH
        self.D = NH * 64
        self.NQ = L // 2
        self.HF = HF
        self.DC = self.D // 128
        self.NJT = self.D // 128
        self.NKT = L // 128
        self.NQC = self.NQ // 512
        self.NTT = self.NQ // 128


class Emitter:
    def __init__(self, nc):
        self.nc = nc
        self.streams = {e: [] for e in ENGS}
        self.sems = {}
        self.cnt = {}

    def sem(self, name):
        if name not in self.sems:
            self.sems[name] = self.nc.alloc_semaphore(name)
            self.cnt[name] = 0
        return self.sems[name]

    def val(self, name):
        self.sem(name)
        return self.cnt[name]

    def op(self, eng, fn, inc=None, n=1):
        if inc is not None:
            h = self.sem(inc)
            self.cnt[inc] += n

            def wrapped(E, fn=fn, h=h, n=n):
                fn(E).then_inc(h, n)
            self.streams[eng].append(wrapped)
        else:
            self.streams[eng].append(fn)

    def wait(self, eng, name, value=None):
        if value is None:
            value = self.val(name)
        if value <= 0:
            return
        h = self.sem(name)
        self.streams[eng].append(lambda E, h=h, v=value: E.wait_ge(h, v))

    def run(self, block):
        for eng, meth in (("pe", block.tensor), ("act", block.scalar),
                          ("ve", block.vector), ("sp", block.sync),
                          ("gp", block.gpsimd)):
            fns = self.streams[eng]
            if fns:
                meth(lambda E, fns=fns: [f(E) for f in fns])
            self.streams[eng] = []


def build(cfg, debug=()):
    nc = bass.Bass("TRN2", target_bir_lowering=False)
    em = Emitter(nc)
    L, NH, D, NQ, HF = cfg.L, cfg.NH, cfg.D, cfg.NQ, cfg.HF
    DC, NJT, NKT, NQC, NTT = cfg.DC, cfg.NJT, cfg.NKT, cfg.NQC, cfg.NTT
    NHT = HF // 128
    NQUARTER = 4
    QL = L // NQUARTER
    QT_TILES = QL // 128
    QTC = QL // 512
    assert QTC >= 1 and NQC >= 1
    NHX = 128
    VHW = min(512, D)
    NVH = D // VHW
    BCW = min(512, D)

    x_d = nc.dram_tensor("x", [L, D], F32, kind="ExternalInput").ap()
    qkw_d = nc.dram_tensor("qk_w", [D, D], F32, kind="ExternalInput").ap()
    qkb_d = nc.dram_tensor("qk_b", [D], F32, kind="ExternalInput").ap()
    vw_d = nc.dram_tensor("v_w", [D, D], F32, kind="ExternalInput").ap()
    vb_d = nc.dram_tensor("v_b", [D], F32, kind="ExternalInput").ap()
    ha_d = nc.dram_tensor("hash_a", [66], F32, kind="ExternalInput").ap()
    w1_d = nc.dram_tensor("w1", [D, HF], F32, kind="ExternalInput").ap()
    b1_d = nc.dram_tensor("b1", [HF], F32, kind="ExternalInput").ap()
    w2_d = nc.dram_tensor("w2", [HF, D], F32, kind="ExternalInput").ap()
    b2_d = nc.dram_tensor("b2", [D], F32, kind="ExternalInput").ap()
    out_d = nc.dram_tensor("out", [NQ, D], F32, kind="ExternalOutput").ap()

    qkT_s = nc.dram_tensor("qkT_s", [D, L], BF16, kind="Internal").ap()
    xbf_s = nc.dram_tensor("xbf_s", [L, D], BF16, kind="Internal").ap()
    qkw_bf = nc.dram_tensor("qkw_bf", [D, D], BF16, kind="Internal").ap()
    vw_bf = nc.dram_tensor("vw_bf", [D, D], BF16, kind="Internal").ap()
    v_s = nc.dram_tensor("v_s", [L, D], BF16, kind="Internal").ap()
    ctx_s = nc.dram_tensor("ctx_s", [NH * NQ, 65], BF16, kind="Internal").ap()
    bits_s = nc.dram_tensor("bits_s", [NH, L + NQ], BF16, kind="Internal").ap()

    dbg = {}
    if "qkT" in debug:
        dbg["qkT"] = nc.dram_tensor("dbg_qkT", [D, L], F32, kind="ExternalOutput").ap()
    if "stats" in debug:
        dbg["dot1"] = nc.dram_tensor("dbg_dot1", [NH, L], F32, kind="ExternalOutput").ap()
        dbg["n2"] = nc.dram_tensor("dbg_n2", [NH, L], F32, kind="ExternalOutput").ap()
        dbg["bits"] = nc.dram_tensor("dbg_bits", [NH, L + NQ], F32, kind="ExternalOutput").ap()
    if "v" in debug:
        dbg["v"] = nc.dram_tensor("dbg_v", [L, D], F32, kind="ExternalOutput").ap()
    if "ctx" in debug:
        dbg["ctx"] = nc.dram_tensor("dbg_ctx", [NH * NQ, 65], F32, kind="ExternalOutput").ap()
    if "xn" in debug:
        dbg["xn"] = nc.dram_tensor("dbg_xn", [D, NQ], F32, kind="ExternalOutput").ap()

    guards = []

    def AL(name, shape, dtype, side="left"):
        g = nc.sbuf_tensor(name, shape, dtype, side=side)
        t = g.__enter__()
        guards.append((name, g))
        return t

    def FREE(names):
        for name in reversed(names):
            for i in range(len(guards) - 1, -1, -1):
                if guards[i][0] == name:
                    guards[i][1].__exit__(None, None, None)
                    guards.pop(i)
                    break

    ident = AL("ident", [128, 128], F32)
    expb = AL("expb", [128, 1], F32)
    lneps = AL("lneps", [128, 1], F32)
    st = AL("st", [NH, 16], F32)
    a23 = AL("a23", [NH, 2], F32)
    qkb_sb = AL("qkb_sb", [128, NJT], F32)
    b1_sb = AL("b1_sb", [128, NHT], F32)
    b2_sb = AL("b2_sb", [128, NJT], F32)

    dbufs = {}
    if "qkT" in debug:
        dbufs["dbq"] = AL("dbq", [128, L], F32)
    if "v" in debug:
        dbufs["dbv"] = AL("dbv", [128, D], F32)
    if "ctx" in debug:
        dbufs["dbc"] = AL("dbc", [128, NTT * 65], F32)

    dot1 = AL("dot1", [NH, L], F32)
    n2 = AL("n2", [NH, L], F32)
    bits = AL("bits", [NH, L + NQ], BF16)

    qk_wT = AL("qk_wT", [128, DC * D], BF16, side="right")
    v_wT = AL("v_wT", [128, DC * D], BF16, side="right")
    vbb_sb = AL("vbb_sb", [128, D], BF16, side="right")
    a1sel = AL("a1sel", [128, NJT * NH], F32, side="right")
    n1sel = AL("n1sel", [128, NJT * NH], BF16, side="right")
    onesb = AL("onesb", [1, 128], BF16, side="right")
    vbrow = AL("vbrow", [1, D], BF16, side="right")
    onesf = AL("onesf", [1, NH], F32, side="right")

    psA = nc.alloc_psum_tensor("psA", [128, 1536], F32)
    psB = nc.alloc_psum_tensor("psB", [128, 1536], F32)
    psC = nc.alloc_psum_tensor("psC", [128, 512], F32)
    psT = nc.alloc_psum_tensor("psT", [128, 512], F32)


    def emit_phase_w():
        em.op("gp", lambda E: E.memset(ident[:, :], 0.0), inc="gp_i0")
        em.wait("gp", "gp_i0")
        em.op("gp", lambda E: E.affine_select(
            out=ident[:, :], in_=ident[:, :], compare_op=ALU.not_equal,
            fill=1.0, base=0, pattern=[[-1, 128]], channel_multiplier=1), inc="gp_init")
        em.op("sp", lambda E: E.dma_start(
            qkb_sb[:, 0:NJT], qkb_d.rearrange("(j p) -> p j", p=128),
            allow_slow_non_contiguous=True), inc="sp_w", n=16)
        em.op("sp", lambda E: E.dma_start(
            b1_sb[:, 0:NHT], b1_d.rearrange("(t p) -> p t", p=128),
            allow_slow_non_contiguous=True), inc="sp_w", n=16)
        em.op("sp", lambda E: E.dma_start(
            b2_sb[:, 0:NJT], b2_d.rearrange("(t p) -> p t", p=128),
            allow_slow_non_contiguous=True), inc="sp_w", n=16)
        em.op("ve", lambda E: E.memset(expb[:, :], EXP_BIAS), inc="ve_w")
        em.op("ve", lambda E: E.memset(lneps[:, :], LN_EPS), inc="ve_w")
        em.op("ve", lambda E: E.memset(a1sel[:, :], 0.0), inc="ve_w")
        em.op("ve", lambda E: E.memset(n1sel[:, :], 0.0), inc="ve_w")
        em.wait("ve", "ve_w", em.val("ve_w"))
        for jt in range(NJT):
            c0 = jt * NH + 2 * jt
            em.wait("sp", "ve_w", em.val("ve_w"))
            em.op("sp", lambda E, c0=c0: E.dma_start(
                a1sel[0:64, c0:c0 + 1],
                ha_d[0:64].rearrange("(p o) -> p o", o=1),
                allow_slow_non_contiguous=True), inc="sp_w", n=16)
            em.op("sp", lambda E, c0=c0: E.dma_start(
                a1sel[64:128, c0 + 1:c0 + 2],
                ha_d[0:64].rearrange("(p o) -> p o", o=1),
                allow_slow_non_contiguous=True), inc="sp_w", n=16)
            em.op("ve", lambda E, c0=c0: E.memset(n1sel[0:64, c0:c0 + 1], 1.0))
            em.op("ve", lambda E, c0=c0: E.memset(n1sel[64:128, c0 + 1:c0 + 2], 1.0),
                  inc="ve_w")
        em.op("ve", lambda E: E.memset(onesb[:, :], 1.0), inc="ve_w")
        em.op("sp", lambda E: E.dma_start(
            st[0:1, 0:2], ha_d[64:66].rearrange("(o c) -> o c", o=1)), inc="sp_w", n=16)
        em.op("gp", lambda E: E.dma_start(
            vbrow[0:1, :], vb_d.rearrange("(o c) -> o c", o=1)), inc="gp_w", n=16)
        em.wait("pe", "gp_init")
        em.wait("pe", "ve_w")
        em.wait("pe", "sp_w")
        em.wait("pe", "gp_w")
        em.op("ve", lambda E: E.memset(onesf[:, :], 1.0), inc="ve_w")
        em.wait("pe", "ve_w")
        em.op("pe", lambda E: E.matmul(psC[0:NH, 0:2], onesf[0:1, 0:NH],
                                       st[0:1, 0:2], start=True, stop=True), inc="pe_w")
        em.wait("ve", "pe_w", em.val("pe_w"))
        em.op("ve", lambda E: E.tensor_copy(a23[:, :], psC[0:NH, 0:2]), inc="ve_a23")
        for half in range(D // BCW):
            em.wait("pe", "ve_vbb", em.val("ve_vbb") if half > 0 else 0)
            em.wait("pe", "ve_a23", em.val("ve_a23"))
            em.op("pe", lambda E, h=half: E.matmul(
                psC[:, 0:BCW], onesb[0:1, 0:128],
                vbrow[0:1, h * BCW:(h + 1) * BCW], start=True, stop=True), inc="pe_w")
            em.wait("ve", "pe_w", em.val("pe_w"))
            em.op("ve", lambda E, h=half: E.tensor_copy(
                vbb_sb[:, h * BCW:(h + 1) * BCW], psC[:, 0:BCW]), inc="ve_vbb")
        em.op("gp", lambda E: E.dma_start(qkw_bf, qkw_d), inc="gp_wbf", n=16)
        em.op("gp", lambda E: E.dma_start(vw_bf, vw_d), inc="gp_wbf", n=16)
        em.op("gp", lambda E: E.dma_start(xbf_s, x_d), inc="gp_xbf", n=16)
        em.wait("sp", "gp_wbf", em.val("gp_wbf"))
        for wi, (src, dstT) in enumerate(((qkw_bf, qk_wT), (vw_bf, v_wT))):
            for dc in range(DC):
                em.op("sp", lambda E, s=src, d=dstT, dc=dc: E.dma_start(
                    d[:, dc * D:(dc + 1) * D], s[:, dc * 128:(dc + 1) * 128],
                    transpose=True), inc="sp_wt", n=16)

    xT = AL("xT", [128, DC * QL], BF16, side="right")
    qkf = AL("qkf", [128, 2 * 512], F32, side="right")
    sqb = AL("sqb", [128, 2 * 512], BF16, side="right")
    qstage = AL("qstage", [128, 2 * NJT * 512], BF16, side="right")
    vstage = AL("vstage", [128, 2 * D], BF16, side="right")

    def emit_phase_p():
        em.wait("act", "sp_w", em.val("sp_w"))
        em.wait("act", "ve_w", em.val("ve_w"))
        mm_slots = [psA[:, 1024:1536], psB[:, 0:512]]
        d1_slots = [psB[:, 512:1024], psB[:, 1024:1536]]
        n2_slots = [psC[:, 0:512], psT[:, 0:512]]
        em.wait("pe", "ve_vbb", em.val("ve_vbb"))
        em.wait("sp", "gp_xbf", em.val("gp_xbf"))
        for q in range(NQUARTER):
            if q > 0:
                em.wait("sp", "pe_qk", em.val("pe_qk"))
                em.wait("sp", "pe_v", em.val("pe_v"))
            for dc in range(DC):
                em.op("sp", lambda E, q=q, dc=dc: E.dma_start(
                    xT[:, dc * QL:(dc + 1) * QL],
                    xbf_s[q * QL:(q + 1) * QL, dc * 128:(dc + 1) * 128],
                    transpose=True), inc="sp_xt", n=16)
            em.wait("pe", "sp_xt", em.val("sp_xt"))
            em.wait("pe", "sp_wt", em.val("sp_wt"))
            em.wait("pe", "ve_v", em.val("ve_v"))
            for tc in range(QTC):
                gi = q * QTC + tc
                d1b = d1_slots[gi % 2]
                n2b = n2_slots[gi % 2]
                def stats_mms(jt, d1b, n2b, qf, sq):
                    em.wait("pe", "act_qk", em.val("act_qk"))
                    em.op("pe", lambda E, d1b=d1b, jt=jt, qf=qf: E.matmul(
                        d1b[0:NH, :], a1sel[:, jt * NH:jt * NH + NH], qf,
                        start=(jt == 0), stop=(jt == NJT - 1)), inc="pe_d1")
                    em.wait("pe", "ve_sq", em.val("ve_sq"))
                    em.op("pe", lambda E, n2b=n2b, jt=jt, sq=sq: E.matmul(
                        n2b[0:NH, :], n1sel[:, jt * NH:jt * NH + NH], sq,
                        start=(jt == 0), stop=(jt == NJT - 1)), inc="pe_n2")

                pending = None
                for jt in range(NJT):
                    mi = gi * NJT + jt
                    mmb = mm_slots[mi % 2]
                    if mi >= 2:
                        em.wait("pe", "act_qk", em.val("act_qk") - 1)
                    if jt == 0 and gi >= 2:
                        em.wait("pe", "ve_d1", em.val("ve_d1") - 1)
                        em.wait("pe", "ve_n2", em.val("ve_n2") - 1)
                    for dc in range(DC):
                        em.op("pe", lambda E, m=mmb, dc=dc, jt=jt, q=q, tc=tc: E.matmul(
                            m, qk_wT[:, (dc * NJT + jt) * 128:(dc * NJT + jt) * 128 + 128],
                            xT[:, dc * QL + tc * 512: dc * QL + tc * 512 + 512],
                            start=(dc == 0), stop=(dc == DC - 1)),
                            inc=("pe_qk" if dc == DC - 1 else None))
                    if pending is not None:
                        stats_mms(*pending)
                    qf = qkf[:, (mi % 2) * 512:(mi % 2) * 512 + 512]
                    sq = sqb[:, (mi % 2) * 512:(mi % 2) * 512 + 512]
                    em.wait("act", "pe_qk", em.val("pe_qk"))
                    if mi >= 2:
                        em.wait("act", "pe_d1", em.val("pe_d1"))
                    em.op("act", lambda E, m=mmb, qf=qf, jt=jt: E.activation(
                        qf, m, AF.Identity, bias=qkb_sb[:, jt:jt + 1], scale=1.0),
                        inc="act_qk")
                    em.wait("ve", "act_qk", em.val("act_qk"))
                    if mi >= 2:
                        em.wait("ve", "pe_n2", em.val("pe_n2"))
                    em.op("ve", lambda E, sq=sq, qf=qf: E.tensor_mul(sq, qf, qf),
                          inc="ve_sq")
                    stg = qstage[:, (gi % 2) * NJT * 512 + jt * 512:
                                 (gi % 2) * NJT * 512 + jt * 512 + 512]
                    if gi >= 2 and jt == 0:
                        em.wait("ve", "gp_qs%d" % (gi % 2), em.val("gp_qs%d" % (gi % 2)))
                    em.op("ve", lambda E, stg=stg, qf=qf: E.tensor_copy(stg, qf),
                          inc="ve_qs")
                    pending = (jt, d1b, n2b, qf, sq)
                stats_mms(*pending)
                em.wait("ve", "pe_d1", em.val("pe_d1"))
                em.wait("ve", "pe_n2", em.val("pe_n2"))
                cs = q * QL + tc * 512
                em.op("ve", lambda E, d1b=d1b, cs=cs: E.tensor_copy(
                    dot1[:, cs:cs + 512], d1b[0:NH, :]), inc="ve_d1")
                em.op("ve", lambda E, n2b=n2b, cs=cs: E.tensor_copy(
                    n2[:, cs:cs + 512], n2b[0:NH, :]), inc="ve_n2")
                em.wait("gp", "ve_qs", em.val("ve_qs"))
                src = qstage[:, (gi % 2) * NJT * 512:(gi % 2) * NJT * 512 + NJT * 512]
                dst = qkT_s.rearrange("(jt p) Lx -> p jt Lx", p=128)[:, :, cs:cs + 512]
                em.op("gp", lambda E, s=src, d=dst: E.dma_start(
                    d, s.rearrange("p (jt c) -> p jt c", c=512)),
                    inc="gp_qs%d" % (gi % 2), n=16)
            em.wait("pe", "act_qk", em.val("act_qk"))
            for t in range(QT_TILES):
                gt = q * QT_TILES + t
                vs = vstage[:, (gt % 2) * D:(gt % 2) * D + D]
                if gt >= 2:
                    em.wait("ve", "gp_vs%d" % (gt % 2), em.val("gp_vs%d" % (gt % 2)))
                for half in range(NVH):
                    vi = gt * NVH + half
                    mmb = mm_slots[vi % 2]
                    em.wait("pe", "ve_v", max(0, em.val("ve_v") - 1))
                    for dc in range(DC):
                        em.op("pe", lambda E, m=mmb, dc=dc, t=t, h=half: E.matmul(
                            m[:, 0:VHW],
                            xT[:, dc * QL + t * 128: dc * QL + t * 128 + 128],
                            v_wT[:, dc * D + h * VHW: dc * D + h * VHW + VHW],
                            start=(dc == 0), stop=(dc == DC - 1)),
                            inc=("pe_v" if dc == DC - 1 else None))
                    em.wait("ve", "pe_v", em.val("pe_v"))
                    em.op("ve", lambda E, m=mmb, vs=vs, h=half: E.tensor_tensor(
                        vs[:, h * VHW:h * VHW + VHW], m[:, 0:VHW],
                        vbb_sb[:, h * VHW:h * VHW + VHW], ALU.add), inc="ve_v")
                em.wait("gp", "ve_v", em.val("ve_v"))
                em.op("gp", lambda E, vs=vs, q=q, t=t: E.dma_start(
                    v_s[q * QL + t * 128: q * QL + t * 128 + 128, :], vs),
                    inc="gp_vs%d" % (gt % 2), n=16)
        if "qkT" in debug:
            db = dbufs["dbq"]
            for jt in range(NJT):
                em.wait("gp", "gp_qs0", em.val("gp_qs0"))
                em.wait("gp", "gp_qs1", em.val("gp_qs1"))
                em.wait("gp", "sp_dbg", em.val("sp_dbg"))
                em.op("gp", lambda E, jt=jt: E.dma_start(
                    db[:, :], qkT_s[jt * 128:(jt + 1) * 128, :]), inc="sp_dbg", n=16)
                em.wait("gp", "sp_dbg")
                em.op("gp", lambda E, jt=jt: E.dma_start(
                    dbg["qkT"][jt * 128:(jt + 1) * 128, :], db[:, :]), inc="sp_dbg", n=16)
        if "v" in debug:
            dbv = dbufs["dbv"]
            for t in range(L // 128):
                em.wait("gp", "gp_vs0", em.val("gp_vs0"))
                em.wait("gp", "gp_vs1", em.val("gp_vs1"))
                em.wait("gp", "sp_dbgv", em.val("sp_dbgv"))
                em.op("gp", lambda E, t=t: E.dma_start(
                    dbv[:, :], v_s[t * 128:(t + 1) * 128, :]), inc="sp_dbgv", n=16)
                em.wait("gp", "sp_dbgv")
                em.op("gp", lambda E, t=t: E.dma_start(
                    dbg["v"][t * 128:(t + 1) * 128, :], dbv[:, :]), inc="sp_dbgv", n=16)

    FREE(["xT", "qkf", "sqb", "qstage", "vstage"])
    FREE(["onesf"])
    FREE(["qk_wT", "v_wT", "vbb_sb", "a1sel", "n1sel", "onesb", "vbrow"])
    hkv = AL("hkv", [NH, L], F32)

    def emit_phase_s():
        V = "ve"
        for nm in ("pe_qk", "pe_v", "pe_d1", "pe_n2", "gp_qs0", "gp_qs1",
                   "gp_vs0", "gp_vs1", "act_qk", "ve_qs", "ve_v", "ve_sq"):
            em.wait(V, nm, em.val(nm))
        em.wait(V, "ve_a23", em.val("ve_a23"))
        c = lambda i: st[:, i:i + 1]
        em.wait(V, "ve_s", em.val("ve_s"))
        em.op(V, lambda E: E.reduce_max(c(0), n2[:, :], axis=AX), inc="ve_s")
        em.wait("act", "ve_s", em.val("ve_s"))
        em.op("act", lambda E: E.activation(c(1), c(0), AF.Sqrt), inc="act_s")
        em.wait(V, "act_s", em.val("act_s"))
        em.wait(V, "ve_s", em.val("ve_s"))
        em.op(V, lambda E: E.tensor_single_scalar(c(2), c(1), 1e-12, ALU.max), inc="ve_s")
        em.wait(V, "ve_s", em.val("ve_s"))
        em.op(V, lambda E: E.reciprocal(c(3), c(2)), inc="ve_s")
        em.wait(V, "ve_s", em.val("ve_s"))
        em.op(V, lambda E: E.tensor_single_scalar(c(2), c(3), 0.75, ALU.mult), inc="ve_s")
        em.wait(V, "ve_s", em.val("ve_s"))
        em.op(V, lambda E: E.tensor_mul(c(4), c(2), c(2)), inc="ve_s")
        em.wait(V, "ve_s", em.val("ve_s"))
        em.op(V, lambda E: E.tensor_scalar_mul(hkv[:, :], n2[:, :], c(4)), inc="ve_s")
        em.wait(V, "ve_s", em.val("ve_s"))
        em.op(V, lambda E: E.tensor_scalar(n2[:, :], hkv[:, :], -1.0, 0.5,
                                           ALU.mult, ALU.add), inc="ve_s")
        em.wait(V, "ve_s", em.val("ve_s"))
        em.op(V, lambda E: E.tensor_mul(hkv[:, :], hkv[:, :], hkv[:, :]), inc="ve_s")
        em.wait(V, "ve_s", em.val("ve_s"))
        em.op(V, lambda E: E.tensor_scalar(hkv[:, :], hkv[:, :], -1.0, 0.5,
                                           ALU.mult, ALU.add), inc="ve_s")
        em.wait(V, "ve_s", em.val("ve_s"))
        em.op(V, lambda E: E.tensor_scalar_mul(hkv[:, :], hkv[:, :], a23[:, 1:2]), inc="ve_s")
        em.wait(V, "ve_s", em.val("ve_s"))
        em.op(V, lambda E: E.scalar_tensor_tensor(
            hkv[:, :], n2[:, :], a23[:, 0:1], hkv[:, :], ALU.mult, ALU.add), inc="ve_s")
        em.wait(V, "ve_s", em.val("ve_s"))
        em.op(V, lambda E: E.scalar_tensor_tensor(
            hkv[:, :], dot1[:, :], c(2), hkv[:, :], ALU.mult, ALU.add), inc="ve_s")
        em.wait(V, "ve_s", em.val("ve_s"))
        em.op(V, lambda E: E.tensor_single_scalar(hkv[:, :], hkv[:, :], 0.0, ALU.is_ge),
              inc="ve_s")
        em.wait(V, "ve_s", em.val("ve_s"))
        em.op(V, lambda E: E.tensor_scalar(bits[:, 0:L], hkv[:, :], 128.0, -64.0,
                                           ALU.mult, ALU.add), inc="ve_s")
        em.wait(V, "ve_s", em.val("ve_s"))
        em.op(V, lambda E: E.reduce_max(c(5), hkv[:, :], axis=AX), inc="ve_s")
        em.wait(V, "ve_s", em.val("ve_s"))
        em.op(V, lambda E: E.tensor_reduce(c(6), hkv[:, :], axis=AX, op=ALU.min), inc="ve_s")
        em.wait(V, "ve_s", em.val("ve_s"))
        em.op(V, lambda E: E.tensor_sub(c(7), c(5), c(6)), inc="ve_s")
        em.wait(V, "ve_s", em.val("ve_s"))
        em.op(V, lambda E: E.tensor_add(c(8), c(5), c(6)), inc="ve_s")
        em.wait(V, "ve_s", em.val("ve_s"))
        em.op(V, lambda E: E.tensor_single_scalar(c(8), c(8), 1.0, ALU.subtract),
              inc="ve_s")
        em.wait(V, "ve_s", em.val("ve_s"))
        em.op(V, lambda E: E.tensor_single_scalar(c(9), c(7), 156.0, ALU.mult), inc="ve_s")
        em.wait(V, "ve_s", em.val("ve_s"))
        em.op(V, lambda E: E.tensor_sub(c(10), c(8), c(7)), inc="ve_s")
        em.wait(V, "ve_s", em.val("ve_s"))
        em.op(V, lambda E: E.tensor_single_scalar(c(10), c(10), 78.0, ALU.mult), inc="ve_s")
        em.wait(V, "ve_s", em.val("ve_s"))
        em.op(V, lambda E: E.tensor_single_scalar(dot1[:, 0:NQ], dot1[:, 0:NQ], 0.0,
                                                  ALU.is_ge), inc="ve_s")
        em.wait(V, "ve_s", em.val("ve_s"))
        em.op(V, lambda E: E.tensor_scalar(bits[:, L:L + NQ], dot1[:, 0:NQ], c(9), c(10),
                                           ALU.mult, ALU.add), inc="ve_s")
        em.wait("gp", "ve_s", em.val("ve_s"))
        em.op("gp", lambda E: E.dma_start(bits_s, bits[:, :]), inc="gp_bits", n=16)
        if "stats" in debug:
            em.wait("gp", "ve_s", em.val("ve_s"))
            em.op("gp", lambda E: E.dma_start(dbg["dot1"], dot1[:, :]), inc="sp_dbgs", n=16)
            em.op("gp", lambda E: E.dma_start(dbg["n2"], n2[:, :]), inc="sp_dbgs", n=16)
            em.op("gp", lambda E: E.dma_start(dbg["bits"], bits[:, :]), inc="sp_dbgs", n=16)

    FREE(["hkv"])
    FREE(["dot1", "n2", "bits"])
    KT = AL("KT", [65, 2 * L], BF16)
    QTb = AL("QTb", [65, NQ], BF16)
    vhb = AL("vhb", [128, 2 * NKT * 65], BF16)
    ctxT = AL("ctxT", [65, NQ], F32)
    pt4 = AL("pt4", [128, 4 * 1536], BF16)
    cstage = AL("cstage", [128, NTT * 65], BF16)

    groups = []
    k = 0
    while k < NKT:
        size = min(3, NKT - k)
        groups.append((k, size))
        k += size

    ctx_w = ctx_s.rearrange("(h t p) j -> h p t j", p=128, t=NTT)

    def emit_phase_a():
        em.wait("pe", "ve_d1", em.val("ve_d1"))
        em.wait("pe", "ve_n2", em.val("ve_n2"))
        em.wait("pe", "ve_v", em.val("ve_v"))
        em.wait("pe", "act_qk", em.val("act_qk"))
        em.wait("pe", "evac_xp", em.val("evac_xp"))

        em.wait("act", "ve_w", em.val("ve_w"))
        state = {"exp_at_bank": {0: 0, 1: 0}, "ctx_evac": 0}

        def prefetch(h, into):
            kth = KT[:, into * L: into * L + L]
            em.op("sp", lambda E, h=h, kth=kth: E.dma_start(
                kth[0:64, :], qkT_s[h * 64:(h + 1) * 64, :]),
                inc="sp_kt%d" % into, n=16)
            em.op("gp", lambda E, h=h, kth=kth: E.dma_start(
                kth[64:65, :], bits_s[h:h + 1, 0:L]), inc="gp_kt%d" % into, n=16)
            vb = vhb[:, into * NKT * 65: into * NKT * 65 + NKT * 65]
            dstv = vb.rearrange("p (k j) -> p k j", j=65)[:, :, 0:64]
            srcv = v_s[:, h * 64:(h + 1) * 64].rearrange("(k p) j -> p k j", p=128)
            em.op("gp", lambda E, d=dstv, s=srcv: E.dma_start(d, s),
                inc="gp_vh%d" % into, n=16)

        em.wait("ve", "sp_dbgs", em.val("sp_dbgs"))
        em.op("ve", lambda E: E.memset(vhb[:, :], 1.0), inc="ve_ones")
        em.wait("gp", "ve_ones", em.val("ve_ones"))
        em.wait("gp", "gp_bits", em.val("gp_bits"))
        em.wait("sp", "ve_s", em.val("ve_s"))
        em.wait("sp", "gp_bits", em.val("gp_bits"))
        em.wait("sp", "sp_dbgs", em.val("sp_dbgs"))
        em.wait("gp", "sp_dbgs", em.val("sp_dbgs"))
        em.wait("sp", "gp_qs0", em.val("gp_qs0"))
        em.wait("sp", "gp_qs1", em.val("gp_qs1"))
        em.wait("gp", "gp_vs0", em.val("gp_vs0"))
        em.wait("gp", "gp_vs1", em.val("gp_vs1"))
        prefetch(0, 0)
        prefetch(1, 1)

        ct_at_qc = [0] * NQC
        for h in range(NH):
            par = h % 2
            kth = KT[:, par * L: par * L + L]
            vb = vhb[:, par * NKT * 65: par * NKT * 65 + NKT * 65]
            em.wait("ve", "sp_kt%d" % par, em.val("sp_kt%d" % par))
            em.wait("ve", "gp_kt%d" % par, em.val("gp_kt%d" % par))
            em.wait("ve", "pe_s", em.val("pe_s"))
            em.op("ve", lambda E, kth=kth: E.tensor_scalar_mul(
                QTb[0:64, :], kth[0:64, 0:NQ], 0.125), inc="ve_qt")
            em.wait("gp", "pe_s", em.val("pe_s"))
            em.op("gp", lambda E, h=h: E.dma_start(
                QTb[64:65, :], bits_s[h:h + 1, L:L + NQ]), inc="gp_qt", n=16)
            em.wait("pe", "ve_qt", em.val("ve_qt"))
            em.wait("pe", "gp_qt", em.val("gp_qt"))
            em.wait("pe", "gp_vh%d" % par, em.val("gp_vh%d" % par))
            em.wait("pe", "sp_kt%d" % par, em.val("sp_kt%d" % par))
            em.wait("pe", "gp_kt%d" % par, em.val("gp_kt%d" % par))

            evac_at_qc = [0] * NQC

            def epilogue_block(b, h):
                em.wait("pe", "ve_ctx", evac_at_qc[b])
                em.wait("pe", "ve_cs", em.val("ve_cs"))
                for j in range(4):
                    tt = b * 4 + j
                    em.op("pe", lambda E, j=j, tt=tt: E.transpose(
                        psT[:, j * 80:j * 80 + 65], ctxT[:, tt * 128:tt * 128 + 128],
                        ident[0:65, 0:65]), inc=("pe_ct" if j == 3 else None))
                ct_at_qc[b] = em.val("pe_ct")
                em.wait("ve", "pe_ct", em.val("pe_ct"))
                if b == 0:
                    em.wait("ve", "gp_cs", em.val("gp_cs"))
                em.op("ve", lambda E, b=b: E.tensor_copy(
                    cstage[:, b * 4 * 65:(b + 1) * 4 * 65].rearrange(
                        "p (t j) -> p t j", j=65),
                    psT[:, 0:320].rearrange("p (t j) -> p t j", j=80)[:, :, 0:65]),
                    inc="ve_cs")

            for qc in range(NQC):
                exp_at_g = []

                def av_group(j):
                    k0, size = groups[j]
                    ptb = pt4[:, (j % 4) * 1536:(j % 4) * 1536 + 1536]
                    em.wait("pe", "act_e", exp_at_g[j])
                    if k0 == 0:
                        em.wait("pe", "ve_ctx", state["ctx_evac"])
                    for j2 in range(size):
                        kt = k0 + j2
                        em.op("pe", lambda E, v=vb, kt=kt, p=ptb, j2=j2: E.matmul(
                            psC[0:65, :], v[:, kt * 65:kt * 65 + 65],
                            p[:, j2 * 512:j2 * 512 + 512],
                            start=(kt == 0), stop=(kt == NKT - 1),
                            skip_group_check=True),
                            inc=("pe_av" if kt == NKT - 1 else None))

                for g, (k0, size) in enumerate(groups):
                    bank = psA if g % 2 == 0 else psB
                    bpar = g % 2
                    ptb = pt4[:, (g % 4) * 1536:(g % 4) * 1536 + 1536]
                    em.wait("pe", "act_e", state["exp_at_bank"][bpar])
                    for j in range(size):
                        kt = k0 + j
                        em.op("pe", lambda E, b=bank, j=j, kt=kt, kth=kth, qc=qc: E.matmul(
                            b[:, j * 512:j * 512 + 512],
                            kth[:, kt * 128:kt * 128 + 128],
                            QTb[:, qc * 512:qc * 512 + 512],
                            start=True, stop=True),
                            inc=("pe_s" if j == size - 1 else None))
                    em.wait("act", "pe_s", em.val("pe_s"))
                    em.op("act", lambda E, b=bank, p=ptb, w=size * 512: E.activation(
                        p[:, 0:w], b[:, 0:w], AF.Exp, bias=expb[:, 0:1], scale=1.0),
                        inc="act_e")
                    state["exp_at_bank"][bpar] = em.val("act_e")
                    exp_at_g.append(em.val("act_e"))
                    if g >= 2:
                        av_group(g - 2)
                    if g == 1 and qc >= 1:
                        epilogue_block(qc - 1, h)
                av_group(len(groups) - 2)
                av_group(len(groups) - 1)
                em.wait("ve", "pe_av", em.val("pe_av"))
                em.wait("ve", "pe_ct", ct_at_qc[qc])
                em.op("ve", lambda E, qc=qc: E.tensor_copy(
                    ctxT[:, qc * 512:qc * 512 + 512], psC[0:65, :]), inc="ve_ctx")
                state["ctx_evac"] = em.val("ve_ctx")
                evac_at_qc[qc] = em.val("ve_ctx")

            epilogue_block(NQC - 1, h)
            em.wait("gp", "ve_cs", em.val("ve_cs"))
            em.op("gp", lambda E, h=h: E.dma_start(
                ctx_w[h], cstage[:, :].rearrange("p (t j) -> p t j", j=65)),
                inc="gp_cs", n=16)
            if h + 2 < NH:
                em.wait("sp", "pe_s", em.val("pe_s"))
                em.wait("gp", "pe_s", em.val("pe_s"))
                em.wait("gp", "pe_av", em.val("pe_av"))
                prefetch(h + 2, par)
        if "ctx" in debug:
            dbc = dbufs["dbc"]
            dbg_w = dbg["ctx"].rearrange("(h t p) j -> h p t j", p=128, t=NTT)
            for h in range(NH):
                em.wait("gp", "gp_cs", em.val("gp_cs"))
                em.wait("gp", "sp_dbgc", em.val("sp_dbgc"))
                em.op("gp", lambda E, h=h: E.dma_start(
                    dbc[:, :].rearrange("p (t j) -> p t j", j=65), ctx_w[h]),
                    inc="sp_dbgc", n=16)
                em.wait("gp", "sp_dbgc")
                em.op("gp", lambda E, h=h: E.dma_start(
                    dbg_w[h], dbc[:, :].rearrange("p (t j) -> p t j", j=65)),
                    inc="sp_dbgc", n=16)

    FREE(["KT", "QTb", "vhb", "ctxT", "pt4", "cstage"])
    w1_sb = AL("w1_sb", [128, DC * HF], BF16)
    w2_sb = AL("w2_sb", [128, NHT * D], BF16)
    xres = AL("xres", [128, 2 * D], F32)
    cres = AL("cres", [128, 2 * NH * 65], BF16)
    cres2 = AL("cres2", [128, 2 * D], BF16)
    rcpt = AL("rcpt", [128, 2 * NH], F32)
    zbuf = AL("zbuf", [128, D], F32)
    zzbuf = AL("zzbuf", [128, D], F32)
    lns = AL("lns", [128, 16], F32)
    xnf = AL("xnf", [128, D], F32)
    xnT = AL("xnT", [128, DC * NQ], BF16)
    hT = AL("hT", [128, NHT * 512], BF16)
    z2T = AL("z2T", [128, 2 * 512], F32)
    z2n = AL("z2n", [128, 4 * D], BF16)
    ostage = AL("ostage", [128, 2 * D], F32)

    ctx_r = ctx_s.rearrange("(h t p) j -> t p h j", p=128, t=NTT)

    def emit_ln(zin, outbuf, col):
        s0 = lns[:, col:col + 1]
        s1 = lns[:, col + 1:col + 2]
        s2 = lns[:, col + 2:col + 3]
        W = lambda: em.wait("ve", "ve_ln", em.val("ve_ln"))
        W()
        em.op("ve", lambda E: E.reduce_sum(s0, zin, axis=AX), inc="ve_ln")
        em.op("ve", lambda E: E.scalar_tensor_tensor(
            zzbuf[:, 0:D], zin, 1.0, zin, ALU.mult, ALU.mult, accum_out=s1), inc="ve_ln")
        W()
        em.op("ve", lambda E: E.tensor_single_scalar(s0, s0, 1.0 / D, ALU.mult), inc="ve_ln")
        W()
        em.op("ve", lambda E: E.tensor_single_scalar(s1, s1, 1.0 / D, ALU.mult), inc="ve_ln")
        W()
        em.op("ve", lambda E: E.scalar_tensor_tensor(
            s2, s0, -1.0, s0, ALU.mult, ALU.mult), inc="ve_ln")
        W()
        em.op("ve", lambda E: E.tensor_add(s1, s1, s2), inc="ve_ln")
        em.wait("act", "ve_ln", em.val("ve_ln"))
        em.op("act", lambda E: E.activation(s2, s1, AF.Sqrt, bias=lneps[0:NHX, 0:1]), inc="act_ln")
        em.wait("ve", "act_ln", em.val("act_ln"))
        em.op("ve", lambda E: E.reciprocal(s1, s2), inc="ve_ln")
        W()
        em.op("ve", lambda E: E.tensor_scalar(outbuf, zin, s0, s1,
                                              ALU.subtract, ALU.mult), inc="ve_ln")

    def emit_phase_f():
        em.wait("act", "ve_w", em.val("ve_w"))
        em.wait("act", "sp_w", em.val("sp_w"))
        em.wait("pe", "act_e", em.val("act_e"))
        for eng in ("sp", "gp", "ve", "act"):
            em.wait(eng, "pe_s", em.val("pe_s"))
            em.wait(eng, "pe_av", em.val("pe_av"))
            em.wait(eng, "pe_ct", em.val("pe_ct"))
            em.wait(eng, "ve_cs", em.val("ve_cs"))
            em.wait(eng, "ve_ct2", em.val("ve_ct2"))
            for nm in ("sp_kt0", "sp_kt1", "gp_kt0", "gp_kt1", "gp_vh0", "gp_vh1",
                       "gp_qt", "gp_cs"):
                em.wait(eng, nm, em.val(nm))
        em.op("gp", lambda E: E.dma_start(
            w1_sb[:, :].rearrange("p (dc h) -> p dc h", h=HF),
            w1_d.rearrange("(dc p) h -> p dc h", p=128)), inc="gp_wffn", n=16)
        em.op("gp", lambda E: E.dma_start(
            w2_sb[:, :].rearrange("p (ht d) -> p ht d", d=D),
            w2_d.rearrange("(ht p) d -> p ht d", p=128)), inc="gp_wffn", n=16)
        em.wait("pe", "gp_wffn", em.val("gp_wffn"))
        tp_slots = [psA[:, 0:128], psA[:, 512:640]]

        def ln1_tile(t):
                xb = xres[:, (t % 2) * D:(t % 2) * D + D]
                cb = cres[:, (t % 2) * NH * 65:(t % 2) * NH * 65 + NH * 65]
                c2 = cres2[:, (t % 2) * D:(t % 2) * D + D]
                rc = rcpt[:, (t % 2) * NH:(t % 2) * NH + NH]
                if t >= 2:
                    em.wait("sp", "ve_z", em.val("ve_z") - 1)
                em.op("sp", lambda E, xb=xb, t=t: E.dma_start(
                    xb, x_d[t * 128:t * 128 + 128, :]), inc="sp_xr%d" % (t % 2), n=16)
                if t >= 2:
                    em.wait("gp", "ve_rcp", em.val("ve_rcp") - 1)
                em.op("gp", lambda E, cb=cb, t=t: E.dma_start(
                    cb.rearrange("p (h j) -> p h j", j=65), ctx_r[t]),
                    inc="gp_cr%d" % (t % 2), n=16)
                em.wait("ve", "sp_xr%d" % (t % 2), em.val("sp_xr%d" % (t % 2)))
                em.wait("ve", "gp_cr%d" % (t % 2), em.val("gp_cr%d" % (t % 2)))
                cbh = cb.rearrange("p (h j) -> p h j", j=65)
                if t >= 2:
                    em.wait("ve", "ve_z", em.val("ve_z") - 1)
                em.op("ve", lambda E, cbh=cbh, rc=rc: E.reciprocal(
                    rc.rearrange("p (h o) -> p h o", o=1), cbh[:, :, 64:65]), inc="ve_rcp")
                em.wait("ve", "ve_rcp", em.val("ve_rcp"))
                rc_b = bass.AP(rcpt, (t % 2) * NH, [[2 * NH, 128], [1, NH], [0, 64]])
                c2h = c2.rearrange("p (h j) -> p h j", j=64)
                em.op("ve", lambda E, c2h=c2h, cbh=cbh, rb=rc_b: E.tensor_mul(
                    c2h, cbh[:, :, 0:64], rb), inc="ve_rcp")
                em.wait("ve", "ve_rcp", em.val("ve_rcp"))
                if t >= 1:
                    em.wait("ve", "pe_xn", em.val("pe_xn"))
                em.op("ve", lambda E, xb=xb, c2=c2: E.scalar_tensor_tensor(
                    zbuf[:, :], xb, 1.0, c2, ALU.mult, ALU.add), inc="ve_z")
                em.wait("ve", "ve_z", em.val("ve_z"))
                emit_ln(zbuf[:, :], xnf[:, :], 0)
                em.wait("pe", "ve_ln", em.val("ve_ln"))
                for dc in range(DC):
                    k = t * DC + dc
                    ecnt = "evac_xnv" if k % 2 == 0 else "evac_xna"
                    if k >= 2:
                        em.wait("pe", ecnt, em.val(ecnt))
                    em.op("pe", lambda E, s=tp_slots[k % 2], dc=dc: E.transpose(
                        s, xnf[:, dc * 128:dc * 128 + 128], ident[:, :]), inc="pe_xn")
                    eng = "ve" if dc % 2 == 0 else "act"
                    em.wait(eng, "pe_xn", em.val("pe_xn"))
                    dst = xnT[:, dc * NQ + t * 128: dc * NQ + t * 128 + 128]
                    if eng == "ve":
                        em.op("ve", lambda E, s=tp_slots[k % 2], d=dst: E.tensor_copy(d, s),
                              inc=ecnt)
                    else:
                        em.op("act", lambda E, s=tp_slots[k % 2], d=dst: E.copy(d, s),
                              inc=ecnt)

        NTC = NQ // 512
        mm_slots = [psB[:, 0:512], psB[:, 512:1024]]
        tslots = [psA[:, 1024:1152], psT[:, 0:128]]
        for tc in range(NTC):
            for t in range(4 * tc, 4 * tc + 4):
                ln1_tile(t)
            em.wait("pe", "evac_xnv", em.val("evac_xnv"))
            em.wait("pe", "evac_xna", em.val("evac_xna"))
            em.wait("pe", "act_y", em.val("act_y"))
            for ht in range(NHT):
                mi = tc * NHT + ht
                m = mm_slots[mi % 2]
                if mi >= 2:
                    em.wait("pe", "act_h", em.val("act_h") - 1)
                for dc in range(DC):
                    em.op("pe", lambda E, m=m, dc=dc, ht=ht, tc=tc: E.matmul(
                        m, w1_sb[:, dc * HF + ht * 128: dc * HF + ht * 128 + 128],
                        xnT[:, dc * NQ + tc * 512: dc * NQ + tc * 512 + 512],
                        start=(dc == 0), stop=(dc == DC - 1)),
                        inc=("pe_h" if dc == DC - 1 else None))
                em.wait("act", "pe_h", em.val("pe_h"))
                em.op("act", lambda E, m=m, ht=ht: E.activation(
                    hT[:, ht * 512:ht * 512 + 512], m, AF.Relu,
                    bias=b1_sb[:, ht:ht + 1], scale=1.0), inc="act_h")
            def z2_transposes(dt, zslot):
                em.wait("pe", "ve_y", em.val("ve_y"))
                for st4 in range(4):
                    k = dt * 4 + st4
                    sl = tslots[k % 2]
                    if True:
                        em.wait("pe", "ve_z2", max(0, em.val("ve_z2") - 1))
                    em.op("pe", lambda E, sl=sl, st4=st4, z=zslot: E.transpose(
                        sl, z[:, st4 * 128:st4 * 128 + 128], ident[:, :]), inc="pe_z2")
                    em.wait("ve", "pe_z2", em.val("pe_z2"))
                    em.op("ve", lambda E, sl=sl, st4=st4, dt=dt: E.tensor_copy(
                        z2n[:, st4 * D + dt * 128: st4 * D + dt * 128 + 128], sl),
                        inc="ve_z2")

            pending_z = None
            for dt in range(NJT):
                mi2 = tc * NJT + dt
                m = mm_slots[mi2 % 2]
                zslot = z2T[:, (dt % 2) * 512:(dt % 2) * 512 + 512]
                em.wait("pe", "act_h", em.val("act_h"))
                if mi2 >= 2:
                    em.wait("pe", "act_y", em.val("act_y") - 1)
                for ht in range(NHT):
                    em.op("pe", lambda E, m=m, ht=ht, dt=dt: E.matmul(
                        m, w2_sb[:, ht * D + dt * 128: ht * D + dt * 128 + 128],
                        hT[:, ht * 512:ht * 512 + 512],
                        start=(ht == 0), stop=(ht == NHT - 1)),
                        inc=("pe_y" if ht == NHT - 1 else None))
                if pending_z is not None:
                    z2_transposes(*pending_z)
                em.wait("act", "pe_y", em.val("pe_y"))
                em.wait("act", "pe_z2", max(0, em.val("pe_z2") - 4))
                em.op("act", lambda E, m=m, dt=dt, z=zslot: E.activation(
                    z, m, AF.Identity, bias=b2_sb[:, dt:dt + 1], scale=1.0),
                    inc="act_y")
                em.wait("ve", "act_y", em.val("act_y"))
                em.op("ve", lambda E, dt=dt, tc=tc, z=zslot: E.tensor_tensor(
                    z, z, xnT[:, dt * NQ + tc * 512: dt * NQ + tc * 512 + 512], ALU.add),
                    inc="ve_y")
                pending_z = (dt, zslot)
            z2_transposes(*pending_z)
            for st4 in range(4):
                oi = tc * 4 + st4
                em.wait("ve", "ve_z2", em.val("ve_z2"))
                ob = ostage[:, (oi % 2) * D:(oi % 2) * D + D]
                if oi >= 2:
                    em.wait("ve", "sp_out%d" % (oi % 2), em.val("sp_out%d" % (oi % 2)))
                emit_ln(z2n[:, st4 * D:st4 * D + D], ob, 4)
                em.wait("sp", "ve_ln", em.val("ve_ln"))
                em.op("sp", lambda E, ob=ob, tc=tc, st4=st4: E.dma_start(
                    out_d[tc * 512 + st4 * 128: tc * 512 + st4 * 128 + 128, :], ob),
                    inc="sp_out%d" % (oi % 2), n=16)
        if "xn" in debug:
            for dc in range(DC):
                em.wait("gp", "evac_xnv", em.val("evac_xnv"))
                em.wait("gp", "evac_xna", em.val("evac_xna"))
                em.op("gp", lambda E, dc=dc: E.dma_start(
                    dbg["xn"][dc * 128:(dc + 1) * 128, :],
                    xnT[:, dc * NQ:(dc + 1) * NQ]), inc="sp_dbgx", n=16)
        em.wait("sp", "sp_out0", em.val("sp_out0"))
        em.wait("sp", "sp_out1", em.val("sp_out1"))

    emit_phase_w()
    emit_phase_p()
    emit_phase_s()
    emit_phase_a()
    emit_phase_f()

    with nc.Block() as blk:
        em.run(blk)
    return nc


_NC_CACHE = {}


def _get_nc(key, cfg, debug=()):
    if key not in _NC_CACHE:
        _NC_CACHE[key] = build(cfg, debug=debug)
    return _NC_CACHE[key]


def kernel(L0_attn_hidden_states, L0_attn_qk_w, L0_attn_qk_b, L0_attn_v_w,
           L0_attn_v_b, L0_attn_hash_a, L0_ffn_w_gate, L0_ffn_w_noise,
           L0_ffn_W1, L0_ffn_b1, L0_ffn_W2, L0_ffn_b2):
    x = np.ascontiguousarray(np.asarray(L0_attn_hidden_states, dtype=np.float32))
    Bsz, L, D = x.shape
    cfg = CFG(L=L, NH=D // 64, HF=np.asarray(L0_ffn_W1).shape[2])
    nc = _get_nc(("full", L, D), cfg)

    com = {
        "qk_w": np.ascontiguousarray(L0_attn_qk_w, dtype=np.float32),
        "qk_b": np.ascontiguousarray(L0_attn_qk_b, dtype=np.float32),
        "v_w": np.ascontiguousarray(L0_attn_v_w, dtype=np.float32),
        "v_b": np.ascontiguousarray(L0_attn_v_b, dtype=np.float32),
        "hash_a": np.ascontiguousarray(L0_attn_hash_a, dtype=np.float32),
        "w1": np.ascontiguousarray(np.asarray(L0_ffn_W1)[1], dtype=np.float32),
        "b1": np.ascontiguousarray(np.asarray(L0_ffn_b1)[1], dtype=np.float32),
        "w2": np.ascontiguousarray(np.asarray(L0_ffn_W2)[1], dtype=np.float32),
        "b2": np.ascontiguousarray(np.asarray(L0_ffn_b2)[1], dtype=np.float32),
    }
    NQ = L // 2
    in_maps = []
    for c in range(8):
        b = c // 2
        if c % 2 == 0:
            xb = x[b]
        else:
            xb = np.concatenate([x[b, NQ:], x[b, :NQ]], axis=0)
        m = dict(com)
        m["x"] = np.ascontiguousarray(xb)
        in_maps.append(m)

    res = run_bass_kernel_spmd(nc, in_maps, core_ids=list(range(8)))
    out = np.empty((Bsz, L, D), dtype=np.float32)
    for c in range(8):
        b = c // 2
        r = res.results[c]["out"]
        if c % 2 == 0:
            out[b, :NQ] = r
        else:
            out[b, NQ:] = r
    return out


# version 04
# speedup vs baseline: 1.0799x; 1.0799x over previous
import numpy as np
import concourse.bass as bass
import concourse.mybir as mybir
from concourse.bass_utils import run_bass_kernel_spmd

F32 = mybir.dt.float32
BF16 = mybir.dt.bfloat16
AF = mybir.ActivationFunctionType
ALU = mybir.AluOpType
AX = mybir.AxisListType.X

CBIT = 4992.0
EXP_BIAS = -(CBIT + 5.0)
LN_EPS = 1e-5

ENGS = ("pe", "act", "ve", "sp", "gp")


class CFG:
    def __init__(self, L=8192, NH=16, HF=2048):
        self.L = L
        self.NH = NH
        self.D = NH * 64
        self.NQ = L // 2
        self.HF = HF
        self.DC = self.D // 128
        self.NJT = self.D // 128
        self.NKT = L // 128
        self.NQC = self.NQ // 512
        self.NTT = self.NQ // 128


class Emitter:
    def __init__(self, nc):
        self.nc = nc
        self.streams = {e: [] for e in ENGS}
        self.sems = {}
        self.cnt = {}

    def sem(self, name):
        if name not in self.sems:
            self.sems[name] = self.nc.alloc_semaphore(name)
            self.cnt[name] = 0
        return self.sems[name]

    def val(self, name):
        self.sem(name)
        return self.cnt[name]

    def op(self, eng, fn, inc=None, n=1):
        if inc is not None:
            h = self.sem(inc)
            self.cnt[inc] += n

            def wrapped(E, fn=fn, h=h, n=n):
                fn(E).then_inc(h, n)
            self.streams[eng].append(wrapped)
        else:
            self.streams[eng].append(fn)

    def wait(self, eng, name, value=None):
        if value is None:
            value = self.val(name)
        if value <= 0:
            return
        h = self.sem(name)
        self.streams[eng].append(lambda E, h=h, v=value: E.wait_ge(h, v))

    def run(self, block):
        for eng, meth in (("pe", block.tensor), ("act", block.scalar),
                          ("ve", block.vector), ("sp", block.sync),
                          ("gp", block.gpsimd)):
            fns = self.streams[eng]
            if fns:
                meth(lambda E, fns=fns: [f(E) for f in fns])
            self.streams[eng] = []


def build(cfg, debug=()):
    nc = bass.Bass("TRN2", target_bir_lowering=False)
    em = Emitter(nc)
    L, NH, D, NQ, HF = cfg.L, cfg.NH, cfg.D, cfg.NQ, cfg.HF
    DC, NJT, NKT, NQC, NTT = cfg.DC, cfg.NJT, cfg.NKT, cfg.NQC, cfg.NTT
    NHT = HF // 128
    NQUARTER = 4
    QL = L // NQUARTER
    QT_TILES = QL // 128
    QTC = QL // 512
    assert QTC >= 1 and NQC >= 1
    NHX = 128
    VHW = min(512, D)
    NVH = D // VHW
    BCW = min(512, D)

    x_d = nc.dram_tensor("x", [L, D], F32, kind="ExternalInput").ap()
    qkw_d = nc.dram_tensor("qk_w", [D, D], F32, kind="ExternalInput").ap()
    qkb_d = nc.dram_tensor("qk_b", [D], F32, kind="ExternalInput").ap()
    vw_d = nc.dram_tensor("v_w", [D, D], F32, kind="ExternalInput").ap()
    vb_d = nc.dram_tensor("v_b", [D], F32, kind="ExternalInput").ap()
    ha_d = nc.dram_tensor("hash_a", [66], F32, kind="ExternalInput").ap()
    w1_d = nc.dram_tensor("w1", [D, HF], F32, kind="ExternalInput").ap()
    b1_d = nc.dram_tensor("b1", [HF], F32, kind="ExternalInput").ap()
    w2_d = nc.dram_tensor("w2", [HF, D], F32, kind="ExternalInput").ap()
    b2_d = nc.dram_tensor("b2", [D], F32, kind="ExternalInput").ap()
    out_d = nc.dram_tensor("out", [NQ, D], F32, kind="ExternalOutput").ap()

    qkT_s = nc.dram_tensor("qkT_s", [D, L], BF16, kind="Internal").ap()
    xbf_s = nc.dram_tensor("xbf_s", [L, D], BF16, kind="Internal").ap()
    qkw_bf = nc.dram_tensor("qkw_bf", [D, D], BF16, kind="Internal").ap()
    vw_bf = nc.dram_tensor("vw_bf", [D, D], BF16, kind="Internal").ap()
    v_s = nc.dram_tensor("v_s", [L, D], BF16, kind="Internal").ap()
    ctx_s = nc.dram_tensor("ctx_s", [NH * NQ, 65], BF16, kind="Internal").ap()
    bits_s = nc.dram_tensor("bits_s", [NH, L + NQ], BF16, kind="Internal").ap()

    dbg = {}
    if "qkT" in debug:
        dbg["qkT"] = nc.dram_tensor("dbg_qkT", [D, L], F32, kind="ExternalOutput").ap()
    if "stats" in debug:
        dbg["dot1"] = nc.dram_tensor("dbg_dot1", [NH, L], F32, kind="ExternalOutput").ap()
        dbg["n2"] = nc.dram_tensor("dbg_n2", [NH, L], F32, kind="ExternalOutput").ap()
        dbg["bits"] = nc.dram_tensor("dbg_bits", [NH, L + NQ], F32, kind="ExternalOutput").ap()
    if "v" in debug:
        dbg["v"] = nc.dram_tensor("dbg_v", [L, D], F32, kind="ExternalOutput").ap()
    if "ctx" in debug:
        dbg["ctx"] = nc.dram_tensor("dbg_ctx", [NH * NQ, 65], F32, kind="ExternalOutput").ap()
    if "xn" in debug:
        dbg["xn"] = nc.dram_tensor("dbg_xn", [D, NQ], F32, kind="ExternalOutput").ap()

    guards = []

    def AL(name, shape, dtype, side="left"):
        g = nc.sbuf_tensor(name, shape, dtype, side=side)
        t = g.__enter__()
        guards.append((name, g))
        return t

    def FREE(names):
        for name in reversed(names):
            for i in range(len(guards) - 1, -1, -1):
                if guards[i][0] == name:
                    guards[i][1].__exit__(None, None, None)
                    guards.pop(i)
                    break

    ident = AL("ident", [128, 128], F32)
    expb = AL("expb", [128, 1], F32)
    lneps = AL("lneps", [128, 1], F32)
    st = AL("st", [NH, 16], F32)
    a23 = AL("a23", [NH, 2], F32)
    qkb_sb = AL("qkb_sb", [128, NJT], F32)
    b1_sb = AL("b1_sb", [128, NHT], F32)
    b2_sb = AL("b2_sb", [128, NJT], F32)

    dbufs = {}
    if "qkT" in debug:
        dbufs["dbq"] = AL("dbq", [128, L], F32)
    if "v" in debug:
        dbufs["dbv"] = AL("dbv", [128, D], F32)
    if "ctx" in debug:
        dbufs["dbc"] = AL("dbc", [128, NTT * 65], F32)

    dot1 = AL("dot1", [NH, L], F32)
    n2 = AL("n2", [NH, L], F32)
    bits = AL("bits", [NH, L + NQ], BF16)

    qk_wT = AL("qk_wT", [128, DC * D], BF16, side="right")
    v_wT = AL("v_wT", [128, DC * D], BF16, side="right")
    vbb_sb = AL("vbb_sb", [128, D], BF16, side="right")
    a1sel = AL("a1sel", [128, NJT * NH], F32, side="right")
    n1sel = AL("n1sel", [128, NJT * NH], BF16, side="right")
    onesb = AL("onesb", [1, 128], BF16, side="right")
    vbrow = AL("vbrow", [1, D], BF16, side="right")
    onesf = AL("onesf", [1, NH], F32, side="right")

    psA = nc.alloc_psum_tensor("psA", [128, 1536], F32)
    psB = nc.alloc_psum_tensor("psB", [128, 1536], F32)
    psC = nc.alloc_psum_tensor("psC", [128, 512], F32)
    psT = nc.alloc_psum_tensor("psT", [128, 512], F32)


    def emit_phase_w():
        em.op("gp", lambda E: E.memset(ident[:, :], 0.0), inc="gp_i0")
        em.wait("gp", "gp_i0")
        em.op("gp", lambda E: E.affine_select(
            out=ident[:, :], in_=ident[:, :], compare_op=ALU.not_equal,
            fill=1.0, base=0, pattern=[[-1, 128]], channel_multiplier=1), inc="gp_init")
        em.op("sp", lambda E: E.dma_start(
            qkb_sb[:, 0:NJT], qkb_d.rearrange("(j p) -> p j", p=128),
            allow_slow_non_contiguous=True), inc="sp_w", n=16)
        em.op("sp", lambda E: E.dma_start(
            b1_sb[:, 0:NHT], b1_d.rearrange("(t p) -> p t", p=128),
            allow_slow_non_contiguous=True), inc="sp_w", n=16)
        em.op("sp", lambda E: E.dma_start(
            b2_sb[:, 0:NJT], b2_d.rearrange("(t p) -> p t", p=128),
            allow_slow_non_contiguous=True), inc="sp_w", n=16)
        em.op("ve", lambda E: E.memset(expb[:, :], EXP_BIAS), inc="ve_w")
        em.op("ve", lambda E: E.memset(lneps[:, :], LN_EPS), inc="ve_w")
        em.op("ve", lambda E: E.memset(a1sel[:, :], 0.0), inc="ve_w")
        em.op("ve", lambda E: E.memset(n1sel[:, :], 0.0), inc="ve_w")
        em.wait("ve", "ve_w", em.val("ve_w"))
        for jt in range(NJT):
            c0 = jt * NH + 2 * jt
            em.wait("sp", "ve_w", em.val("ve_w"))
            em.op("sp", lambda E, c0=c0: E.dma_start(
                a1sel[0:64, c0:c0 + 1],
                ha_d[0:64].rearrange("(p o) -> p o", o=1),
                allow_slow_non_contiguous=True), inc="sp_w", n=16)
            em.op("sp", lambda E, c0=c0: E.dma_start(
                a1sel[64:128, c0 + 1:c0 + 2],
                ha_d[0:64].rearrange("(p o) -> p o", o=1),
                allow_slow_non_contiguous=True), inc="sp_w", n=16)
            em.op("ve", lambda E, c0=c0: E.memset(n1sel[0:64, c0:c0 + 1], 1.0))
            em.op("ve", lambda E, c0=c0: E.memset(n1sel[64:128, c0 + 1:c0 + 2], 1.0),
                  inc="ve_w")
        em.op("ve", lambda E: E.memset(onesb[:, :], 1.0), inc="ve_w")
        em.op("sp", lambda E: E.dma_start(
            st[0:1, 0:2], ha_d[64:66].rearrange("(o c) -> o c", o=1)), inc="sp_w", n=16)
        em.op("gp", lambda E: E.dma_start(
            vbrow[0:1, :], vb_d.rearrange("(o c) -> o c", o=1)), inc="gp_w", n=16)
        em.wait("pe", "gp_init")
        em.wait("pe", "ve_w")
        em.wait("pe", "sp_w")
        em.wait("pe", "gp_w")
        em.op("ve", lambda E: E.memset(onesf[:, :], 1.0), inc="ve_w")
        em.wait("pe", "ve_w")
        em.op("pe", lambda E: E.matmul(psC[0:NH, 0:2], onesf[0:1, 0:NH],
                                       st[0:1, 0:2], start=True, stop=True), inc="pe_w")
        em.wait("ve", "pe_w", em.val("pe_w"))
        em.op("ve", lambda E: E.tensor_copy(a23[:, :], psC[0:NH, 0:2]), inc="ve_a23")
        for half in range(D // BCW):
            em.wait("pe", "ve_vbb", em.val("ve_vbb") if half > 0 else 0)
            em.wait("pe", "ve_a23", em.val("ve_a23"))
            em.op("pe", lambda E, h=half: E.matmul(
                psC[:, 0:BCW], onesb[0:1, 0:128],
                vbrow[0:1, h * BCW:(h + 1) * BCW], start=True, stop=True), inc="pe_w")
            em.wait("ve", "pe_w", em.val("pe_w"))
            em.op("ve", lambda E, h=half: E.tensor_copy(
                vbb_sb[:, h * BCW:(h + 1) * BCW], psC[:, 0:BCW]), inc="ve_vbb")
        em.op("gp", lambda E: E.dma_start(qkw_bf, qkw_d), inc="gp_wbf", n=16)
        em.op("gp", lambda E: E.dma_start(vw_bf, vw_d), inc="gp_wbf", n=16)
        em.op("gp", lambda E: E.dma_start(xbf_s, x_d), inc="gp_xbf", n=16)
        em.wait("sp", "gp_wbf", em.val("gp_wbf"))
        for wi, (src, dstT) in enumerate(((qkw_bf, qk_wT), (vw_bf, v_wT))):
            for dc in range(DC):
                em.op("sp", lambda E, s=src, d=dstT, dc=dc: E.dma_start(
                    d[:, dc * D:(dc + 1) * D], s[:, dc * 128:(dc + 1) * 128],
                    transpose=True), inc="sp_wt", n=16)

    xT = AL("xT", [128, DC * QL], BF16, side="right")
    qkf = AL("qkf", [128, 2 * 512], F32, side="right")
    sqb = AL("sqb", [128, 2 * 512], BF16, side="right")
    qstage = AL("qstage", [128, 2 * NJT * 512], BF16, side="right")
    vstage = AL("vstage", [128, 2 * D], BF16, side="right")

    def emit_phase_p():
        em.wait("act", "sp_w", em.val("sp_w"))
        em.wait("act", "ve_w", em.val("ve_w"))
        mm_slots = [psA[:, 1024:1536], psB[:, 0:512]]
        d1_slots = [psB[:, 512:1024], psB[:, 1024:1536]]
        n2_slots = [psC[:, 0:512], psT[:, 0:512]]
        em.wait("pe", "ve_vbb", em.val("ve_vbb"))
        em.wait("sp", "gp_xbf", em.val("gp_xbf"))
        for q in range(NQUARTER):
            if q > 0:
                em.wait("sp", "pe_qk", em.val("pe_qk"))
                em.wait("sp", "pe_v", em.val("pe_v"))
            for dc in range(DC):
                em.op("sp", lambda E, q=q, dc=dc: E.dma_start(
                    xT[:, dc * QL:(dc + 1) * QL],
                    xbf_s[q * QL:(q + 1) * QL, dc * 128:(dc + 1) * 128],
                    transpose=True), inc="sp_xt", n=16)
            em.wait("pe", "sp_xt", em.val("sp_xt"))
            em.wait("pe", "sp_wt", em.val("sp_wt"))
            em.wait("pe", "ve_v", em.val("ve_v"))
            for tc in range(QTC):
                gi = q * QTC + tc
                d1b = d1_slots[gi % 2]
                n2b = n2_slots[gi % 2]
                def stats_mms(jt, d1b, n2b, qf, sq):
                    em.wait("pe", "act_qk", em.val("act_qk"))
                    em.op("pe", lambda E, d1b=d1b, jt=jt, qf=qf: E.matmul(
                        d1b[0:NH, :], a1sel[:, jt * NH:jt * NH + NH], qf,
                        start=(jt == 0), stop=(jt == NJT - 1)), inc="pe_d1")
                    em.wait("pe", "ve_sq", em.val("ve_sq"))
                    em.op("pe", lambda E, n2b=n2b, jt=jt, sq=sq: E.matmul(
                        n2b[0:NH, :], n1sel[:, jt * NH:jt * NH + NH], sq,
                        start=(jt == 0), stop=(jt == NJT - 1)), inc="pe_n2")

                pending = None
                for jt in range(NJT):
                    mi = gi * NJT + jt
                    mmb = mm_slots[mi % 2]
                    if mi >= 2:
                        em.wait("pe", "act_qk", em.val("act_qk") - 1)
                    if jt == 0 and gi >= 2:
                        em.wait("pe", "ve_d1", em.val("ve_d1") - 1)
                        em.wait("pe", "ve_n2", em.val("ve_n2") - 1)
                    for dc in range(DC):
                        em.op("pe", lambda E, m=mmb, dc=dc, jt=jt, q=q, tc=tc: E.matmul(
                            m, qk_wT[:, (dc * NJT + jt) * 128:(dc * NJT + jt) * 128 + 128],
                            xT[:, dc * QL + tc * 512: dc * QL + tc * 512 + 512],
                            start=(dc == 0), stop=(dc == DC - 1)),
                            inc=("pe_qk" if dc == DC - 1 else None))
                    if pending is not None:
                        stats_mms(*pending)
                    qf = qkf[:, (mi % 2) * 512:(mi % 2) * 512 + 512]
                    sq = sqb[:, (mi % 2) * 512:(mi % 2) * 512 + 512]
                    em.wait("act", "pe_qk", em.val("pe_qk"))
                    if mi >= 2:
                        em.wait("act", "pe_d1", em.val("pe_d1"))
                    em.op("act", lambda E, m=mmb, qf=qf, jt=jt: E.activation(
                        qf, m, AF.Identity, bias=qkb_sb[:, jt:jt + 1], scale=1.0),
                        inc="act_qk")
                    em.wait("ve", "act_qk", em.val("act_qk"))
                    if mi >= 2:
                        em.wait("ve", "pe_n2", em.val("pe_n2"))
                    em.op("ve", lambda E, sq=sq, qf=qf: E.tensor_mul(sq, qf, qf),
                          inc="ve_sq")
                    stg = qstage[:, (gi % 2) * NJT * 512 + jt * 512:
                                 (gi % 2) * NJT * 512 + jt * 512 + 512]
                    if gi >= 2 and jt == 0:
                        em.wait("ve", "gp_qs%d" % (gi % 2), em.val("gp_qs%d" % (gi % 2)))
                    em.op("ve", lambda E, stg=stg, qf=qf: E.tensor_copy(stg, qf),
                          inc="ve_qs")
                    pending = (jt, d1b, n2b, qf, sq)
                stats_mms(*pending)
                em.wait("ve", "pe_d1", em.val("pe_d1"))
                em.wait("ve", "pe_n2", em.val("pe_n2"))
                cs = q * QL + tc * 512
                em.op("ve", lambda E, d1b=d1b, cs=cs: E.tensor_copy(
                    dot1[:, cs:cs + 512], d1b[0:NH, :]), inc="ve_d1")
                em.op("ve", lambda E, n2b=n2b, cs=cs: E.tensor_copy(
                    n2[:, cs:cs + 512], n2b[0:NH, :]), inc="ve_n2")
                em.wait("gp", "ve_qs", em.val("ve_qs"))
                src = qstage[:, (gi % 2) * NJT * 512:(gi % 2) * NJT * 512 + NJT * 512]
                dst = qkT_s.rearrange("(jt p) Lx -> p jt Lx", p=128)[:, :, cs:cs + 512]
                em.op("gp", lambda E, s=src, d=dst: E.dma_start(
                    d, s.rearrange("p (jt c) -> p jt c", c=512)),
                    inc="gp_qs%d" % (gi % 2), n=16)
            em.wait("pe", "act_qk", em.val("act_qk"))
            for t in range(QT_TILES):
                gt = q * QT_TILES + t
                vs = vstage[:, (gt % 2) * D:(gt % 2) * D + D]
                if gt >= 2:
                    em.wait("ve", "gp_vs%d" % (gt % 2), em.val("gp_vs%d" % (gt % 2)))
                for half in range(NVH):
                    vi = gt * NVH + half
                    mmb = mm_slots[vi % 2]
                    em.wait("pe", "ve_v", max(0, em.val("ve_v") - 1))
                    for dc in range(DC):
                        em.op("pe", lambda E, m=mmb, dc=dc, t=t, h=half: E.matmul(
                            m[:, 0:VHW],
                            xT[:, dc * QL + t * 128: dc * QL + t * 128 + 128],
                            v_wT[:, dc * D + h * VHW: dc * D + h * VHW + VHW],
                            start=(dc == 0), stop=(dc == DC - 1)),
                            inc=("pe_v" if dc == DC - 1 else None))
                    em.wait("ve", "pe_v", em.val("pe_v"))
                    em.op("ve", lambda E, m=mmb, vs=vs, h=half: E.tensor_tensor(
                        vs[:, h * VHW:h * VHW + VHW], m[:, 0:VHW],
                        vbb_sb[:, h * VHW:h * VHW + VHW], ALU.add), inc="ve_v")
                em.wait("gp", "ve_v", em.val("ve_v"))
                em.op("gp", lambda E, vs=vs, q=q, t=t: E.dma_start(
                    v_s[q * QL + t * 128: q * QL + t * 128 + 128, :], vs),
                    inc="gp_vs%d" % (gt % 2), n=16)
        if "qkT" in debug:
            db = dbufs["dbq"]
            for jt in range(NJT):
                em.wait("gp", "gp_qs0", em.val("gp_qs0"))
                em.wait("gp", "gp_qs1", em.val("gp_qs1"))
                em.wait("gp", "sp_dbg", em.val("sp_dbg"))
                em.op("gp", lambda E, jt=jt: E.dma_start(
                    db[:, :], qkT_s[jt * 128:(jt + 1) * 128, :]), inc="sp_dbg", n=16)
                em.wait("gp", "sp_dbg")
                em.op("gp", lambda E, jt=jt: E.dma_start(
                    dbg["qkT"][jt * 128:(jt + 1) * 128, :], db[:, :]), inc="sp_dbg", n=16)
        if "v" in debug:
            dbv = dbufs["dbv"]
            for t in range(L // 128):
                em.wait("gp", "gp_vs0", em.val("gp_vs0"))
                em.wait("gp", "gp_vs1", em.val("gp_vs1"))
                em.wait("gp", "sp_dbgv", em.val("sp_dbgv"))
                em.op("gp", lambda E, t=t: E.dma_start(
                    dbv[:, :], v_s[t * 128:(t + 1) * 128, :]), inc="sp_dbgv", n=16)
                em.wait("gp", "sp_dbgv")
                em.op("gp", lambda E, t=t: E.dma_start(
                    dbg["v"][t * 128:(t + 1) * 128, :], dbv[:, :]), inc="sp_dbgv", n=16)

    FREE(["xT", "qkf", "sqb", "qstage", "vstage"])
    FREE(["onesf"])
    FREE(["qk_wT", "v_wT", "vbb_sb", "a1sel", "n1sel", "onesb", "vbrow"])
    hkv = AL("hkv", [NH, L], F32)

    def emit_phase_s():
        V = "ve"
        for nm in ("pe_qk", "pe_v", "pe_d1", "pe_n2", "gp_qs0", "gp_qs1",
                   "gp_vs0", "gp_vs1", "act_qk", "ve_qs", "ve_v", "ve_sq"):
            em.wait(V, nm, em.val(nm))
        em.wait(V, "ve_a23", em.val("ve_a23"))
        c = lambda i: st[:, i:i + 1]
        em.wait(V, "ve_s", em.val("ve_s"))
        em.op(V, lambda E: E.reduce_max(c(0), n2[:, :], axis=AX), inc="ve_s")
        em.wait("act", "ve_s", em.val("ve_s"))
        em.op("act", lambda E: E.activation(c(1), c(0), AF.Sqrt), inc="act_s")
        em.wait(V, "act_s", em.val("act_s"))
        em.wait(V, "ve_s", em.val("ve_s"))
        em.op(V, lambda E: E.tensor_single_scalar(c(2), c(1), 1e-12, ALU.max), inc="ve_s")
        em.wait(V, "ve_s", em.val("ve_s"))
        em.op(V, lambda E: E.reciprocal(c(3), c(2)), inc="ve_s")
        em.wait(V, "ve_s", em.val("ve_s"))
        em.op(V, lambda E: E.tensor_single_scalar(c(2), c(3), 0.75, ALU.mult), inc="ve_s")
        em.wait(V, "ve_s", em.val("ve_s"))
        em.op(V, lambda E: E.tensor_mul(c(4), c(2), c(2)), inc="ve_s")
        em.wait(V, "ve_s", em.val("ve_s"))
        em.op(V, lambda E: E.tensor_scalar_mul(hkv[:, :], n2[:, :], c(4)), inc="ve_s")
        em.wait(V, "ve_s", em.val("ve_s"))
        em.op(V, lambda E: E.tensor_scalar(n2[:, :], hkv[:, :], -1.0, 0.5,
                                           ALU.mult, ALU.add), inc="ve_s")
        em.wait(V, "ve_s", em.val("ve_s"))
        em.op(V, lambda E: E.tensor_mul(hkv[:, :], hkv[:, :], hkv[:, :]), inc="ve_s")
        em.wait(V, "ve_s", em.val("ve_s"))
        em.op(V, lambda E: E.tensor_scalar(hkv[:, :], hkv[:, :], -1.0, 0.5,
                                           ALU.mult, ALU.add), inc="ve_s")
        em.wait(V, "ve_s", em.val("ve_s"))
        em.op(V, lambda E: E.tensor_scalar_mul(hkv[:, :], hkv[:, :], a23[:, 1:2]), inc="ve_s")
        em.wait(V, "ve_s", em.val("ve_s"))
        em.op(V, lambda E: E.scalar_tensor_tensor(
            hkv[:, :], n2[:, :], a23[:, 0:1], hkv[:, :], ALU.mult, ALU.add), inc="ve_s")
        em.wait(V, "ve_s", em.val("ve_s"))
        em.op(V, lambda E: E.scalar_tensor_tensor(
            hkv[:, :], dot1[:, :], c(2), hkv[:, :], ALU.mult, ALU.add), inc="ve_s")
        em.wait(V, "ve_s", em.val("ve_s"))
        em.op(V, lambda E: E.tensor_single_scalar(hkv[:, :], hkv[:, :], 0.0, ALU.is_ge),
              inc="ve_s")
        em.wait(V, "ve_s", em.val("ve_s"))
        em.op(V, lambda E: E.tensor_scalar(bits[:, 0:L], hkv[:, :], 128.0, -64.0,
                                           ALU.mult, ALU.add), inc="ve_s")
        em.wait(V, "ve_s", em.val("ve_s"))
        em.op(V, lambda E: E.reduce_max(c(5), hkv[:, :], axis=AX), inc="ve_s")
        em.wait(V, "ve_s", em.val("ve_s"))
        em.op(V, lambda E: E.tensor_reduce(c(6), hkv[:, :], axis=AX, op=ALU.min), inc="ve_s")
        em.wait(V, "ve_s", em.val("ve_s"))
        em.op(V, lambda E: E.tensor_sub(c(7), c(5), c(6)), inc="ve_s")
        em.wait(V, "ve_s", em.val("ve_s"))
        em.op(V, lambda E: E.tensor_add(c(8), c(5), c(6)), inc="ve_s")
        em.wait(V, "ve_s", em.val("ve_s"))
        em.op(V, lambda E: E.tensor_single_scalar(c(8), c(8), 1.0, ALU.subtract),
              inc="ve_s")
        em.wait(V, "ve_s", em.val("ve_s"))
        em.op(V, lambda E: E.tensor_single_scalar(c(9), c(7), 156.0, ALU.mult), inc="ve_s")
        em.wait(V, "ve_s", em.val("ve_s"))
        em.op(V, lambda E: E.tensor_sub(c(10), c(8), c(7)), inc="ve_s")
        em.wait(V, "ve_s", em.val("ve_s"))
        em.op(V, lambda E: E.tensor_single_scalar(c(10), c(10), 78.0, ALU.mult), inc="ve_s")
        em.wait(V, "ve_s", em.val("ve_s"))
        em.op(V, lambda E: E.tensor_single_scalar(dot1[:, 0:NQ], dot1[:, 0:NQ], 0.0,
                                                  ALU.is_ge), inc="ve_s")
        em.wait(V, "ve_s", em.val("ve_s"))
        em.op(V, lambda E: E.tensor_scalar(bits[:, L:L + NQ], dot1[:, 0:NQ], c(9), c(10),
                                           ALU.mult, ALU.add), inc="ve_s")
        em.wait("gp", "ve_s", em.val("ve_s"))
        em.op("gp", lambda E: E.dma_start(bits_s, bits[:, :]), inc="gp_bits", n=16)
        if "stats" in debug:
            em.wait("gp", "ve_s", em.val("ve_s"))
            em.op("gp", lambda E: E.dma_start(dbg["dot1"], dot1[:, :]), inc="sp_dbgs", n=16)
            em.op("gp", lambda E: E.dma_start(dbg["n2"], n2[:, :]), inc="sp_dbgs", n=16)
            em.op("gp", lambda E: E.dma_start(dbg["bits"], bits[:, :]), inc="sp_dbgs", n=16)

    FREE(["hkv"])
    FREE(["dot1", "n2", "bits"])
    KT = AL("KT", [65, 2 * L], BF16)
    QTb = AL("QTb", [65, 2 * NQ], BF16)
    vhb = AL("vhb", [128, 2 * NKT * 65], BF16)
    ctxT = AL("ctxT", [65, NQ], F32)
    pt4 = AL("pt4", [128, 4 * 1536], BF16)
    cstage = AL("cstage", [128, NTT * 65], BF16)

    groups = []
    k = 0
    while k < NKT:
        size = min(3, NKT - k)
        groups.append((k, size))
        k += size

    ctx_w = ctx_s.rearrange("(h t p) j -> h p t j", p=128, t=NTT)

    def emit_phase_a():
        em.wait("pe", "ve_d1", em.val("ve_d1"))
        em.wait("pe", "ve_n2", em.val("ve_n2"))
        em.wait("pe", "ve_v", em.val("ve_v"))
        em.wait("pe", "act_qk", em.val("act_qk"))
        em.wait("pe", "evac_xp", em.val("evac_xp"))

        em.wait("act", "ve_w", em.val("ve_w"))
        state = {"exp_at_bank": {0: 0, 1: 0}, "ctx_evac": 0,
                 "av_done": {0: 0, 1: 0, 2: 0, 3: 0}}

        def prefetch(h, into):
            kth = KT[:, into * L: into * L + L]
            em.op("sp", lambda E, h=h, kth=kth: E.dma_start(
                kth[0:64, :], qkT_s[h * 64:(h + 1) * 64, :]),
                inc="sp_kt%d" % into, n=16)
            em.op("gp", lambda E, h=h, kth=kth: E.dma_start(
                kth[64:65, :], bits_s[h:h + 1, 0:L]), inc="gp_kt%d" % into, n=16)
            vb = vhb[:, into * NKT * 65: into * NKT * 65 + NKT * 65]
            dstv = vb.rearrange("p (k j) -> p k j", j=65)[:, :, 0:64]
            srcv = v_s[:, h * 64:(h + 1) * 64].rearrange("(k p) j -> p k j", p=128)
            KH = NKT // 2
            em.op("sp", lambda E, d=dstv, s=srcv, KH=KH: E.dma_start(
                d[:, 0:KH, :], s[:, 0:KH, :]), inc="sp_vh%d" % into, n=16)
            em.op("gp", lambda E, d=dstv, s=srcv, KH=KH: E.dma_start(
                d[:, KH:NKT, :], s[:, KH:NKT, :]), inc="gp_vh%d" % into, n=16)

        em.wait("ve", "sp_dbgs", em.val("sp_dbgs"))
        em.wait("ve", "gp_bits", em.val("gp_bits"))
        em.op("ve", lambda E: E.memset(vhb[:, :], 1.0), inc="ve_ones")
        em.wait("gp", "ve_ones", em.val("ve_ones"))
        em.wait("sp", "ve_ones", em.val("ve_ones"))
        em.wait("gp", "gp_bits", em.val("gp_bits"))
        em.wait("sp", "ve_s", em.val("ve_s"))
        em.wait("sp", "gp_bits", em.val("gp_bits"))
        em.wait("sp", "sp_dbgs", em.val("sp_dbgs"))
        em.wait("gp", "sp_dbgs", em.val("sp_dbgs"))
        em.wait("sp", "gp_qs0", em.val("gp_qs0"))
        em.wait("sp", "gp_qs1", em.val("gp_qs1"))
        em.wait("gp", "gp_vs0", em.val("gp_vs0"))
        em.wait("gp", "gp_vs1", em.val("gp_vs1"))
        prefetch(0, 0)
        prefetch(1, 1)

        ct_at_qc = [0] * NQC

        def build_qt(h):
            par = h % 2
            kth = KT[:, par * L: par * L + L]
            qtb = QTb[:, par * NQ: par * NQ + NQ]
            em.wait("ve", "sp_kt%d" % par, em.val("sp_kt%d" % par))
            em.wait("ve", "gp_kt%d" % par, em.val("gp_kt%d" % par))
            em.wait("ve", "pe_s", em.val("pe_s"))
            em.op("ve", lambda E, kth=kth, q=qtb: E.tensor_scalar_mul(
                q[0:64, :], kth[0:64, 0:NQ], 0.125), inc="ve_qt")
            em.wait("gp", "pe_s", em.val("pe_s"))
            em.op("gp", lambda E, h=h, q=qtb: E.dma_start(
                q[64:65, :], bits_s[h:h + 1, L:L + NQ]), inc="gp_qt%d" % par, n=16)

        build_qt(0)
        for h in range(NH):
            par = h % 2
            kth = KT[:, par * L: par * L + L]
            vb = vhb[:, par * NKT * 65: par * NKT * 65 + NKT * 65]
            qtb = QTb[:, par * NQ: par * NQ + NQ]
            if h + 1 < NH:
                build_qt(h + 1)
            em.wait("pe", "ve_qt", em.val("ve_qt") - (1 if h + 1 < NH else 0))
            em.wait("pe", "gp_qt%d" % par, em.val("gp_qt%d" % par))
            em.wait("pe", "gp_vh%d" % par, em.val("gp_vh%d" % par))
            em.wait("pe", "sp_vh%d" % par, em.val("sp_vh%d" % par))
            em.wait("pe", "sp_kt%d" % par, em.val("sp_kt%d" % par))
            em.wait("pe", "gp_kt%d" % par, em.val("gp_kt%d" % par))

            evac_at_qc = [0] * NQC

            def epilogue_block(b, h):
                em.wait("pe", "ve_ctx", evac_at_qc[b])
                em.wait("pe", "ve_cs", em.val("ve_cs"))
                for j in range(4):
                    tt = b * 4 + j
                    em.op("pe", lambda E, j=j, tt=tt: E.transpose(
                        psT[:, j * 80:j * 80 + 65], ctxT[:, tt * 128:tt * 128 + 128],
                        ident[0:65, 0:65]), inc=("pe_ct" if j == 3 else None))
                ct_at_qc[b] = em.val("pe_ct")
                em.wait("ve", "pe_ct", em.val("pe_ct"))
                if b == 0:
                    em.wait("ve", "gp_cs", em.val("gp_cs"))
                em.op("ve", lambda E, b=b: E.tensor_copy(
                    cstage[:, b * 4 * 65:(b + 1) * 4 * 65].rearrange(
                        "p (t j) -> p t j", j=65),
                    psT[:, 0:320].rearrange("p (t j) -> p t j", j=80)[:, :, 0:65]),
                    inc="ve_cs")

            pending_tail = [None]

            def av_group(qc_ctx, j):
                exp_at_g, vbx = qc_ctx
                k0, size = groups[j]
                ptb = pt4[:, (j % 4) * 1536:(j % 4) * 1536 + 1536]
                em.wait("pe", "act_e", exp_at_g[j])
                if k0 == 0:
                    em.wait("pe", "ve_ctx", state["ctx_evac"])
                for j2 in range(size):
                    kt = k0 + j2
                    em.op("pe", lambda E, v=vbx, kt=kt, p=ptb, j2=j2: E.matmul(
                        psC[0:65, :], v[:, kt * 65:kt * 65 + 65],
                        p[:, j2 * 512:j2 * 512 + 512],
                        start=(kt == 0), stop=(kt == NKT - 1),
                        skip_group_check=True),
                        inc=("pe_avg" if j2 == size - 1 else None))
                state["av_done"][j % 4] = em.val("pe_avg")

            def flush_tail():
                if pending_tail[0] is None:
                    return
                qc_ctx, tqc = pending_tail[0]
                pending_tail[0] = None
                av_group(qc_ctx, len(groups) - 2)
                av_group(qc_ctx, len(groups) - 1)
                em.wait("ve", "pe_avg", em.val("pe_avg"))
                em.wait("ve", "pe_ct", ct_at_qc[tqc])
                em.op("ve", lambda E, tqc=tqc: E.tensor_copy(
                    ctxT[:, tqc * 512:tqc * 512 + 512], psC[0:65, :]), inc="ve_ctx")
                state["ctx_evac"] = em.val("ve_ctx")
                evac_at_qc[tqc] = em.val("ve_ctx")

            for qc in range(NQC):
                exp_at_g = []
                qc_ctx = (exp_at_g, vb)
                for g, (k0, size) in enumerate(groups):
                    bank = psA if g % 2 == 0 else psB
                    bpar = g % 2
                    ptb = pt4[:, (g % 4) * 1536:(g % 4) * 1536 + 1536]
                    em.wait("pe", "act_e", state["exp_at_bank"][bpar])
                    for j in range(size):
                        kt = k0 + j
                        em.op("pe", lambda E, b=bank, j=j, kt=kt, kth=kth, qc=qc,
                              qtb=qtb: E.matmul(
                            b[:, j * 512:j * 512 + 512],
                            kth[:, kt * 128:kt * 128 + 128],
                            qtb[:, qc * 512:qc * 512 + 512],
                            start=True, stop=True),
                            inc=("pe_s" if j == size - 1 else None))
                    if g == 0:
                        flush_tail()
                    em.wait("act", "pe_avg", state["av_done"][g % 4])
                    em.wait("act", "pe_s", em.val("pe_s"))
                    em.op("act", lambda E, b=bank, p=ptb, w=size * 512: E.activation(
                        p[:, 0:w], b[:, 0:w], AF.Exp, bias=expb[:, 0:1], scale=1.0),
                        inc="act_e")
                    state["exp_at_bank"][bpar] = em.val("act_e")
                    exp_at_g.append(em.val("act_e"))
                    if g >= 2:
                        av_group(qc_ctx, g - 2)
                    if g == 1 and qc >= 1:
                        epilogue_block(qc - 1, h)
                pending_tail[0] = (qc_ctx, qc)
            flush_tail()
            epilogue_block(NQC - 1, h)
            em.wait("gp", "ve_cs", em.val("ve_cs"))
            em.op("gp", lambda E, h=h: E.dma_start(
                ctx_w[h], cstage[:, :].rearrange("p (t j) -> p t j", j=65)),
                inc="gp_cs", n=16)
            if h + 2 < NH:
                em.wait("sp", "pe_s", em.val("pe_s"))
                em.wait("gp", "pe_s", em.val("pe_s"))
                em.wait("gp", "pe_avg", em.val("pe_avg"))
                em.wait("sp", "pe_avg", em.val("pe_avg"))
                prefetch(h + 2, par)
        if "ctx" in debug:
            dbc = dbufs["dbc"]
            dbg_w = dbg["ctx"].rearrange("(h t p) j -> h p t j", p=128, t=NTT)
            for h in range(NH):
                em.wait("gp", "gp_cs", em.val("gp_cs"))
                em.wait("gp", "sp_dbgc", em.val("sp_dbgc"))
                em.op("gp", lambda E, h=h: E.dma_start(
                    dbc[:, :].rearrange("p (t j) -> p t j", j=65), ctx_w[h]),
                    inc="sp_dbgc", n=16)
                em.wait("gp", "sp_dbgc")
                em.op("gp", lambda E, h=h: E.dma_start(
                    dbg_w[h], dbc[:, :].rearrange("p (t j) -> p t j", j=65)),
                    inc="sp_dbgc", n=16)

    FREE(["KT", "QTb", "vhb", "ctxT", "pt4", "cstage"])
    w1_sb = AL("w1_sb", [128, DC * HF], BF16)
    w2_sb = AL("w2_sb", [128, NHT * D], BF16)
    xres = AL("xres", [128, 2 * D], F32)
    cres = AL("cres", [128, 2 * NH * 65], BF16)
    cres2 = AL("cres2", [128, 2 * D], BF16)
    rcpt = AL("rcpt", [128, 2 * NH], F32)
    zbuf = AL("zbuf", [128, D], F32)
    zzbuf = AL("zzbuf", [128, D], F32)
    lns = AL("lns", [128, 16], F32)
    xnf = AL("xnf", [128, 4 * D], F32)
    xnT = AL("xnT", [128, DC * NQ], BF16)
    hT = AL("hT", [128, NHT * 512], BF16)
    z2T = AL("z2T", [128, 2 * 512], F32)
    z2n = AL("z2n", [128, 4 * D], BF16)
    ostage = AL("ostage", [128, 2 * D], F32)

    ctx_r = ctx_s.rearrange("(h t p) j -> t p h j", p=128, t=NTT)

    def emit_ln(zin, outbuf, col):
        s0 = lns[:, col:col + 1]
        s1 = lns[:, col + 1:col + 2]
        s2 = lns[:, col + 2:col + 3]
        W = lambda: em.wait("ve", "ve_ln", em.val("ve_ln"))
        W()
        em.op("ve", lambda E: E.reduce_sum(s0, zin, axis=AX), inc="ve_ln")
        em.op("ve", lambda E: E.scalar_tensor_tensor(
            zzbuf[:, 0:D], zin, 1.0, zin, ALU.mult, ALU.mult, accum_out=s1), inc="ve_ln")
        W()
        em.op("ve", lambda E: E.tensor_single_scalar(s0, s0, 1.0 / D, ALU.mult), inc="ve_ln")
        W()
        em.op("ve", lambda E: E.tensor_single_scalar(s1, s1, 1.0 / D, ALU.mult), inc="ve_ln")
        W()
        em.op("ve", lambda E: E.scalar_tensor_tensor(
            s2, s0, -1.0, s0, ALU.mult, ALU.mult), inc="ve_ln")
        W()
        em.op("ve", lambda E: E.tensor_add(s1, s1, s2), inc="ve_ln")
        em.wait("act", "ve_ln", em.val("ve_ln"))
        em.op("act", lambda E: E.activation(s2, s1, AF.Sqrt, bias=lneps[0:NHX, 0:1]), inc="act_ln")
        em.wait("ve", "act_ln", em.val("act_ln"))
        em.op("ve", lambda E: E.reciprocal(s1, s2), inc="ve_ln")
        W()
        em.op("ve", lambda E: E.tensor_scalar(outbuf, zin, s0, s1,
                                              ALU.subtract, ALU.mult), inc="ve_ln")

    def emit_phase_f():
        em.wait("act", "ve_w", em.val("ve_w"))
        em.wait("act", "sp_w", em.val("sp_w"))
        em.wait("pe", "act_e", em.val("act_e"))
        for eng in ("sp", "gp", "ve", "act"):
            em.wait(eng, "pe_s", em.val("pe_s"))
            em.wait(eng, "pe_av", em.val("pe_av"))
            em.wait(eng, "pe_ct", em.val("pe_ct"))
            em.wait(eng, "ve_cs", em.val("ve_cs"))
            em.wait(eng, "ve_ct2", em.val("ve_ct2"))
            for nm in ("sp_kt0", "sp_kt1", "gp_kt0", "gp_kt1", "gp_vh0", "gp_vh1",
                       "gp_qt", "gp_cs"):
                em.wait(eng, nm, em.val(nm))
        em.op("gp", lambda E: E.dma_start(
            w1_sb[:, :].rearrange("p (dc h) -> p dc h", h=HF),
            w1_d.rearrange("(dc p) h -> p dc h", p=128)), inc="gp_wffn", n=16)
        em.op("gp", lambda E: E.dma_start(
            w2_sb[:, :].rearrange("p (ht d) -> p ht d", d=D),
            w2_d.rearrange("(ht p) d -> p ht d", p=128)), inc="gp_wffn", n=16)
        em.wait("pe", "gp_wffn", em.val("gp_wffn"))
        tp_slots = [psA[:, 0:128], psA[:, 512:640]]

        def ln1_ve(t):
                xb = xres[:, (t % 2) * D:(t % 2) * D + D]
                cb = cres[:, (t % 2) * NH * 65:(t % 2) * NH * 65 + NH * 65]
                c2 = cres2[:, (t % 2) * D:(t % 2) * D + D]
                rc = rcpt[:, (t % 2) * NH:(t % 2) * NH + NH]
                xo = xnf[:, (t % 4) * D:(t % 4) * D + D]
                if t >= 2:
                    em.wait("sp", "ve_z", em.val("ve_z") - 1)
                em.op("sp", lambda E, xb=xb, t=t: E.dma_start(
                    xb, x_d[t * 128:t * 128 + 128, :]), inc="sp_xr%d" % (t % 2), n=16)
                if t >= 2:
                    em.wait("gp", "ve_rcp", em.val("ve_rcp") - 1)
                em.op("gp", lambda E, cb=cb, t=t: E.dma_start(
                    cb.rearrange("p (h j) -> p h j", j=65), ctx_r[t]),
                    inc="gp_cr%d" % (t % 2), n=16)
                em.wait("ve", "sp_xr%d" % (t % 2), em.val("sp_xr%d" % (t % 2)))
                em.wait("ve", "gp_cr%d" % (t % 2), em.val("gp_cr%d" % (t % 2)))
                cbh = cb.rearrange("p (h j) -> p h j", j=65)
                if t >= 2:
                    em.wait("ve", "ve_z", em.val("ve_z") - 1)
                em.op("ve", lambda E, cbh=cbh, rc=rc: E.reciprocal(
                    rc.rearrange("p (h o) -> p h o", o=1), cbh[:, :, 64:65]),
                    inc="ve_rcp")
                em.wait("ve", "ve_rcp", em.val("ve_rcp"))
                rc_b = bass.AP(rcpt, (t % 2) * NH, [[2 * NH, 128], [1, NH], [0, 64]])
                c2h = c2.rearrange("p (h j) -> p h j", j=64)
                em.op("ve", lambda E, c2h=c2h, cbh=cbh, rb=rc_b: E.tensor_mul(
                    c2h, cbh[:, :, 0:64], rb), inc="ve_rcp")
                em.wait("ve", "ve_rcp", em.val("ve_rcp"))
                em.wait("ve", "pe_xn", em.val("pe_xn"))
                em.op("ve", lambda E, xb=xb, c2=c2: E.scalar_tensor_tensor(
                    zbuf[:, :], xb, 1.0, c2, ALU.mult, ALU.add), inc="ve_z")
                em.wait("ve", "ve_z", em.val("ve_z"))
                emit_ln(zbuf[:, :], xo, 0)

        def ln1_pe(t):
                xo = xnf[:, (t % 4) * D:(t % 4) * D + D]
                em.wait("pe", "ve_ln", lnval_at[t])
                for dc in range(DC):
                    k = t * DC + dc
                    ecnt = "evac_xnv" if k % 2 == 0 else "evac_xna"
                    if k >= 2:
                        em.wait("pe", ecnt, em.val(ecnt))
                    em.op("pe", lambda E, s=tp_slots[k % 2], dc=dc, xo=xo: E.transpose(
                        s, xo[:, dc * 128:dc * 128 + 128], ident[:, :]), inc="pe_xn")
                    eng = "ve" if dc % 2 == 0 else "act"
                    em.wait(eng, "pe_xn", em.val("pe_xn"))
                    dst = xnT[:, dc * NQ + t * 128: dc * NQ + t * 128 + 128]
                    if eng == "ve":
                        em.op("ve", lambda E, s=tp_slots[k % 2], d=dst: E.tensor_copy(d, s),
                              inc=ecnt)
                    else:
                        em.op("act", lambda E, s=tp_slots[k % 2], d=dst: E.copy(d, s),
                              inc=ecnt)

        lnval_at = {}

        def ln1_batch_ve(b):
            for t in range(4 * b, min(4 * b + 4, NTT)):
                ln1_ve(t)
                lnval_at[t] = em.val("ve_ln")

        def ln1_batch_pe(b):
            for t in range(4 * b, min(4 * b + 4, NTT)):
                ln1_pe(t)

        NTC = NQ // 512
        mm_slots = [psB[:, 0:512], psB[:, 512:1024]]
        tslots = [psA[:, 1024:1152], psT[:, 0:128]]
        ln1_batch_ve(0)
        ln1_batch_pe(0)
        for tc in range(NTC):
            if tc + 1 < NTC:
                ln1_batch_ve(tc + 1)
            em.wait("pe", "evac_xnv", em.val("evac_xnv"))
            em.wait("pe", "evac_xna", em.val("evac_xna"))
            em.wait("pe", "act_y", em.val("act_y"))
            for ht in range(NHT):
                mi = tc * NHT + ht
                m = mm_slots[mi % 2]
                if mi >= 2:
                    em.wait("pe", "act_h", em.val("act_h") - 1)
                for dc in range(DC):
                    em.op("pe", lambda E, m=m, dc=dc, ht=ht, tc=tc: E.matmul(
                        m, w1_sb[:, dc * HF + ht * 128: dc * HF + ht * 128 + 128],
                        xnT[:, dc * NQ + tc * 512: dc * NQ + tc * 512 + 512],
                        start=(dc == 0), stop=(dc == DC - 1)),
                        inc=("pe_h" if dc == DC - 1 else None))
                em.wait("act", "pe_h", em.val("pe_h"))
                em.op("act", lambda E, m=m, ht=ht: E.activation(
                    hT[:, ht * 512:ht * 512 + 512], m, AF.Relu,
                    bias=b1_sb[:, ht:ht + 1], scale=1.0), inc="act_h")
            def z2_transposes(dt, zslot):
                em.wait("pe", "ve_y", em.val("ve_y"))
                for st4 in range(4):
                    k = dt * 4 + st4
                    sl = tslots[k % 2]
                    if True:
                        em.wait("pe", "ve_z2", max(0, em.val("ve_z2") - 1))
                    em.op("pe", lambda E, sl=sl, st4=st4, z=zslot: E.transpose(
                        sl, z[:, st4 * 128:st4 * 128 + 128], ident[:, :]), inc="pe_z2")
                    em.wait("ve", "pe_z2", em.val("pe_z2"))
                    em.op("ve", lambda E, sl=sl, st4=st4, dt=dt: E.tensor_copy(
                        z2n[:, st4 * D + dt * 128: st4 * D + dt * 128 + 128], sl),
                        inc="ve_z2")

            pending_z = None
            for dt in range(NJT):
                mi2 = tc * NJT + dt
                m = mm_slots[mi2 % 2]
                zslot = z2T[:, (dt % 2) * 512:(dt % 2) * 512 + 512]
                em.wait("pe", "act_h", em.val("act_h"))
                if mi2 >= 2:
                    em.wait("pe", "act_y", em.val("act_y") - 1)
                for ht in range(NHT):
                    em.op("pe", lambda E, m=m, ht=ht, dt=dt: E.matmul(
                        m, w2_sb[:, ht * D + dt * 128: ht * D + dt * 128 + 128],
                        hT[:, ht * 512:ht * 512 + 512],
                        start=(ht == 0), stop=(ht == NHT - 1)),
                        inc=("pe_y" if ht == NHT - 1 else None))
                if pending_z is not None:
                    z2_transposes(*pending_z)
                em.wait("act", "pe_y", em.val("pe_y"))
                em.wait("act", "pe_z2", max(0, em.val("pe_z2") - 4))
                em.op("act", lambda E, m=m, dt=dt, z=zslot: E.activation(
                    z, m, AF.Identity, bias=b2_sb[:, dt:dt + 1], scale=1.0),
                    inc="act_y")
                em.wait("ve", "act_y", em.val("act_y"))
                em.op("ve", lambda E, dt=dt, tc=tc, z=zslot: E.tensor_tensor(
                    z, z, xnT[:, dt * NQ + tc * 512: dt * NQ + tc * 512 + 512], ALU.add),
                    inc="ve_y")
                pending_z = (dt, zslot)
            z2_transposes(*pending_z)
            for st4 in range(4):
                oi = tc * 4 + st4
                em.wait("ve", "ve_z2", em.val("ve_z2"))
                ob = ostage[:, (oi % 2) * D:(oi % 2) * D + D]
                if oi >= 2:
                    em.wait("ve", "sp_out%d" % (oi % 2), em.val("sp_out%d" % (oi % 2)))
                emit_ln(z2n[:, st4 * D:st4 * D + D], ob, 4)
                em.wait("sp", "ve_ln", em.val("ve_ln"))
                em.op("sp", lambda E, ob=ob, tc=tc, st4=st4: E.dma_start(
                    out_d[tc * 512 + st4 * 128: tc * 512 + st4 * 128 + 128, :], ob),
                    inc="sp_out%d" % (oi % 2), n=16)
            if tc + 1 < NTC:
                ln1_batch_pe(tc + 1)
        if "xn" in debug:
            for dc in range(DC):
                em.wait("gp", "evac_xnv", em.val("evac_xnv"))
                em.wait("gp", "evac_xna", em.val("evac_xna"))
                em.op("gp", lambda E, dc=dc: E.dma_start(
                    dbg["xn"][dc * 128:(dc + 1) * 128, :],
                    xnT[:, dc * NQ:(dc + 1) * NQ]), inc="sp_dbgx", n=16)
        em.wait("sp", "sp_out0", em.val("sp_out0"))
        em.wait("sp", "sp_out1", em.val("sp_out1"))

    emit_phase_w()
    emit_phase_p()
    emit_phase_s()
    emit_phase_a()
    emit_phase_f()

    with nc.Block() as blk:
        em.run(blk)
    return nc


_NC_CACHE = {}


def _get_nc(key, cfg, debug=()):
    if key not in _NC_CACHE:
        _NC_CACHE[key] = build(cfg, debug=debug)
    return _NC_CACHE[key]


def kernel(L0_attn_hidden_states, L0_attn_qk_w, L0_attn_qk_b, L0_attn_v_w,
           L0_attn_v_b, L0_attn_hash_a, L0_ffn_w_gate, L0_ffn_w_noise,
           L0_ffn_W1, L0_ffn_b1, L0_ffn_W2, L0_ffn_b2):
    x = np.ascontiguousarray(np.asarray(L0_attn_hidden_states, dtype=np.float32))
    Bsz, L, D = x.shape
    cfg = CFG(L=L, NH=D // 64, HF=np.asarray(L0_ffn_W1).shape[2])
    nc = _get_nc(("full", L, D), cfg)

    com = {
        "qk_w": np.ascontiguousarray(L0_attn_qk_w, dtype=np.float32),
        "qk_b": np.ascontiguousarray(L0_attn_qk_b, dtype=np.float32),
        "v_w": np.ascontiguousarray(L0_attn_v_w, dtype=np.float32),
        "v_b": np.ascontiguousarray(L0_attn_v_b, dtype=np.float32),
        "hash_a": np.ascontiguousarray(L0_attn_hash_a, dtype=np.float32),
        "w1": np.ascontiguousarray(np.asarray(L0_ffn_W1)[1], dtype=np.float32),
        "b1": np.ascontiguousarray(np.asarray(L0_ffn_b1)[1], dtype=np.float32),
        "w2": np.ascontiguousarray(np.asarray(L0_ffn_W2)[1], dtype=np.float32),
        "b2": np.ascontiguousarray(np.asarray(L0_ffn_b2)[1], dtype=np.float32),
    }
    NQ = L // 2
    in_maps = []
    for c in range(8):
        b = c // 2
        if c % 2 == 0:
            xb = x[b]
        else:
            xb = np.concatenate([x[b, NQ:], x[b, :NQ]], axis=0)
        m = dict(com)
        m["x"] = np.ascontiguousarray(xb)
        in_maps.append(m)

    res = run_bass_kernel_spmd(nc, in_maps, core_ids=list(range(8)))
    out = np.empty((Bsz, L, D), dtype=np.float32)
    for c in range(8):
        b = c // 2
        r = res.results[c]["out"]
        if c % 2 == 0:
            out[b, :NQ] = r
        else:
            out[b, NQ:] = r
    return out


# version 06
# speedup vs baseline: 1.0924x; 1.0116x over previous
import numpy as np
import concourse.bass as bass
import concourse.mybir as mybir
from concourse.bass_utils import run_bass_kernel_spmd

F32 = mybir.dt.float32
BF16 = mybir.dt.bfloat16
AF = mybir.ActivationFunctionType
ALU = mybir.AluOpType
AX = mybir.AxisListType.X

CBIT = 4992.0
EXP_BIAS = -(CBIT + 5.0)
LN_EPS = 1e-5

ENGS = ("pe", "act", "ve", "sp", "gp")


class CFG:
    def __init__(self, L=8192, NH=16, HF=2048):
        self.L = L
        self.NH = NH
        self.D = NH * 64
        self.NQ = L // 2
        self.HF = HF
        self.DC = self.D // 128
        self.NJT = self.D // 128
        self.NKT = L // 128
        self.NQC = self.NQ // 512
        self.NTT = self.NQ // 128


class Emitter:
    def __init__(self, nc):
        self.nc = nc
        self.streams = {e: [] for e in ENGS}
        self.sems = {}
        self.cnt = {}

    def sem(self, name):
        if name not in self.sems:
            self.sems[name] = self.nc.alloc_semaphore(name)
            self.cnt[name] = 0
        return self.sems[name]

    def val(self, name):
        self.sem(name)
        return self.cnt[name]

    def op(self, eng, fn, inc=None, n=1):
        if inc is not None:
            h = self.sem(inc)
            self.cnt[inc] += n

            def wrapped(E, fn=fn, h=h, n=n):
                fn(E).then_inc(h, n)
            self.streams[eng].append(wrapped)
        else:
            self.streams[eng].append(fn)

    def wait(self, eng, name, value=None):
        if value is None:
            value = self.val(name)
        if value <= 0:
            return
        h = self.sem(name)
        self.streams[eng].append(lambda E, h=h, v=value: E.wait_ge(h, v))

    def run(self, block):
        for eng, meth in (("pe", block.tensor), ("act", block.scalar),
                          ("ve", block.vector), ("sp", block.sync),
                          ("gp", block.gpsimd)):
            fns = self.streams[eng]
            if fns:
                meth(lambda E, fns=fns: [f(E) for f in fns])
            self.streams[eng] = []


def build(cfg, debug=()):
    nc = bass.Bass("TRN2", target_bir_lowering=False)
    em = Emitter(nc)
    L, NH, D, NQ, HF = cfg.L, cfg.NH, cfg.D, cfg.NQ, cfg.HF
    DC, NJT, NKT, NQC, NTT = cfg.DC, cfg.NJT, cfg.NKT, cfg.NQC, cfg.NTT
    NHT = HF // 128
    NQUARTER = 4
    QL = L // NQUARTER
    QT_TILES = QL // 128
    QTC = QL // 512
    assert QTC >= 1 and NQC >= 1
    NHX = 128
    VHW = min(512, D)
    NVH = D // VHW
    BCW = min(512, D)

    x_d = nc.dram_tensor("x", [L, D], F32, kind="ExternalInput").ap()
    qkw_d = nc.dram_tensor("qk_w", [D, D], F32, kind="ExternalInput").ap()
    qkb_d = nc.dram_tensor("qk_b", [D], F32, kind="ExternalInput").ap()
    vw_d = nc.dram_tensor("v_w", [D, D], F32, kind="ExternalInput").ap()
    vb_d = nc.dram_tensor("v_b", [D], F32, kind="ExternalInput").ap()
    ha_d = nc.dram_tensor("hash_a", [66], F32, kind="ExternalInput").ap()
    w1_d = nc.dram_tensor("w1", [D, HF], F32, kind="ExternalInput").ap()
    b1_d = nc.dram_tensor("b1", [HF], F32, kind="ExternalInput").ap()
    w2_d = nc.dram_tensor("w2", [HF, D], F32, kind="ExternalInput").ap()
    b2_d = nc.dram_tensor("b2", [D], F32, kind="ExternalInput").ap()
    out_d = nc.dram_tensor("out", [NQ, D], F32, kind="ExternalOutput").ap()

    qkT_s = nc.dram_tensor("qkT_s", [D, L], BF16, kind="Internal").ap()
    xbf_s = nc.dram_tensor("xbf_s", [L, D], BF16, kind="Internal").ap()
    qkw_bf = nc.dram_tensor("qkw_bf", [D, D], BF16, kind="Internal").ap()
    vw_bf = nc.dram_tensor("vw_bf", [D, D], BF16, kind="Internal").ap()
    v_s = nc.dram_tensor("v_s", [L, D], BF16, kind="Internal").ap()
    ctx_s = nc.dram_tensor("ctx_s", [NH * NQ, 65], BF16, kind="Internal").ap()
    bits_s = nc.dram_tensor("bits_s", [NH, L + NQ], BF16, kind="Internal").ap()

    dbg = {}
    if "qkT" in debug:
        dbg["qkT"] = nc.dram_tensor("dbg_qkT", [D, L], F32, kind="ExternalOutput").ap()
    if "stats" in debug:
        dbg["dot1"] = nc.dram_tensor("dbg_dot1", [NH, L], F32, kind="ExternalOutput").ap()
        dbg["n2"] = nc.dram_tensor("dbg_n2", [NH, L], F32, kind="ExternalOutput").ap()
        dbg["bits"] = nc.dram_tensor("dbg_bits", [NH, L + NQ], F32, kind="ExternalOutput").ap()
    if "v" in debug:
        dbg["v"] = nc.dram_tensor("dbg_v", [L, D], F32, kind="ExternalOutput").ap()
    if "ctx" in debug:
        dbg["ctx"] = nc.dram_tensor("dbg_ctx", [NH * NQ, 65], F32, kind="ExternalOutput").ap()
    if "xn" in debug:
        dbg["xn"] = nc.dram_tensor("dbg_xn", [D, NQ], F32, kind="ExternalOutput").ap()

    guards = []

    def AL(name, shape, dtype, side="left"):
        g = nc.sbuf_tensor(name, shape, dtype, side=side)
        t = g.__enter__()
        guards.append((name, g))
        return t

    def FREE(names):
        for name in reversed(names):
            for i in range(len(guards) - 1, -1, -1):
                if guards[i][0] == name:
                    guards[i][1].__exit__(None, None, None)
                    guards.pop(i)
                    break

    ident = AL("ident", [128, 128], F32)
    expb = AL("expb", [128, 1], F32)
    lneps = AL("lneps", [128, 1], F32)
    st = AL("st", [NH, 16], F32)
    a23 = AL("a23", [NH, 2], F32)
    qkb_sb = AL("qkb_sb", [128, NJT], F32)
    b1_sb = AL("b1_sb", [128, NHT], F32)
    b2_sb = AL("b2_sb", [128, NJT], F32)

    dbufs = {}
    if "qkT" in debug:
        dbufs["dbq"] = AL("dbq", [128, L], F32)
    if "v" in debug:
        dbufs["dbv"] = AL("dbv", [128, D], F32)
    if "ctx" in debug:
        dbufs["dbc"] = AL("dbc", [128, NTT * 65], F32)

    dot1 = AL("dot1", [NH, L], F32)
    n2 = AL("n2", [NH, L], BF16)
    bits = AL("bits", [NH, L + NQ], BF16)

    qk_wT = AL("qk_wT", [128, DC * D], BF16, side="right")
    v_wT = AL("v_wT", [128, DC * D], BF16, side="right")
    vbb_sb = AL("vbb_sb", [128, D], BF16, side="right")
    a1sel = AL("a1sel", [128, NJT * NH], F32, side="right")
    n1sel = AL("n1sel", [128, NJT * NH], BF16, side="right")
    onesb = AL("onesb", [1, 128], BF16, side="right")
    vbrow = AL("vbrow", [1, D], BF16, side="right")
    onesf = AL("onesf", [1, NH], F32, side="right")

    psA = nc.alloc_psum_tensor("psA", [128, 1536], F32)
    psB = nc.alloc_psum_tensor("psB", [128, 1536], F32)
    psC = nc.alloc_psum_tensor("psC", [128, 512], F32)
    psT = nc.alloc_psum_tensor("psT", [128, 512], F32)


    def emit_phase_w():
        em.op("gp", lambda E: E.memset(ident[:, :], 0.0), inc="gp_i0")
        em.wait("gp", "gp_i0")
        em.op("gp", lambda E: E.affine_select(
            out=ident[:, :], in_=ident[:, :], compare_op=ALU.not_equal,
            fill=1.0, base=0, pattern=[[-1, 128]], channel_multiplier=1), inc="gp_init")
        em.op("sp", lambda E: E.dma_start(
            qkb_sb[:, 0:NJT], qkb_d.rearrange("(j p) -> p j", p=128),
            allow_slow_non_contiguous=True), inc="sp_w", n=16)
        em.op("sp", lambda E: E.dma_start(
            b1_sb[:, 0:NHT], b1_d.rearrange("(t p) -> p t", p=128),
            allow_slow_non_contiguous=True), inc="sp_w", n=16)
        em.op("sp", lambda E: E.dma_start(
            b2_sb[:, 0:NJT], b2_d.rearrange("(t p) -> p t", p=128),
            allow_slow_non_contiguous=True), inc="sp_w", n=16)
        em.op("ve", lambda E: E.memset(expb[:, :], EXP_BIAS), inc="ve_w")
        em.op("ve", lambda E: E.memset(lneps[:, :], LN_EPS), inc="ve_w")
        em.op("ve", lambda E: E.memset(a1sel[:, :], 0.0), inc="ve_w")
        em.op("ve", lambda E: E.memset(n1sel[:, :], 0.0), inc="ve_w")
        em.wait("ve", "ve_w", em.val("ve_w"))
        for jt in range(NJT):
            c0 = jt * NH + 2 * jt
            em.wait("sp", "ve_w", em.val("ve_w"))
            em.op("sp", lambda E, c0=c0: E.dma_start(
                a1sel[0:64, c0:c0 + 1],
                ha_d[0:64].rearrange("(p o) -> p o", o=1),
                allow_slow_non_contiguous=True), inc="sp_w", n=16)
            em.op("sp", lambda E, c0=c0: E.dma_start(
                a1sel[64:128, c0 + 1:c0 + 2],
                ha_d[0:64].rearrange("(p o) -> p o", o=1),
                allow_slow_non_contiguous=True), inc="sp_w", n=16)
            em.op("ve", lambda E, c0=c0: E.memset(n1sel[0:64, c0:c0 + 1], 1.0))
            em.op("ve", lambda E, c0=c0: E.memset(n1sel[64:128, c0 + 1:c0 + 2], 1.0),
                  inc="ve_w")
        em.op("ve", lambda E: E.memset(onesb[:, :], 1.0), inc="ve_w")
        em.op("sp", lambda E: E.dma_start(
            st[0:1, 0:2], ha_d[64:66].rearrange("(o c) -> o c", o=1)), inc="sp_w", n=16)
        em.op("gp", lambda E: E.dma_start(
            vbrow[0:1, :], vb_d.rearrange("(o c) -> o c", o=1)), inc="gp_w", n=16)
        em.wait("pe", "gp_init")
        em.wait("pe", "ve_w")
        em.wait("pe", "sp_w")
        em.wait("pe", "gp_w")
        em.op("ve", lambda E: E.memset(onesf[:, :], 1.0), inc="ve_w")
        em.wait("pe", "ve_w")
        em.op("pe", lambda E: E.matmul(psC[0:NH, 0:2], onesf[0:1, 0:NH],
                                       st[0:1, 0:2], start=True, stop=True), inc="pe_w")
        em.wait("ve", "pe_w", em.val("pe_w"))
        em.op("ve", lambda E: E.tensor_copy(a23[:, :], psC[0:NH, 0:2]), inc="ve_a23")
        for half in range(D // BCW):
            em.wait("pe", "ve_vbb", em.val("ve_vbb") if half > 0 else 0)
            em.wait("pe", "ve_a23", em.val("ve_a23"))
            em.op("pe", lambda E, h=half: E.matmul(
                psC[:, 0:BCW], onesb[0:1, 0:128],
                vbrow[0:1, h * BCW:(h + 1) * BCW], start=True, stop=True), inc="pe_w")
            em.wait("ve", "pe_w", em.val("pe_w"))
            em.op("ve", lambda E, h=half: E.tensor_copy(
                vbb_sb[:, h * BCW:(h + 1) * BCW], psC[:, 0:BCW]), inc="ve_vbb")
        em.op("gp", lambda E: E.dma_start(qkw_bf, qkw_d), inc="gp_wbf", n=16)
        em.op("gp", lambda E: E.dma_start(vw_bf, vw_d), inc="gp_wbf", n=16)
        em.op("gp", lambda E: E.dma_start(xbf_s, x_d), inc="gp_xbf", n=16)
        em.wait("sp", "gp_wbf", em.val("gp_wbf"))
        for wi, (src, dstT) in enumerate(((qkw_bf, qk_wT), (vw_bf, v_wT))):
            for dc in range(DC):
                em.op("sp", lambda E, s=src, d=dstT, dc=dc: E.dma_start(
                    d[:, dc * D:(dc + 1) * D], s[:, dc * 128:(dc + 1) * 128],
                    transpose=True), inc="sp_wt", n=16)

    xT = AL("xT", [128, 2 * DC * QL], BF16, side="right")
    qkf = AL("qkf", [128, 2 * 512], F32, side="right")
    sqb = AL("sqb", [128, 2 * 512], BF16, side="right")
    qstage = AL("qstage", [128, 2 * NJT * 512], BF16, side="right")
    vstage = AL("vstage", [128, 2 * D], BF16, side="right")

    def emit_phase_p():
        em.wait("act", "sp_w", em.val("sp_w"))
        em.wait("act", "ve_w", em.val("ve_w"))
        mm_slots = [psA[:, 1024:1536], psB[:, 0:512]]
        d1_slots = [psB[:, 512:1024], psB[:, 1024:1536]]
        n2_slots = [psC[:, 0:512], psT[:, 0:512]]
        em.wait("pe", "ve_vbb", em.val("ve_vbb"))
        em.wait("sp", "gp_xbf", em.val("gp_xbf"))
        def issue_xt(q):
            par = q % 2
            for dc in range(DC):
                em.op("sp", lambda E, q=q, dc=dc, par=par: E.dma_start(
                    xT[:, par * DC * QL + dc * QL: par * DC * QL + (dc + 1) * QL],
                    xbf_s[q * QL:(q + 1) * QL, dc * 128:(dc + 1) * 128],
                    transpose=True), inc="sp_xt%d" % par, n=16)

        issue_xt(0)
        for q in range(NQUARTER):
            par = q % 2
            xTq = xT[:, par * DC * QL:(par + 1) * DC * QL]
            if q + 1 < NQUARTER:
                em.wait("sp", "pe_qk", em.val("pe_qk"))
                em.wait("sp", "pe_v", em.val("pe_v"))
                issue_xt(q + 1)
            em.wait("pe", "sp_xt%d" % par, em.val("sp_xt%d" % par))
            em.wait("pe", "sp_wt", em.val("sp_wt"))
            em.wait("pe", "ve_v", em.val("ve_v"))
            for tc in range(QTC):
                gi = q * QTC + tc
                d1b = d1_slots[gi % 2]
                n2b = n2_slots[gi % 2]
                def stats_mms(jt, d1b, n2b, qf, sq):
                    em.wait("pe", "act_qk", em.val("act_qk"))
                    em.op("pe", lambda E, d1b=d1b, jt=jt, qf=qf: E.matmul(
                        d1b[0:NH, :], a1sel[:, jt * NH:jt * NH + NH], qf,
                        start=(jt == 0), stop=(jt == NJT - 1)), inc="pe_d1")
                    em.wait("pe", "ve_sq", em.val("ve_sq"))
                    em.op("pe", lambda E, n2b=n2b, jt=jt, sq=sq: E.matmul(
                        n2b[0:NH, :], n1sel[:, jt * NH:jt * NH + NH], sq,
                        start=(jt == 0), stop=(jt == NJT - 1)), inc="pe_n2")

                pending = None
                for jt in range(NJT):
                    mi = gi * NJT + jt
                    mmb = mm_slots[mi % 2]
                    if mi >= 2:
                        em.wait("pe", "act_qk", em.val("act_qk") - 1)
                    if jt == 0 and gi >= 2:
                        em.wait("pe", "ve_d1", em.val("ve_d1") - 1)
                        em.wait("pe", "ve_n2", em.val("ve_n2") - 1)
                    for dc in range(DC):
                        em.op("pe", lambda E, m=mmb, dc=dc, jt=jt, tc=tc, xTq=xTq: E.matmul(
                            m, qk_wT[:, (dc * NJT + jt) * 128:(dc * NJT + jt) * 128 + 128],
                            xTq[:, dc * QL + tc * 512: dc * QL + tc * 512 + 512],
                            start=(dc == 0), stop=(dc == DC - 1)),
                            inc=("pe_qk" if dc == DC - 1 else None))
                    if pending is not None:
                        stats_mms(*pending)
                    qf = qkf[:, (mi % 2) * 512:(mi % 2) * 512 + 512]
                    sq = sqb[:, (mi % 2) * 512:(mi % 2) * 512 + 512]
                    em.wait("act", "pe_qk", em.val("pe_qk"))
                    if mi >= 2:
                        em.wait("act", "pe_d1", em.val("pe_d1"))
                    em.op("act", lambda E, m=mmb, qf=qf, jt=jt: E.activation(
                        qf, m, AF.Identity, bias=qkb_sb[:, jt:jt + 1], scale=1.0),
                        inc="act_qk")
                    em.wait("ve", "act_qk", em.val("act_qk"))
                    if mi >= 2:
                        em.wait("ve", "pe_n2", em.val("pe_n2"))
                    em.op("ve", lambda E, sq=sq, qf=qf: E.tensor_mul(sq, qf, qf),
                          inc="ve_sq")
                    stg = qstage[:, (gi % 2) * NJT * 512 + jt * 512:
                                 (gi % 2) * NJT * 512 + jt * 512 + 512]
                    if gi >= 2 and jt == 0:
                        em.wait("ve", "gp_qs%d" % (gi % 2), em.val("gp_qs%d" % (gi % 2)))
                    em.op("ve", lambda E, stg=stg, qf=qf: E.tensor_copy(stg, qf),
                          inc="ve_qs")
                    pending = (jt, d1b, n2b, qf, sq)
                stats_mms(*pending)
                em.wait("ve", "pe_d1", em.val("pe_d1"))
                em.wait("ve", "pe_n2", em.val("pe_n2"))
                cs = q * QL + tc * 512
                em.op("ve", lambda E, d1b=d1b, cs=cs: E.tensor_copy(
                    dot1[:, cs:cs + 512], d1b[0:NH, :]), inc="ve_d1")
                em.op("ve", lambda E, n2b=n2b, cs=cs: E.tensor_copy(
                    n2[:, cs:cs + 512], n2b[0:NH, :]), inc="ve_n2")
                em.wait("gp", "ve_qs", em.val("ve_qs"))
                src = qstage[:, (gi % 2) * NJT * 512:(gi % 2) * NJT * 512 + NJT * 512]
                dst = qkT_s.rearrange("(jt p) Lx -> p jt Lx", p=128)[:, :, cs:cs + 512]
                em.op("gp", lambda E, s=src, d=dst: E.dma_start(
                    d, s.rearrange("p (jt c) -> p jt c", c=512)),
                    inc="gp_qs%d" % (gi % 2), n=16)
            em.wait("pe", "act_qk", em.val("act_qk"))
            for t in range(QT_TILES):
                gt = q * QT_TILES + t
                vs = vstage[:, (gt % 2) * D:(gt % 2) * D + D]
                if gt >= 2:
                    em.wait("ve", "gp_vs%d" % (gt % 2), em.val("gp_vs%d" % (gt % 2)))
                for half in range(NVH):
                    vi = gt * NVH + half
                    mmb = mm_slots[vi % 2]
                    em.wait("pe", "ve_v", max(0, em.val("ve_v") - 1))
                    for dc in range(DC):
                        em.op("pe", lambda E, m=mmb, dc=dc, t=t, h=half, xTq=xTq: E.matmul(
                            m[:, 0:VHW],
                            xTq[:, dc * QL + t * 128: dc * QL + t * 128 + 128],
                            v_wT[:, dc * D + h * VHW: dc * D + h * VHW + VHW],
                            start=(dc == 0), stop=(dc == DC - 1)),
                            inc=("pe_v" if dc == DC - 1 else None))
                    em.wait("ve", "pe_v", em.val("pe_v"))
                    em.op("ve", lambda E, m=mmb, vs=vs, h=half: E.tensor_tensor(
                        vs[:, h * VHW:h * VHW + VHW], m[:, 0:VHW],
                        vbb_sb[:, h * VHW:h * VHW + VHW], ALU.add), inc="ve_v")
                em.wait("gp", "ve_v", em.val("ve_v"))
                em.op("gp", lambda E, vs=vs, q=q, t=t: E.dma_start(
                    v_s[q * QL + t * 128: q * QL + t * 128 + 128, :], vs),
                    inc="gp_vs%d" % (gt % 2), n=16)
        if "qkT" in debug:
            db = dbufs["dbq"]
            for jt in range(NJT):
                em.wait("gp", "gp_qs0", em.val("gp_qs0"))
                em.wait("gp", "gp_qs1", em.val("gp_qs1"))
                em.wait("gp", "sp_dbg", em.val("sp_dbg"))
                em.op("gp", lambda E, jt=jt: E.dma_start(
                    db[:, :], qkT_s[jt * 128:(jt + 1) * 128, :]), inc="sp_dbg", n=16)
                em.wait("gp", "sp_dbg")
                em.op("gp", lambda E, jt=jt: E.dma_start(
                    dbg["qkT"][jt * 128:(jt + 1) * 128, :], db[:, :]), inc="sp_dbg", n=16)
        if "v" in debug:
            dbv = dbufs["dbv"]
            for t in range(L // 128):
                em.wait("gp", "gp_vs0", em.val("gp_vs0"))
                em.wait("gp", "gp_vs1", em.val("gp_vs1"))
                em.wait("gp", "sp_dbgv", em.val("sp_dbgv"))
                em.op("gp", lambda E, t=t: E.dma_start(
                    dbv[:, :], v_s[t * 128:(t + 1) * 128, :]), inc="sp_dbgv", n=16)
                em.wait("gp", "sp_dbgv")
                em.op("gp", lambda E, t=t: E.dma_start(
                    dbg["v"][t * 128:(t + 1) * 128, :], dbv[:, :]), inc="sp_dbgv", n=16)

    FREE(["xT", "qkf", "sqb", "qstage", "vstage"])
    FREE(["onesf"])
    FREE(["qk_wT", "v_wT", "vbb_sb", "a1sel", "n1sel", "onesb", "vbrow"])
    hkv = AL("hkv", [NH, L], F32)

    def emit_phase_s():
        V = "ve"
        for nm in ("pe_qk", "pe_v", "pe_d1", "pe_n2", "gp_qs0", "gp_qs1",
                   "gp_vs0", "gp_vs1", "act_qk", "ve_qs", "ve_v", "ve_sq"):
            em.wait(V, nm, em.val(nm))
        em.wait(V, "ve_a23", em.val("ve_a23"))
        c = lambda i: st[:, i:i + 1]
        em.wait(V, "ve_s", em.val("ve_s"))
        em.op(V, lambda E: E.reduce_max(c(0), n2[:, :], axis=AX), inc="ve_s")
        em.wait("act", "ve_s", em.val("ve_s"))
        em.op("act", lambda E: E.activation(c(1), c(0), AF.Sqrt), inc="act_s")
        em.wait(V, "act_s", em.val("act_s"))
        em.wait(V, "ve_s", em.val("ve_s"))
        em.op(V, lambda E: E.tensor_single_scalar(c(2), c(1), 1e-12, ALU.max), inc="ve_s")
        em.wait(V, "ve_s", em.val("ve_s"))
        em.op(V, lambda E: E.reciprocal(c(3), c(2)), inc="ve_s")
        em.wait(V, "ve_s", em.val("ve_s"))
        em.op(V, lambda E: E.tensor_single_scalar(c(2), c(3), 0.75, ALU.mult), inc="ve_s")
        em.wait(V, "ve_s", em.val("ve_s"))
        em.op(V, lambda E: E.tensor_mul(c(4), c(2), c(2)), inc="ve_s")
        em.wait(V, "ve_s", em.val("ve_s"))
        em.op(V, lambda E: E.tensor_scalar_mul(hkv[:, :], n2[:, :], c(4)), inc="ve_s")
        em.wait(V, "ve_s", em.val("ve_s"))
        em.op(V, lambda E: E.tensor_scalar(n2[:, :], hkv[:, :], -1.0, 0.5,
                                           ALU.mult, ALU.add), inc="ve_s")
        em.wait(V, "ve_s", em.val("ve_s"))
        em.op(V, lambda E: E.tensor_mul(hkv[:, :], hkv[:, :], hkv[:, :]), inc="ve_s")
        em.wait(V, "ve_s", em.val("ve_s"))
        em.op(V, lambda E: E.tensor_scalar(hkv[:, :], hkv[:, :], -1.0, 0.5,
                                           ALU.mult, ALU.add), inc="ve_s")
        em.wait(V, "ve_s", em.val("ve_s"))
        em.op(V, lambda E: E.tensor_scalar_mul(hkv[:, :], hkv[:, :], a23[:, 1:2]), inc="ve_s")
        em.wait(V, "ve_s", em.val("ve_s"))
        em.op(V, lambda E: E.scalar_tensor_tensor(
            hkv[:, :], n2[:, :], a23[:, 0:1], hkv[:, :], ALU.mult, ALU.add), inc="ve_s")
        em.wait(V, "ve_s", em.val("ve_s"))
        em.op(V, lambda E: E.scalar_tensor_tensor(
            hkv[:, :], dot1[:, :], c(2), hkv[:, :], ALU.mult, ALU.add), inc="ve_s")
        em.wait(V, "ve_s", em.val("ve_s"))
        em.op(V, lambda E: E.tensor_single_scalar(hkv[:, :], hkv[:, :], 0.0, ALU.is_ge),
              inc="ve_s")
        em.wait(V, "ve_s", em.val("ve_s"))
        em.op(V, lambda E: E.tensor_scalar(bits[:, 0:L], hkv[:, :], 128.0, -64.0,
                                           ALU.mult, ALU.add), inc="ve_s")
        em.wait(V, "ve_s", em.val("ve_s"))
        em.op(V, lambda E: E.reduce_max(c(5), hkv[:, :], axis=AX), inc="ve_s")
        em.wait(V, "ve_s", em.val("ve_s"))
        em.op(V, lambda E: E.tensor_reduce(c(6), hkv[:, :], axis=AX, op=ALU.min), inc="ve_s")
        em.wait(V, "ve_s", em.val("ve_s"))
        em.op(V, lambda E: E.tensor_sub(c(7), c(5), c(6)), inc="ve_s")
        em.wait(V, "ve_s", em.val("ve_s"))
        em.op(V, lambda E: E.tensor_add(c(8), c(5), c(6)), inc="ve_s")
        em.wait(V, "ve_s", em.val("ve_s"))
        em.op(V, lambda E: E.tensor_single_scalar(c(8), c(8), 1.0, ALU.subtract),
              inc="ve_s")
        em.wait(V, "ve_s", em.val("ve_s"))
        em.op(V, lambda E: E.tensor_single_scalar(c(9), c(7), 156.0, ALU.mult), inc="ve_s")
        em.wait(V, "ve_s", em.val("ve_s"))
        em.op(V, lambda E: E.tensor_sub(c(10), c(8), c(7)), inc="ve_s")
        em.wait(V, "ve_s", em.val("ve_s"))
        em.op(V, lambda E: E.tensor_single_scalar(c(10), c(10), 78.0, ALU.mult), inc="ve_s")
        em.wait(V, "ve_s", em.val("ve_s"))
        em.op(V, lambda E: E.tensor_single_scalar(dot1[:, 0:NQ], dot1[:, 0:NQ], 0.0,
                                                  ALU.is_ge), inc="ve_s")
        em.wait(V, "ve_s", em.val("ve_s"))
        em.op(V, lambda E: E.tensor_scalar(bits[:, L:L + NQ], dot1[:, 0:NQ], c(9), c(10),
                                           ALU.mult, ALU.add), inc="ve_s")
        em.wait("gp", "ve_s", em.val("ve_s"))
        em.op("gp", lambda E: E.dma_start(bits_s, bits[:, :]), inc="gp_bits", n=16)
        if "stats" in debug:
            em.wait("gp", "ve_s", em.val("ve_s"))
            em.op("gp", lambda E: E.dma_start(dbg["dot1"], dot1[:, :]), inc="sp_dbgs", n=16)
            em.op("gp", lambda E: E.dma_start(dbg["n2"], n2[:, :]), inc="sp_dbgs", n=16)
            em.op("gp", lambda E: E.dma_start(dbg["bits"], bits[:, :]), inc="sp_dbgs", n=16)

    FREE(["hkv"])
    FREE(["dot1", "n2", "bits"])
    w1_sb = AL("w1_sb", [128, DC * HF], BF16)
    w2_sb = AL("w2_sb", [128, NHT * D], BF16)
    KT = AL("KT", [65, 2 * L], BF16)
    QTb = AL("QTb", [65, 2 * NQ], BF16)
    vhb = AL("vhb", [128, 2 * NKT * 65], BF16)
    ctxT = AL("ctxT", [65, NQ], F32)
    pt4 = AL("pt4", [128, 4 * 1536], BF16)
    cstage = AL("cstage", [128, NTT * 65], BF16)

    groups = []
    k = 0
    while k < NKT:
        size = min(3, NKT - k)
        groups.append((k, size))
        k += size

    ctx_w = ctx_s.rearrange("(h t p) j -> h p t j", p=128, t=NTT)

    def emit_phase_a():
        em.wait("pe", "ve_d1", em.val("ve_d1"))
        em.wait("pe", "ve_n2", em.val("ve_n2"))
        em.wait("pe", "ve_v", em.val("ve_v"))
        em.wait("pe", "act_qk", em.val("act_qk"))
        em.wait("pe", "evac_xp", em.val("evac_xp"))

        em.wait("act", "ve_w", em.val("ve_w"))
        state = {"exp_at_bank": {0: 0, 1: 0}, "ctx_evac": 0,
                 "av_done": {0: 0, 1: 0, 2: 0, 3: 0}}

        def prefetch(h, into):
            kth = KT[:, into * L: into * L + L]
            em.op("sp", lambda E, h=h, kth=kth: E.dma_start(
                kth[0:64, :], qkT_s[h * 64:(h + 1) * 64, :]),
                inc="sp_kt%d" % into, n=16)
            em.op("gp", lambda E, h=h, kth=kth: E.dma_start(
                kth[64:65, :], bits_s[h:h + 1, 0:L]), inc="gp_kt%d" % into, n=16)
            vb = vhb[:, into * NKT * 65: into * NKT * 65 + NKT * 65]
            dstv = vb.rearrange("p (k j) -> p k j", j=65)[:, :, 0:64]
            srcv = v_s[:, h * 64:(h + 1) * 64].rearrange("(k p) j -> p k j", p=128)
            KH = NKT // 2
            em.op("sp", lambda E, d=dstv, s=srcv, KH=KH: E.dma_start(
                d[:, 0:KH, :], s[:, 0:KH, :]), inc="sp_vh%d" % into, n=16)
            em.op("gp", lambda E, d=dstv, s=srcv, KH=KH: E.dma_start(
                d[:, KH:NKT, :], s[:, KH:NKT, :]), inc="gp_vh%d" % into, n=16)

        em.wait("ve", "sp_dbgs", em.val("sp_dbgs"))
        em.wait("ve", "gp_bits", em.val("gp_bits"))
        em.op("ve", lambda E: E.memset(vhb[:, :], 1.0), inc="ve_ones")
        em.wait("gp", "ve_ones", em.val("ve_ones"))
        em.wait("sp", "ve_ones", em.val("ve_ones"))
        em.wait("gp", "gp_bits", em.val("gp_bits"))
        em.wait("sp", "ve_s", em.val("ve_s"))
        em.wait("sp", "gp_bits", em.val("gp_bits"))
        em.wait("sp", "sp_dbgs", em.val("sp_dbgs"))
        em.wait("gp", "sp_dbgs", em.val("sp_dbgs"))
        em.wait("sp", "gp_qs0", em.val("gp_qs0"))
        em.wait("sp", "gp_qs1", em.val("gp_qs1"))
        em.wait("gp", "gp_vs0", em.val("gp_vs0"))
        em.wait("gp", "gp_vs1", em.val("gp_vs1"))
        prefetch(0, 0)
        prefetch(1, 1)

        ct_at_qc = [0] * NQC

        def build_qt(h):
            par = h % 2
            kth = KT[:, par * L: par * L + L]
            qtb = QTb[:, par * NQ: par * NQ + NQ]
            em.wait("ve", "sp_kt%d" % par, em.val("sp_kt%d" % par))
            em.wait("ve", "gp_kt%d" % par, em.val("gp_kt%d" % par))
            em.wait("ve", "pe_s", em.val("pe_s"))
            em.op("ve", lambda E, kth=kth, q=qtb: E.tensor_scalar_mul(
                q[0:64, :], kth[0:64, 0:NQ], 0.125), inc="ve_qt")
            em.wait("gp", "pe_s", em.val("pe_s"))
            em.op("gp", lambda E, h=h, q=qtb: E.dma_start(
                q[64:65, :], bits_s[h:h + 1, L:L + NQ]), inc="gp_qt%d" % par, n=16)

        build_qt(0)
        for h in range(NH):
            par = h % 2
            kth = KT[:, par * L: par * L + L]
            vb = vhb[:, par * NKT * 65: par * NKT * 65 + NKT * 65]
            qtb = QTb[:, par * NQ: par * NQ + NQ]
            if h + 1 < NH:
                build_qt(h + 1)
            em.wait("pe", "ve_qt", em.val("ve_qt") - (1 if h + 1 < NH else 0))
            em.wait("pe", "gp_qt%d" % par, em.val("gp_qt%d" % par))
            em.wait("pe", "gp_vh%d" % par, em.val("gp_vh%d" % par))
            em.wait("pe", "sp_vh%d" % par, em.val("sp_vh%d" % par))
            em.wait("pe", "sp_kt%d" % par, em.val("sp_kt%d" % par))
            em.wait("pe", "gp_kt%d" % par, em.val("gp_kt%d" % par))

            evac_at_qc = [0] * NQC

            def epilogue_block(b, h):
                em.wait("pe", "ve_ctx", evac_at_qc[b])
                em.wait("pe", "ve_cs", em.val("ve_cs"))
                for j in range(4):
                    tt = b * 4 + j
                    em.op("pe", lambda E, j=j, tt=tt: E.transpose(
                        psT[:, j * 80:j * 80 + 65], ctxT[:, tt * 128:tt * 128 + 128],
                        ident[0:65, 0:65]), inc=("pe_ct" if j == 3 else None))
                ct_at_qc[b] = em.val("pe_ct")
                em.wait("ve", "pe_ct", em.val("pe_ct"))
                if b == 0:
                    em.wait("ve", "gp_cs", em.val("gp_cs"))
                em.op("ve", lambda E, b=b: E.tensor_copy(
                    cstage[:, b * 4 * 65:(b + 1) * 4 * 65].rearrange(
                        "p (t j) -> p t j", j=65),
                    psT[:, 0:320].rearrange("p (t j) -> p t j", j=80)[:, :, 0:65]),
                    inc="ve_cs")

            pending_tail = [None]

            def av_group(qc_ctx, j):
                exp_at_g, vbx = qc_ctx
                k0, size = groups[j]
                ptb = pt4[:, (j % 4) * 1536:(j % 4) * 1536 + 1536]
                em.wait("pe", "act_e", exp_at_g[j])
                if k0 == 0:
                    em.wait("pe", "ve_ctx", state["ctx_evac"])
                for j2 in range(size):
                    kt = k0 + j2
                    em.op("pe", lambda E, v=vbx, kt=kt, p=ptb, j2=j2: E.matmul(
                        psC[0:65, :], v[:, kt * 65:kt * 65 + 65],
                        p[:, j2 * 512:j2 * 512 + 512],
                        start=(kt == 0), stop=(kt == NKT - 1),
                        skip_group_check=True),
                        inc=("pe_avg" if j2 == size - 1 else None))
                state["av_done"][j % 4] = em.val("pe_avg")

            def flush_tail():
                if pending_tail[0] is None:
                    return
                qc_ctx, tqc = pending_tail[0]
                pending_tail[0] = None
                av_group(qc_ctx, len(groups) - 2)
                av_group(qc_ctx, len(groups) - 1)
                em.wait("ve", "pe_avg", em.val("pe_avg"))
                em.wait("ve", "pe_ct", ct_at_qc[tqc])
                em.op("ve", lambda E, tqc=tqc: E.tensor_copy(
                    ctxT[:, tqc * 512:tqc * 512 + 512], psC[0:65, :]), inc="ve_ctx")
                state["ctx_evac"] = em.val("ve_ctx")
                evac_at_qc[tqc] = em.val("ve_ctx")

            for qc in range(NQC):
                exp_at_g = []
                qc_ctx = (exp_at_g, vb)
                for g, (k0, size) in enumerate(groups):
                    bank = psA if g % 2 == 0 else psB
                    bpar = g % 2
                    ptb = pt4[:, (g % 4) * 1536:(g % 4) * 1536 + 1536]
                    em.wait("pe", "act_e", state["exp_at_bank"][bpar])
                    for j in range(size):
                        kt = k0 + j
                        em.op("pe", lambda E, b=bank, j=j, kt=kt, kth=kth, qc=qc,
                              qtb=qtb: E.matmul(
                            b[:, j * 512:j * 512 + 512],
                            kth[:, kt * 128:kt * 128 + 128],
                            qtb[:, qc * 512:qc * 512 + 512],
                            start=True, stop=True),
                            inc=("pe_s" if j == size - 1 else None))
                    if g == 0:
                        flush_tail()
                    em.wait("act", "pe_avg", state["av_done"][g % 4])
                    em.wait("act", "pe_s", em.val("pe_s"))
                    em.op("act", lambda E, b=bank, p=ptb, w=size * 512: E.activation(
                        p[:, 0:w], b[:, 0:w], AF.Exp, bias=expb[:, 0:1], scale=1.0),
                        inc="act_e")
                    state["exp_at_bank"][bpar] = em.val("act_e")
                    exp_at_g.append(em.val("act_e"))
                    if g >= 2:
                        av_group(qc_ctx, g - 2)
                    if g == 1 and qc >= 1:
                        epilogue_block(qc - 1, h)
                pending_tail[0] = (qc_ctx, qc)
            flush_tail()
            epilogue_block(NQC - 1, h)
            em.wait("gp", "ve_cs", em.val("ve_cs"))
            em.op("gp", lambda E, h=h: E.dma_start(
                ctx_w[h], cstage[:, :].rearrange("p (t j) -> p t j", j=65)),
                inc="gp_cs", n=16)
            if h + 2 < NH:
                em.wait("sp", "pe_s", em.val("pe_s"))
                em.wait("gp", "pe_s", em.val("pe_s"))
                em.wait("gp", "pe_avg", em.val("pe_avg"))
                em.wait("sp", "pe_avg", em.val("pe_avg"))
                prefetch(h + 2, par)
            if h == 1:
                em.op("gp", lambda E: E.dma_start(
                    w1_sb[:, :].rearrange("p (dc h) -> p dc h", h=HF),
                    w1_d.rearrange("(dc p) h -> p dc h", p=128)), inc="gp_wffn", n=16)
                em.op("gp", lambda E: E.dma_start(
                    w2_sb[:, :].rearrange("p (ht d) -> p ht d", d=D),
                    w2_d.rearrange("(ht p) d -> p ht d", p=128)), inc="gp_wffn", n=16)
        if "ctx" in debug:
            dbc = dbufs["dbc"]
            dbg_w = dbg["ctx"].rearrange("(h t p) j -> h p t j", p=128, t=NTT)
            for h in range(NH):
                em.wait("gp", "gp_cs", em.val("gp_cs"))
                em.wait("gp", "sp_dbgc", em.val("sp_dbgc"))
                em.op("gp", lambda E, h=h: E.dma_start(
                    dbc[:, :].rearrange("p (t j) -> p t j", j=65), ctx_w[h]),
                    inc="sp_dbgc", n=16)
                em.wait("gp", "sp_dbgc")
                em.op("gp", lambda E, h=h: E.dma_start(
                    dbg_w[h], dbc[:, :].rearrange("p (t j) -> p t j", j=65)),
                    inc="sp_dbgc", n=16)

    FREE(["KT", "QTb", "vhb", "ctxT", "pt4", "cstage"])
    xres = AL("xres", [128, 2 * D], F32)
    cres = AL("cres", [128, 2 * NH * 65], BF16)
    cres2 = AL("cres2", [128, 2 * D], BF16)
    rcpt = AL("rcpt", [128, 2 * NH], F32)
    zbuf = AL("zbuf", [128, 4 * D], F32)
    zzbuf = AL("zzbuf", [128, 2 * D], F32)
    lns = AL("lns", [128, 40], F32)
    xnT = AL("xnT", [128, DC * NQ], BF16)
    hT = AL("hT", [128, NHT * 512], BF16)
    z2T = AL("z2T", [128, 2 * 512], F32)
    z2n = AL("z2n", [128, 4 * D], BF16)
    ostage = AL("ostage", [128, 2 * D], F32)

    ctx_r = ctx_s.rearrange("(h t p) j -> t p h j", p=128, t=NTT)

    def emit_ln_batch(zios, colbase):
        W = lambda: em.wait("ve", "ve_ln", em.val("ve_ln"))
        cs = [(lns[:, colbase + 4 * i:colbase + 4 * i + 1],
               lns[:, colbase + 4 * i + 1:colbase + 4 * i + 2],
               lns[:, colbase + 4 * i + 2:colbase + 4 * i + 3]) for i in range(len(zios))]
        W()
        for i, (zin, _) in enumerate(zios):
            em.op("ve", lambda E, z=zin, c=cs[i]: E.reduce_sum(c[0], z, axis=AX),
                  inc="ve_ln")
        for i, (zin, _) in enumerate(zios):
            if i == 2:
                W()
            em.op("ve", lambda E, z=zin, c=cs[i], i=i: E.scalar_tensor_tensor(
                zzbuf[:, (i % 2) * D:(i % 2) * D + D], z, 1.0, z, ALU.mult, ALU.mult,
                accum_out=c[1]), inc="ve_ln")
        W()
        for i in range(len(zios)):
            em.op("ve", lambda E, c=cs[i]: E.tensor_single_scalar(
                c[0], c[0], 1.0 / D, ALU.mult), inc="ve_ln")
            em.op("ve", lambda E, c=cs[i]: E.tensor_single_scalar(
                c[1], c[1], 1.0 / D, ALU.mult), inc="ve_ln")
        W()
        for i in range(len(zios)):
            em.op("ve", lambda E, c=cs[i]: E.scalar_tensor_tensor(
                c[2], c[0], -1.0, c[0], ALU.mult, ALU.mult), inc="ve_ln")
        W()
        for i in range(len(zios)):
            em.op("ve", lambda E, c=cs[i]: E.tensor_add(c[1], c[1], c[2]), inc="ve_ln")
        em.wait("act", "ve_ln", em.val("ve_ln"))
        for i in range(len(zios)):
            em.op("act", lambda E, c=cs[i]: E.activation(
                c[2], c[1], AF.Sqrt, bias=lneps[0:NHX, 0:1]), inc="act_ln")
        em.wait("ve", "act_ln", em.val("act_ln"))
        for i in range(len(zios)):
            em.op("ve", lambda E, c=cs[i]: E.reciprocal(c[1], c[2]), inc="ve_ln")
        W()
        for i, (zin, outbuf) in enumerate(zios):
            if outbuf is not None:
                em.op("ve", lambda E, z=zin, o=outbuf, c=cs[i]: E.tensor_scalar(
                    o, z, c[0], c[1], ALU.subtract, ALU.mult), inc="ve_ln")

    def emit_phase_f():
        em.wait("act", "ve_w", em.val("ve_w"))
        em.wait("act", "sp_w", em.val("sp_w"))
        em.wait("pe", "act_e", em.val("act_e"))
        for eng in ("sp", "gp", "ve", "act"):
            em.wait(eng, "pe_s", em.val("pe_s"))
            em.wait(eng, "pe_av", em.val("pe_av"))
            em.wait(eng, "pe_ct", em.val("pe_ct"))
            em.wait(eng, "ve_cs", em.val("ve_cs"))
            em.wait(eng, "ve_ct2", em.val("ve_ct2"))
            for nm in ("sp_kt0", "sp_kt1", "gp_kt0", "gp_kt1", "gp_vh0", "gp_vh1",
                       "gp_qt", "gp_cs"):
                em.wait(eng, nm, em.val(nm))
        em.wait("pe", "gp_wffn", em.val("gp_wffn"))
        tp_slots = [psA[:, 0:128], psA[:, 512:640]]

        def ln1_ve(t):
                xb = xres[:, (t % 2) * D:(t % 2) * D + D]
                cb = cres[:, (t % 2) * NH * 65:(t % 2) * NH * 65 + NH * 65]
                c2 = cres2[:, (t % 2) * D:(t % 2) * D + D]
                rc = rcpt[:, (t % 2) * NH:(t % 2) * NH + NH]
                if t >= 2:
                    em.wait("sp", "ve_z", em.val("ve_z") - 1)
                em.op("sp", lambda E, xb=xb, t=t: E.dma_start(
                    xb, x_d[t * 128:t * 128 + 128, :]), inc="sp_xr%d" % (t % 2), n=16)
                if t >= 2:
                    em.wait("gp", "ve_rcp", em.val("ve_rcp") - 1)
                em.op("gp", lambda E, cb=cb, t=t: E.dma_start(
                    cb.rearrange("p (h j) -> p h j", j=65), ctx_r[t]),
                    inc="gp_cr%d" % (t % 2), n=16)
                em.wait("ve", "sp_xr%d" % (t % 2), em.val("sp_xr%d" % (t % 2)))
                em.wait("ve", "gp_cr%d" % (t % 2), em.val("gp_cr%d" % (t % 2)))
                cbh = cb.rearrange("p (h j) -> p h j", j=65)
                if t >= 2:
                    em.wait("ve", "ve_z", em.val("ve_z") - 1)
                em.op("ve", lambda E, cbh=cbh, rc=rc: E.reciprocal(
                    rc.rearrange("p (h o) -> p h o", o=1), cbh[:, :, 64:65]),
                    inc="ve_rcp")
                em.wait("ve", "ve_rcp", em.val("ve_rcp"))
                rc_b = bass.AP(rcpt, (t % 2) * NH, [[2 * NH, 128], [1, NH], [0, 64]])
                c2h = c2.rearrange("p (h j) -> p h j", j=64)
                em.op("ve", lambda E, c2h=c2h, cbh=cbh, rb=rc_b: E.tensor_mul(
                    c2h, cbh[:, :, 0:64], rb), inc="ve_rcp")
                em.wait("ve", "ve_rcp", em.val("ve_rcp"))
                zo = zbuf[:, (t % 4) * D:(t % 4) * D + D]
                em.op("ve", lambda E, xb=xb, c2=c2, zo=zo: E.scalar_tensor_tensor(
                    zo, xb, 1.0, c2, ALU.mult, ALU.add), inc="ve_z")

        def ln1_pe(t):
                xo = zbuf[:, (t % 4) * D:(t % 4) * D + D]
                em.wait("pe", "ve_ln", lnval_at[t])
                for dc in range(DC):
                    k = t * DC + dc
                    ecnt = "evac_xnv" if k % 2 == 0 else "evac_xna"
                    if k >= 2:
                        em.wait("pe", ecnt, em.val(ecnt))
                    em.op("pe", lambda E, s=tp_slots[k % 2], dc=dc, xo=xo: E.transpose(
                        s, xo[:, dc * 128:dc * 128 + 128], ident[:, :]), inc="pe_xn")
                    eng = "ve" if dc % 2 == 0 else "act"
                    em.wait(eng, "pe_xn", em.val("pe_xn"))
                    dst = xnT[:, dc * NQ + t * 128: dc * NQ + t * 128 + 128]
                    if eng == "ve":
                        em.op("ve", lambda E, s=tp_slots[k % 2], d=dst: E.tensor_copy(d, s),
                              inc=ecnt)
                    else:
                        em.op("act", lambda E, s=tp_slots[k % 2], d=dst: E.copy(d, s),
                              inc=ecnt)

        lnval_at = {}

        def ln1_batch_ve(b):
            tiles = list(range(4 * b, min(4 * b + 4, NTT)))
            em.wait("ve", "pe_xn", em.val("pe_xn"))
            for t in tiles:
                ln1_ve(t)
            em.wait("ve", "ve_z", em.val("ve_z"))
            emit_ln_batch([(zbuf[:, (t % 4) * D:(t % 4) * D + D],
                            zbuf[:, (t % 4) * D:(t % 4) * D + D]) for t in tiles], 0)
            for t in tiles:
                lnval_at[t] = em.val("ve_ln")

        def ln1_batch_pe(b):
            for t in range(4 * b, min(4 * b + 4, NTT)):
                ln1_pe(t)

        NTC = NQ // 512
        mm_slots = [psB[:, 0:512], psB[:, 512:1024]]
        tslots = [psA[:, 1024:1152], psT[:, 0:128]]
        ln1_batch_ve(0)
        ln1_batch_pe(0)
        for tc in range(NTC):
            if tc + 1 < NTC:
                ln1_batch_ve(tc + 1)
            em.wait("pe", "evac_xnv", em.val("evac_xnv"))
            em.wait("pe", "evac_xna", em.val("evac_xna"))
            em.wait("pe", "act_y", em.val("act_y"))
            for ht in range(NHT):
                mi = tc * NHT + ht
                m = mm_slots[mi % 2]
                if mi >= 2:
                    em.wait("pe", "act_h", em.val("act_h") - 1)
                for dc in range(DC):
                    em.op("pe", lambda E, m=m, dc=dc, ht=ht, tc=tc: E.matmul(
                        m, w1_sb[:, dc * HF + ht * 128: dc * HF + ht * 128 + 128],
                        xnT[:, dc * NQ + tc * 512: dc * NQ + tc * 512 + 512],
                        start=(dc == 0), stop=(dc == DC - 1)),
                        inc=("pe_h" if dc == DC - 1 else None))
                em.wait("act", "pe_h", em.val("pe_h"))
                em.op("act", lambda E, m=m, ht=ht: E.activation(
                    hT[:, ht * 512:ht * 512 + 512], m, AF.Relu,
                    bias=b1_sb[:, ht:ht + 1], scale=1.0), inc="act_h")
            def z2_transposes(dt, zslot):
                em.wait("pe", "ve_y", em.val("ve_y"))
                for st4 in range(4):
                    k = dt * 4 + st4
                    sl = tslots[k % 2]
                    if True:
                        em.wait("pe", "ve_z2", max(0, em.val("ve_z2") - 1))
                    em.op("pe", lambda E, sl=sl, st4=st4, z=zslot: E.transpose(
                        sl, z[:, st4 * 128:st4 * 128 + 128], ident[:, :]), inc="pe_z2")
                    em.wait("ve", "pe_z2", em.val("pe_z2"))
                    em.op("ve", lambda E, sl=sl, st4=st4, dt=dt: E.tensor_copy(
                        z2n[:, st4 * D + dt * 128: st4 * D + dt * 128 + 128], sl),
                        inc="ve_z2")

            pending_z = None
            for dt in range(NJT):
                mi2 = tc * NJT + dt
                m = mm_slots[mi2 % 2]
                zslot = z2T[:, (dt % 2) * 512:(dt % 2) * 512 + 512]
                em.wait("pe", "act_h", em.val("act_h"))
                if mi2 >= 2:
                    em.wait("pe", "act_y", em.val("act_y") - 1)
                for ht in range(NHT):
                    em.op("pe", lambda E, m=m, ht=ht, dt=dt: E.matmul(
                        m, w2_sb[:, ht * D + dt * 128: ht * D + dt * 128 + 128],
                        hT[:, ht * 512:ht * 512 + 512],
                        start=(ht == 0), stop=(ht == NHT - 1)),
                        inc=("pe_y" if ht == NHT - 1 else None))
                if pending_z is not None:
                    z2_transposes(*pending_z)
                em.wait("act", "pe_y", em.val("pe_y"))
                em.wait("act", "pe_z2", max(0, em.val("pe_z2") - 4))
                em.op("act", lambda E, m=m, dt=dt, z=zslot: E.activation(
                    z, m, AF.Identity, bias=b2_sb[:, dt:dt + 1], scale=1.0),
                    inc="act_y")
                em.wait("ve", "act_y", em.val("act_y"))
                em.op("ve", lambda E, dt=dt, tc=tc, z=zslot: E.tensor_tensor(
                    z, z, xnT[:, dt * NQ + tc * 512: dt * NQ + tc * 512 + 512], ALU.add),
                    inc="ve_y")
                pending_z = (dt, zslot)
            z2_transposes(*pending_z)
            em.wait("ve", "ve_z2", em.val("ve_z2"))
            emit_ln_batch([(z2n[:, st4 * D:st4 * D + D], None) for st4 in range(4)], 16)
            for st4 in range(4):
                oi = tc * 4 + st4
                c0 = lns[:, 16 + 4 * st4:16 + 4 * st4 + 1]
                c1 = lns[:, 16 + 4 * st4 + 1:16 + 4 * st4 + 2]
                ob = ostage[:, (oi % 2) * D:(oi % 2) * D + D]
                if oi >= 2:
                    em.wait("ve", "sp_out%d" % (oi % 2), em.val("sp_out%d" % (oi % 2)))
                em.op("ve", lambda E, st4=st4, ob=ob, c0=c0, c1=c1: E.tensor_scalar(
                    ob, z2n[:, st4 * D:st4 * D + D], c0, c1,
                    ALU.subtract, ALU.mult), inc="ve_ln")
                em.wait("sp", "ve_ln", em.val("ve_ln"))
                em.op("sp", lambda E, ob=ob, tc=tc, st4=st4: E.dma_start(
                    out_d[tc * 512 + st4 * 128: tc * 512 + st4 * 128 + 128, :], ob),
                    inc="sp_out%d" % (oi % 2), n=16)
            if tc + 1 < NTC:
                ln1_batch_pe(tc + 1)
        if "xn" in debug:
            for dc in range(DC):
                em.wait("gp", "evac_xnv", em.val("evac_xnv"))
                em.wait("gp", "evac_xna", em.val("evac_xna"))
                em.op("gp", lambda E, dc=dc: E.dma_start(
                    dbg["xn"][dc * 128:(dc + 1) * 128, :],
                    xnT[:, dc * NQ:(dc + 1) * NQ]), inc="sp_dbgx", n=16)
        em.wait("sp", "sp_out0", em.val("sp_out0"))
        em.wait("sp", "sp_out1", em.val("sp_out1"))

    emit_phase_w()
    emit_phase_p()
    emit_phase_s()
    emit_phase_a()
    emit_phase_f()

    with nc.Block() as blk:
        em.run(blk)
    return nc


_NC_CACHE = {}


def _get_nc(key, cfg, debug=()):
    if key not in _NC_CACHE:
        _NC_CACHE[key] = build(cfg, debug=debug)
    return _NC_CACHE[key]


def kernel(L0_attn_hidden_states, L0_attn_qk_w, L0_attn_qk_b, L0_attn_v_w,
           L0_attn_v_b, L0_attn_hash_a, L0_ffn_w_gate, L0_ffn_w_noise,
           L0_ffn_W1, L0_ffn_b1, L0_ffn_W2, L0_ffn_b2):
    x = np.ascontiguousarray(np.asarray(L0_attn_hidden_states, dtype=np.float32))
    Bsz, L, D = x.shape
    cfg = CFG(L=L, NH=D // 64, HF=np.asarray(L0_ffn_W1).shape[2])
    nc = _get_nc(("full", L, D), cfg)

    com = {
        "qk_w": np.ascontiguousarray(L0_attn_qk_w, dtype=np.float32),
        "qk_b": np.ascontiguousarray(L0_attn_qk_b, dtype=np.float32),
        "v_w": np.ascontiguousarray(L0_attn_v_w, dtype=np.float32),
        "v_b": np.ascontiguousarray(L0_attn_v_b, dtype=np.float32),
        "hash_a": np.ascontiguousarray(L0_attn_hash_a, dtype=np.float32),
        "w1": np.ascontiguousarray(np.asarray(L0_ffn_W1)[1], dtype=np.float32),
        "b1": np.ascontiguousarray(np.asarray(L0_ffn_b1)[1], dtype=np.float32),
        "w2": np.ascontiguousarray(np.asarray(L0_ffn_W2)[1], dtype=np.float32),
        "b2": np.ascontiguousarray(np.asarray(L0_ffn_b2)[1], dtype=np.float32),
    }
    NQ = L // 2
    in_maps = []
    for c in range(8):
        b = c // 2
        if c % 2 == 0:
            xb = x[b]
        else:
            xb = np.concatenate([x[b, NQ:], x[b, :NQ]], axis=0)
        m = dict(com)
        m["x"] = np.ascontiguousarray(xb)
        in_maps.append(m)

    res = run_bass_kernel_spmd(nc, in_maps, core_ids=list(range(8)))
    out = np.empty((Bsz, L, D), dtype=np.float32)
    for c in range(8):
        b = c // 2
        r = res.results[c]["out"]
        if c % 2 == 0:
            out[b, :NQ] = r
        else:
            out[b, NQ:] = r
    return out


# version 07
# speedup vs baseline: 1.1229x; 1.0280x over previous
import numpy as np
import concourse.bass as bass
import concourse.mybir as mybir
from concourse.bass_utils import run_bass_kernel_spmd

F32 = mybir.dt.float32
BF16 = mybir.dt.bfloat16
AF = mybir.ActivationFunctionType
ALU = mybir.AluOpType
AX = mybir.AxisListType.X

CBIT = 4992.0
EXP_BIAS = -(CBIT + 5.0)
LN_EPS = 1e-5

ENGS = ("pe", "act", "ve", "sp", "gp")


class CFG:
    def __init__(self, L=8192, NH=16, HF=2048):
        self.L = L
        self.NH = NH
        self.D = NH * 64
        self.NQ = L // 2
        self.HF = HF
        self.DC = self.D // 128
        self.NJT = self.D // 128
        self.NKT = L // 128
        self.NQC = self.NQ // 512
        self.NTT = self.NQ // 128


class Emitter:
    def __init__(self, nc):
        self.nc = nc
        self.streams = {e: [] for e in ENGS}
        self.sems = {}
        self.cnt = {}

    def sem(self, name):
        if name not in self.sems:
            self.sems[name] = self.nc.alloc_semaphore(name)
            self.cnt[name] = 0
        return self.sems[name]

    def val(self, name):
        self.sem(name)
        return self.cnt[name]

    def op(self, eng, fn, inc=None, n=1):
        if inc is not None:
            h = self.sem(inc)
            self.cnt[inc] += n

            def wrapped(E, fn=fn, h=h, n=n):
                fn(E).then_inc(h, n)
            self.streams[eng].append(wrapped)
        else:
            self.streams[eng].append(fn)

    def wait(self, eng, name, value=None):
        if value is None:
            value = self.val(name)
        if value <= 0:
            return
        h = self.sem(name)
        self.streams[eng].append(lambda E, h=h, v=value: E.wait_ge(h, v))

    def run(self, block):
        for eng, meth in (("pe", block.tensor), ("act", block.scalar),
                          ("ve", block.vector), ("sp", block.sync),
                          ("gp", block.gpsimd)):
            fns = self.streams[eng]
            if fns:
                meth(lambda E, fns=fns: [f(E) for f in fns])
            self.streams[eng] = []


def build(cfg, debug=()):
    nc = bass.Bass("TRN2", target_bir_lowering=False)
    em = Emitter(nc)
    L, NH, D, NQ, HF = cfg.L, cfg.NH, cfg.D, cfg.NQ, cfg.HF
    DC, NJT, NKT, NQC, NTT = cfg.DC, cfg.NJT, cfg.NKT, cfg.NQC, cfg.NTT
    NHT = HF // 128
    NQUARTER = 4
    QL = L // NQUARTER
    QT_TILES = QL // 128
    QTC = QL // 512
    assert QTC >= 1 and NQC >= 1
    NHX = 128
    VHW = min(512, D)
    NVH = D // VHW
    BCW = min(512, D)

    x_d = nc.dram_tensor("x", [L, D], F32, kind="ExternalInput").ap()
    qkw_d = nc.dram_tensor("qk_w", [D, D], F32, kind="ExternalInput").ap()
    qkb_d = nc.dram_tensor("qk_b", [D], F32, kind="ExternalInput").ap()
    vw_d = nc.dram_tensor("v_w", [D, D], F32, kind="ExternalInput").ap()
    vb_d = nc.dram_tensor("v_b", [D], F32, kind="ExternalInput").ap()
    ha_d = nc.dram_tensor("hash_a", [66], F32, kind="ExternalInput").ap()
    w1_d = nc.dram_tensor("w1", [D, HF], F32, kind="ExternalInput").ap()
    b1_d = nc.dram_tensor("b1", [HF], F32, kind="ExternalInput").ap()
    w2_d = nc.dram_tensor("w2", [HF, D], F32, kind="ExternalInput").ap()
    b2_d = nc.dram_tensor("b2", [D], F32, kind="ExternalInput").ap()
    out_d = nc.dram_tensor("out", [NQ, D], F32, kind="ExternalOutput").ap()

    qkT_s = nc.dram_tensor("qkT_s", [D, L], BF16, kind="Internal").ap()
    xbf_s = nc.dram_tensor("xbf_s", [L, D], BF16, kind="Internal").ap()
    qkw_bf = nc.dram_tensor("qkw_bf", [D, D], BF16, kind="Internal").ap()
    vw_bf = nc.dram_tensor("vw_bf", [D, D], BF16, kind="Internal").ap()
    v_s = nc.dram_tensor("v_s", [L, D], BF16, kind="Internal").ap()
    ctx_s = nc.dram_tensor("ctx_s", [NH * NQ, 65], BF16, kind="Internal").ap()
    bits_s = nc.dram_tensor("bits_s", [NH, L + NQ], BF16, kind="Internal").ap()

    dbg = {}
    if "qkT" in debug:
        dbg["qkT"] = nc.dram_tensor("dbg_qkT", [D, L], F32, kind="ExternalOutput").ap()
    if "stats" in debug:
        dbg["dot1"] = nc.dram_tensor("dbg_dot1", [NH, L], F32, kind="ExternalOutput").ap()
        dbg["n2"] = nc.dram_tensor("dbg_n2", [NH, L], F32, kind="ExternalOutput").ap()
        dbg["bits"] = nc.dram_tensor("dbg_bits", [NH, L + NQ], F32, kind="ExternalOutput").ap()
    if "v" in debug:
        dbg["v"] = nc.dram_tensor("dbg_v", [L, D], F32, kind="ExternalOutput").ap()
    if "ctx" in debug:
        dbg["ctx"] = nc.dram_tensor("dbg_ctx", [NH * NQ, 65], F32, kind="ExternalOutput").ap()
    if "xn" in debug:
        dbg["xn"] = nc.dram_tensor("dbg_xn", [D, NQ], F32, kind="ExternalOutput").ap()

    guards = []

    def AL(name, shape, dtype, side="left"):
        g = nc.sbuf_tensor(name, shape, dtype, side=side)
        t = g.__enter__()
        guards.append((name, g))
        return t

    def FREE(names):
        for name in reversed(names):
            for i in range(len(guards) - 1, -1, -1):
                if guards[i][0] == name:
                    guards[i][1].__exit__(None, None, None)
                    guards.pop(i)
                    break

    ident = AL("ident", [128, 128], F32)
    expb = AL("expb", [128, 1], F32)
    lneps = AL("lneps", [128, 1], F32)
    st = AL("st", [NH, 16], F32)
    a23 = AL("a23", [NH, 2], F32)
    qkb_sb = AL("qkb_sb", [128, NJT], F32)
    b1_sb = AL("b1_sb", [128, NHT], F32)
    b2_sb = AL("b2_sb", [128, NJT], F32)

    dbufs = {}
    if "qkT" in debug:
        dbufs["dbq"] = AL("dbq", [128, L], F32)
    if "v" in debug:
        dbufs["dbv"] = AL("dbv", [128, D], F32)
    if "ctx" in debug:
        dbufs["dbc"] = AL("dbc", [128, NTT * 65], F32)

    dot1 = AL("dot1", [NH, L], F32)
    n2 = AL("n2", [NH, L], BF16)
    bits = AL("bits", [NH, L + NQ], BF16)

    qk_wT = AL("qk_wT", [128, DC * D], BF16, side="right")
    v_wT = AL("v_wT", [128, DC * D], BF16, side="right")
    vbb_sb = AL("vbb_sb", [128, D], BF16, side="right")
    a1sel = AL("a1sel", [128, NJT * NH], F32, side="right")
    n1sel = AL("n1sel", [128, NJT * NH], BF16, side="right")
    onesb = AL("onesb", [1, 128], BF16, side="right")
    vbrow = AL("vbrow", [1, D], BF16, side="right")
    onesf = AL("onesf", [1, NH], F32, side="right")

    psA = nc.alloc_psum_tensor("psA", [128, 1536], F32)
    psB = nc.alloc_psum_tensor("psB", [128, 1536], F32)
    psC = nc.alloc_psum_tensor("psC", [128, 512], F32)
    psT = nc.alloc_psum_tensor("psT", [128, 512], F32)


    def emit_phase_w():
        em.op("gp", lambda E: E.memset(ident[:, :], 0.0), inc="gp_i0")
        em.wait("gp", "gp_i0")
        em.op("gp", lambda E: E.affine_select(
            out=ident[:, :], in_=ident[:, :], compare_op=ALU.not_equal,
            fill=1.0, base=0, pattern=[[-1, 128]], channel_multiplier=1), inc="gp_init")
        em.op("sp", lambda E: E.dma_start(
            qkb_sb[:, 0:NJT], qkb_d.rearrange("(j p) -> p j", p=128),
            allow_slow_non_contiguous=True), inc="sp_w", n=16)
        em.op("sp", lambda E: E.dma_start(
            b1_sb[:, 0:NHT], b1_d.rearrange("(t p) -> p t", p=128),
            allow_slow_non_contiguous=True), inc="sp_w", n=16)
        em.op("sp", lambda E: E.dma_start(
            b2_sb[:, 0:NJT], b2_d.rearrange("(t p) -> p t", p=128),
            allow_slow_non_contiguous=True), inc="sp_w", n=16)
        em.op("ve", lambda E: E.memset(expb[:, :], EXP_BIAS), inc="ve_w")
        em.op("ve", lambda E: E.memset(lneps[:, :], LN_EPS), inc="ve_w")
        em.op("ve", lambda E: E.memset(a1sel[:, :], 0.0), inc="ve_w")
        em.op("ve", lambda E: E.memset(n1sel[:, :], 0.0), inc="ve_w")
        em.wait("ve", "ve_w", em.val("ve_w"))
        for jt in range(NJT):
            c0 = jt * NH + 2 * jt
            em.wait("sp", "ve_w", em.val("ve_w"))
            em.op("sp", lambda E, c0=c0: E.dma_start(
                a1sel[0:64, c0:c0 + 1],
                ha_d[0:64].rearrange("(p o) -> p o", o=1),
                allow_slow_non_contiguous=True), inc="sp_w", n=16)
            em.op("sp", lambda E, c0=c0: E.dma_start(
                a1sel[64:128, c0 + 1:c0 + 2],
                ha_d[0:64].rearrange("(p o) -> p o", o=1),
                allow_slow_non_contiguous=True), inc="sp_w", n=16)
            em.op("ve", lambda E, c0=c0: E.memset(n1sel[0:64, c0:c0 + 1], 1.0))
            em.op("ve", lambda E, c0=c0: E.memset(n1sel[64:128, c0 + 1:c0 + 2], 1.0),
                  inc="ve_w")
        em.op("ve", lambda E: E.memset(onesb[:, :], 1.0), inc="ve_w")
        em.op("sp", lambda E: E.dma_start(
            st[0:1, 0:2], ha_d[64:66].rearrange("(o c) -> o c", o=1)), inc="sp_w", n=16)
        em.op("gp", lambda E: E.dma_start(
            vbrow[0:1, :], vb_d.rearrange("(o c) -> o c", o=1)), inc="gp_w", n=16)
        em.wait("pe", "gp_init")
        em.wait("pe", "ve_w")
        em.wait("pe", "sp_w")
        em.wait("pe", "gp_w")
        em.op("ve", lambda E: E.memset(onesf[:, :], 1.0), inc="ve_w")
        em.wait("pe", "ve_w")
        em.op("pe", lambda E: E.matmul(psC[0:NH, 0:2], onesf[0:1, 0:NH],
                                       st[0:1, 0:2], start=True, stop=True), inc="pe_w")
        em.wait("ve", "pe_w", em.val("pe_w"))
        em.op("ve", lambda E: E.tensor_copy(a23[:, :], psC[0:NH, 0:2]), inc="ve_a23")
        for half in range(D // BCW):
            em.wait("pe", "ve_vbb", em.val("ve_vbb") if half > 0 else 0)
            em.wait("pe", "ve_a23", em.val("ve_a23"))
            em.op("pe", lambda E, h=half: E.matmul(
                psC[:, 0:BCW], onesb[0:1, 0:128],
                vbrow[0:1, h * BCW:(h + 1) * BCW], start=True, stop=True), inc="pe_w")
            em.wait("ve", "pe_w", em.val("pe_w"))
            em.op("ve", lambda E, h=half: E.tensor_copy(
                vbb_sb[:, h * BCW:(h + 1) * BCW], psC[:, 0:BCW]), inc="ve_vbb")
        em.op("gp", lambda E: E.dma_start(qkw_bf, qkw_d), inc="gp_wbf", n=16)
        em.op("gp", lambda E: E.dma_start(vw_bf, vw_d), inc="gp_wbf", n=16)
        em.op("gp", lambda E: E.dma_start(xbf_s, x_d), inc="gp_xbf", n=16)
        em.wait("sp", "gp_wbf", em.val("gp_wbf"))
        for wi, (src, dstT) in enumerate(((qkw_bf, qk_wT), (vw_bf, v_wT))):
            for dc in range(DC):
                em.op("sp", lambda E, s=src, d=dstT, dc=dc: E.dma_start(
                    d[:, dc * D:(dc + 1) * D], s[:, dc * 128:(dc + 1) * 128],
                    transpose=True), inc="sp_wt", n=16)

    xT = AL("xT", [128, 2 * DC * QL], BF16, side="right")
    qkf = AL("qkf", [128, 2 * 512], F32, side="right")
    sqb = AL("sqb", [128, 2 * 512], BF16, side="right")
    qstage = AL("qstage", [128, 2 * NJT * 512], BF16, side="right")
    vstage = AL("vstage", [128, 2 * D], BF16, side="right")

    def emit_phase_p():
        em.wait("act", "sp_w", em.val("sp_w"))
        em.wait("act", "ve_w", em.val("ve_w"))
        mm_slots = [psA[:, 1024:1536], psB[:, 0:512]]
        d1_slots = [psB[:, 512:1024], psB[:, 1024:1536]]
        n2_slots = [psC[:, 0:512], psT[:, 0:512]]
        em.wait("pe", "ve_vbb", em.val("ve_vbb"))
        em.wait("sp", "gp_xbf", em.val("gp_xbf"))
        def issue_xt(q):
            par = q % 2
            for dc in range(DC):
                em.op("sp", lambda E, q=q, dc=dc, par=par: E.dma_start(
                    xT[:, par * DC * QL + dc * QL: par * DC * QL + (dc + 1) * QL],
                    xbf_s[q * QL:(q + 1) * QL, dc * 128:(dc + 1) * 128],
                    transpose=True), inc="sp_xt%d" % par, n=16)

        issue_xt(0)
        for q in range(NQUARTER):
            par = q % 2
            xTq = xT[:, par * DC * QL:(par + 1) * DC * QL]
            if q + 1 < NQUARTER:
                em.wait("sp", "pe_qk", em.val("pe_qk"))
                em.wait("sp", "pe_v", em.val("pe_v"))
                issue_xt(q + 1)
            em.wait("pe", "sp_xt%d" % par, em.val("sp_xt%d" % par))
            em.wait("pe", "sp_wt", em.val("sp_wt"))
            em.wait("pe", "ve_v", em.val("ve_v"))
            for tc in range(QTC):
                gi = q * QTC + tc
                d1b = d1_slots[gi % 2]
                n2b = n2_slots[gi % 2]
                def stats_mms(jt, d1b, n2b, qf, sq):
                    em.wait("pe", "act_qk", em.val("act_qk"))
                    em.op("pe", lambda E, d1b=d1b, jt=jt, qf=qf: E.matmul(
                        d1b[0:NH, :], a1sel[:, jt * NH:jt * NH + NH], qf,
                        start=(jt == 0), stop=(jt == NJT - 1)), inc="pe_d1")
                    em.wait("pe", "ve_sq", em.val("ve_sq"))
                    em.op("pe", lambda E, n2b=n2b, jt=jt, sq=sq: E.matmul(
                        n2b[0:NH, :], n1sel[:, jt * NH:jt * NH + NH], sq,
                        start=(jt == 0), stop=(jt == NJT - 1)), inc="pe_n2")

                pending = None
                for jt in range(NJT):
                    mi = gi * NJT + jt
                    mmb = mm_slots[mi % 2]
                    if mi >= 2:
                        em.wait("pe", "act_qk", em.val("act_qk") - 1)
                    if jt == 0 and gi >= 2:
                        em.wait("pe", "ve_d1", em.val("ve_d1") - 1)
                        em.wait("pe", "ve_n2", em.val("ve_n2") - 1)
                    for dc in range(DC):
                        em.op("pe", lambda E, m=mmb, dc=dc, jt=jt, tc=tc, xTq=xTq: E.matmul(
                            m, qk_wT[:, (dc * NJT + jt) * 128:(dc * NJT + jt) * 128 + 128],
                            xTq[:, dc * QL + tc * 512: dc * QL + tc * 512 + 512],
                            start=(dc == 0), stop=(dc == DC - 1)),
                            inc=("pe_qk" if dc == DC - 1 else None))
                    if pending is not None:
                        stats_mms(*pending)
                    qf = qkf[:, (mi % 2) * 512:(mi % 2) * 512 + 512]
                    sq = sqb[:, (mi % 2) * 512:(mi % 2) * 512 + 512]
                    em.wait("act", "pe_qk", em.val("pe_qk"))
                    if mi >= 2:
                        em.wait("act", "pe_d1", em.val("pe_d1"))
                    em.op("act", lambda E, m=mmb, qf=qf, jt=jt: E.activation(
                        qf, m, AF.Identity, bias=qkb_sb[:, jt:jt + 1], scale=1.0),
                        inc="act_qk")
                    em.wait("ve", "act_qk", em.val("act_qk"))
                    if mi >= 2:
                        em.wait("ve", "pe_n2", em.val("pe_n2"))
                    em.op("ve", lambda E, sq=sq, qf=qf: E.tensor_mul(sq, qf, qf),
                          inc="ve_sq")
                    stg = qstage[:, (gi % 2) * NJT * 512 + jt * 512:
                                 (gi % 2) * NJT * 512 + jt * 512 + 512]
                    if gi >= 2 and jt == 0:
                        em.wait("ve", "gp_qs%d" % (gi % 2), em.val("gp_qs%d" % (gi % 2)))
                    em.op("ve", lambda E, stg=stg, qf=qf: E.tensor_copy(stg, qf),
                          inc="ve_qs")
                    pending = (jt, d1b, n2b, qf, sq)
                stats_mms(*pending)
                em.wait("ve", "pe_d1", em.val("pe_d1"))
                em.wait("ve", "pe_n2", em.val("pe_n2"))
                cs = q * QL + tc * 512
                em.op("ve", lambda E, d1b=d1b, cs=cs: E.tensor_copy(
                    dot1[:, cs:cs + 512], d1b[0:NH, :]), inc="ve_d1")
                em.op("ve", lambda E, n2b=n2b, cs=cs: E.tensor_copy(
                    n2[:, cs:cs + 512], n2b[0:NH, :]), inc="ve_n2")
                em.wait("gp", "ve_qs", em.val("ve_qs"))
                src = qstage[:, (gi % 2) * NJT * 512:(gi % 2) * NJT * 512 + NJT * 512]
                dst = qkT_s.rearrange("(jt p) Lx -> p jt Lx", p=128)[:, :, cs:cs + 512]
                em.op("gp", lambda E, s=src, d=dst: E.dma_start(
                    d, s.rearrange("p (jt c) -> p jt c", c=512)),
                    inc="gp_qs%d" % (gi % 2), n=16)
            em.wait("pe", "act_qk", em.val("act_qk"))
            for t in range(QT_TILES):
                gt = q * QT_TILES + t
                vs = vstage[:, (gt % 2) * D:(gt % 2) * D + D]
                if gt >= 2:
                    em.wait("ve", "gp_vs%d" % (gt % 2), em.val("gp_vs%d" % (gt % 2)))
                for half in range(NVH):
                    vi = gt * NVH + half
                    mmb = mm_slots[vi % 2]
                    em.wait("pe", "ve_v", max(0, em.val("ve_v") - 1))
                    for dc in range(DC):
                        em.op("pe", lambda E, m=mmb, dc=dc, t=t, h=half, xTq=xTq: E.matmul(
                            m[:, 0:VHW],
                            xTq[:, dc * QL + t * 128: dc * QL + t * 128 + 128],
                            v_wT[:, dc * D + h * VHW: dc * D + h * VHW + VHW],
                            start=(dc == 0), stop=(dc == DC - 1)),
                            inc=("pe_v" if dc == DC - 1 else None))
                    em.wait("ve", "pe_v", em.val("pe_v"))
                    em.op("ve", lambda E, m=mmb, vs=vs, h=half: E.tensor_tensor(
                        vs[:, h * VHW:h * VHW + VHW], m[:, 0:VHW],
                        vbb_sb[:, h * VHW:h * VHW + VHW], ALU.add), inc="ve_v")
                em.wait("gp", "ve_v", em.val("ve_v"))
                em.op("gp", lambda E, vs=vs, q=q, t=t: E.dma_start(
                    v_s[q * QL + t * 128: q * QL + t * 128 + 128, :], vs),
                    inc="gp_vs%d" % (gt % 2), n=16)
        if "qkT" in debug:
            db = dbufs["dbq"]
            for jt in range(NJT):
                em.wait("gp", "gp_qs0", em.val("gp_qs0"))
                em.wait("gp", "gp_qs1", em.val("gp_qs1"))
                em.wait("gp", "sp_dbg", em.val("sp_dbg"))
                em.op("gp", lambda E, jt=jt: E.dma_start(
                    db[:, :], qkT_s[jt * 128:(jt + 1) * 128, :]), inc="sp_dbg", n=16)
                em.wait("gp", "sp_dbg")
                em.op("gp", lambda E, jt=jt: E.dma_start(
                    dbg["qkT"][jt * 128:(jt + 1) * 128, :], db[:, :]), inc="sp_dbg", n=16)
        if "v" in debug:
            dbv = dbufs["dbv"]
            for t in range(L // 128):
                em.wait("gp", "gp_vs0", em.val("gp_vs0"))
                em.wait("gp", "gp_vs1", em.val("gp_vs1"))
                em.wait("gp", "sp_dbgv", em.val("sp_dbgv"))
                em.op("gp", lambda E, t=t: E.dma_start(
                    dbv[:, :], v_s[t * 128:(t + 1) * 128, :]), inc="sp_dbgv", n=16)
                em.wait("gp", "sp_dbgv")
                em.op("gp", lambda E, t=t: E.dma_start(
                    dbg["v"][t * 128:(t + 1) * 128, :], dbv[:, :]), inc="sp_dbgv", n=16)

    FREE(["xT", "qkf", "sqb", "qstage", "vstage"])
    FREE(["onesf"])
    FREE(["qk_wT", "v_wT", "vbb_sb", "a1sel", "n1sel", "onesb", "vbrow"])
    hkv = AL("hkv", [NH, L], BF16)

    def emit_phase_s():
        V = "ve"
        for nm in ("pe_qk", "pe_v", "pe_d1", "pe_n2", "gp_qs0", "gp_qs1",
                   "gp_vs0", "gp_vs1", "act_qk", "ve_qs", "ve_v", "ve_sq"):
            em.wait(V, nm, em.val(nm))
        em.wait(V, "ve_a23", em.val("ve_a23"))
        c = lambda i: st[:, i:i + 1]
        em.wait(V, "ve_s", em.val("ve_s"))
        em.op(V, lambda E: E.reduce_max(c(0), n2[:, :], axis=AX), inc="ve_s")
        em.wait("act", "ve_s", em.val("ve_s"))
        em.op("act", lambda E: E.activation(c(1), c(0), AF.Sqrt), inc="act_s")
        em.wait(V, "act_s", em.val("act_s"))
        em.wait(V, "ve_s", em.val("ve_s"))
        em.op(V, lambda E: E.tensor_single_scalar(c(2), c(1), 1e-12, ALU.max), inc="ve_s")
        em.wait(V, "ve_s", em.val("ve_s"))
        em.op(V, lambda E: E.reciprocal(c(3), c(2)), inc="ve_s")
        em.wait(V, "ve_s", em.val("ve_s"))
        em.op(V, lambda E: E.tensor_single_scalar(c(2), c(3), 0.75, ALU.mult), inc="ve_s")
        em.wait(V, "ve_s", em.val("ve_s"))
        em.op(V, lambda E: E.tensor_mul(c(4), c(2), c(2)), inc="ve_s")
        em.wait(V, "ve_s", em.val("ve_s"))
        em.op(V, lambda E: E.tensor_copy(bits[:, 0:L], dot1[:, :]), inc="ve_s")
        em.wait(V, "ve_s", em.val("ve_s"))
        em.op(V, lambda E: E.tensor_scalar_mul(hkv[:, :], n2[:, :], c(4)), inc="ve_s")
        em.wait(V, "ve_s", em.val("ve_s"))
        em.op(V, lambda E: E.tensor_scalar(n2[:, :], hkv[:, :], -1.0, 0.5,
                                           ALU.mult, ALU.add), inc="ve_s")
        em.wait(V, "ve_s", em.val("ve_s"))
        em.op(V, lambda E: E.tensor_mul(hkv[:, :], hkv[:, :], hkv[:, :]), inc="ve_s")
        em.wait(V, "ve_s", em.val("ve_s"))
        em.op(V, lambda E: E.tensor_scalar(hkv[:, :], hkv[:, :], -1.0, 0.5,
                                           ALU.mult, ALU.add), inc="ve_s")
        em.wait(V, "ve_s", em.val("ve_s"))
        em.op(V, lambda E: E.tensor_scalar_mul(hkv[:, :], hkv[:, :], a23[:, 1:2]), inc="ve_s")
        em.wait(V, "ve_s", em.val("ve_s"))
        em.op(V, lambda E: E.scalar_tensor_tensor(
            hkv[:, :], n2[:, :], a23[:, 0:1], hkv[:, :], ALU.mult, ALU.add), inc="ve_s")
        em.wait(V, "ve_s", em.val("ve_s"))
        em.op(V, lambda E: E.scalar_tensor_tensor(
            hkv[:, :], bits[:, 0:L], c(2), hkv[:, :], ALU.mult, ALU.add), inc="ve_s")
        em.wait(V, "ve_s", em.val("ve_s"))
        em.op(V, lambda E: E.tensor_single_scalar(hkv[:, :], hkv[:, :], 0.0, ALU.is_ge),
              inc="ve_s")
        em.wait(V, "ve_s", em.val("ve_s"))
        em.op(V, lambda E: E.tensor_scalar(bits[:, 0:L], hkv[:, :], 128.0, -64.0,
                                           ALU.mult, ALU.add), inc="ve_s")
        em.wait(V, "ve_s", em.val("ve_s"))
        em.op(V, lambda E: E.reduce_max(c(5), hkv[:, :], axis=AX), inc="ve_s")
        em.wait(V, "ve_s", em.val("ve_s"))
        em.op(V, lambda E: E.tensor_reduce(c(6), hkv[:, :], axis=AX, op=ALU.min), inc="ve_s")
        em.wait(V, "ve_s", em.val("ve_s"))
        em.op(V, lambda E: E.tensor_sub(c(7), c(5), c(6)), inc="ve_s")
        em.wait(V, "ve_s", em.val("ve_s"))
        em.op(V, lambda E: E.tensor_add(c(8), c(5), c(6)), inc="ve_s")
        em.wait(V, "ve_s", em.val("ve_s"))
        em.op(V, lambda E: E.tensor_single_scalar(c(8), c(8), 1.0, ALU.subtract),
              inc="ve_s")
        em.wait(V, "ve_s", em.val("ve_s"))
        em.op(V, lambda E: E.tensor_single_scalar(c(9), c(7), 156.0, ALU.mult), inc="ve_s")
        em.wait(V, "ve_s", em.val("ve_s"))
        em.op(V, lambda E: E.tensor_sub(c(10), c(8), c(7)), inc="ve_s")
        em.wait(V, "ve_s", em.val("ve_s"))
        em.op(V, lambda E: E.tensor_single_scalar(c(10), c(10), 78.0, ALU.mult), inc="ve_s")
        em.wait(V, "ve_s", em.val("ve_s"))
        em.op(V, lambda E: E.tensor_single_scalar(dot1[:, 0:NQ], dot1[:, 0:NQ], 0.0,
                                                  ALU.is_ge), inc="ve_s")
        em.wait(V, "ve_s", em.val("ve_s"))
        em.op(V, lambda E: E.tensor_scalar(bits[:, L:L + NQ], dot1[:, 0:NQ], c(9), c(10),
                                           ALU.mult, ALU.add), inc="ve_s")
        em.wait("gp", "ve_s", em.val("ve_s"))
        em.op("gp", lambda E: E.dma_start(bits_s, bits[:, :]), inc="gp_bits", n=16)
        if "stats" in debug:
            em.wait("gp", "ve_s", em.val("ve_s"))
            em.op("gp", lambda E: E.dma_start(dbg["dot1"], dot1[:, :]), inc="sp_dbgs", n=16)
            em.op("gp", lambda E: E.dma_start(dbg["n2"], n2[:, :]), inc="sp_dbgs", n=16)
            em.op("gp", lambda E: E.dma_start(dbg["bits"], bits[:, :]), inc="sp_dbgs", n=16)

    FREE(["hkv"])
    FREE(["dot1", "n2", "bits"])
    w1_sb = AL("w1_sb", [128, DC * HF], BF16)
    w2_sb = AL("w2_sb", [128, NHT * D], BF16)
    KT = AL("KT", [65, 2 * L], BF16)
    QTb = AL("QTb", [65, 2 * NQ], BF16)
    vhb = AL("vhb", [128, 2 * NKT * 65], BF16)
    ctxT = AL("ctxT", [65, NQ], F32)
    pt4 = AL("pt4", [128, 4 * 1536], BF16)
    cstage = AL("cstage", [128, NTT * 65], BF16)

    groups = []
    k = 0
    while k < NKT:
        size = min(3, NKT - k)
        groups.append((k, size))
        k += size

    ctx_w = ctx_s.rearrange("(h t p) j -> h p t j", p=128, t=NTT)

    def emit_phase_a():
        em.wait("pe", "ve_d1", em.val("ve_d1"))
        em.wait("pe", "ve_n2", em.val("ve_n2"))
        em.wait("pe", "ve_v", em.val("ve_v"))
        em.wait("pe", "act_qk", em.val("act_qk"))
        em.wait("pe", "evac_xp", em.val("evac_xp"))

        em.wait("act", "ve_w", em.val("ve_w"))
        state = {"exp_at_bank": {0: 0, 1: 0}, "ctx_evac": 0,
                 "av_done": {0: 0, 1: 0, 2: 0, 3: 0}}

        def prefetch(h, into):
            kth = KT[:, into * L: into * L + L]
            em.op("sp", lambda E, h=h, kth=kth: E.dma_start(
                kth[0:64, :], qkT_s[h * 64:(h + 1) * 64, :]),
                inc="sp_kt%d" % into, n=16)
            em.op("gp", lambda E, h=h, kth=kth: E.dma_start(
                kth[64:65, :], bits_s[h:h + 1, 0:L]), inc="gp_kt%d" % into, n=16)
            vb = vhb[:, into * NKT * 65: into * NKT * 65 + NKT * 65]
            dstv = vb.rearrange("p (k j) -> p k j", j=65)[:, :, 0:64]
            srcv = v_s[:, h * 64:(h + 1) * 64].rearrange("(k p) j -> p k j", p=128)
            KH = NKT // 2
            em.op("sp", lambda E, d=dstv, s=srcv, KH=KH: E.dma_start(
                d[:, 0:KH, :], s[:, 0:KH, :]), inc="sp_vh%d" % into, n=16)
            em.op("gp", lambda E, d=dstv, s=srcv, KH=KH: E.dma_start(
                d[:, KH:NKT, :], s[:, KH:NKT, :]), inc="gp_vh%d" % into, n=16)

        em.wait("ve", "sp_dbgs", em.val("sp_dbgs"))
        em.wait("ve", "gp_bits", em.val("gp_bits"))
        em.op("ve", lambda E: E.memset(vhb[:, :], 1.0), inc="ve_ones")
        em.wait("gp", "ve_ones", em.val("ve_ones"))
        em.wait("sp", "ve_ones", em.val("ve_ones"))
        em.wait("gp", "gp_bits", em.val("gp_bits"))
        em.wait("sp", "ve_s", em.val("ve_s"))
        em.wait("sp", "gp_bits", em.val("gp_bits"))
        em.wait("sp", "sp_dbgs", em.val("sp_dbgs"))
        em.wait("gp", "sp_dbgs", em.val("sp_dbgs"))
        em.wait("sp", "gp_qs0", em.val("gp_qs0"))
        em.wait("sp", "gp_qs1", em.val("gp_qs1"))
        em.wait("gp", "gp_vs0", em.val("gp_vs0"))
        em.wait("gp", "gp_vs1", em.val("gp_vs1"))
        prefetch(0, 0)
        prefetch(1, 1)

        ct_at_qc = [0] * NQC

        def build_qt(h):
            par = h % 2
            kth = KT[:, par * L: par * L + L]
            qtb = QTb[:, par * NQ: par * NQ + NQ]
            em.wait("ve", "sp_kt%d" % par, em.val("sp_kt%d" % par))
            em.wait("ve", "gp_kt%d" % par, em.val("gp_kt%d" % par))
            em.wait("ve", "pe_s", em.val("pe_s"))
            em.op("ve", lambda E, kth=kth, q=qtb: E.tensor_scalar_mul(
                q[0:64, :], kth[0:64, 0:NQ], 0.125), inc="ve_qt")
            em.wait("gp", "pe_s", em.val("pe_s"))
            em.op("gp", lambda E, h=h, q=qtb: E.dma_start(
                q[64:65, :], bits_s[h:h + 1, L:L + NQ]), inc="gp_qt%d" % par, n=16)

        build_qt(0)
        for h in range(NH):
            par = h % 2
            kth = KT[:, par * L: par * L + L]
            vb = vhb[:, par * NKT * 65: par * NKT * 65 + NKT * 65]
            qtb = QTb[:, par * NQ: par * NQ + NQ]
            if h + 1 < NH:
                build_qt(h + 1)
            em.wait("pe", "ve_qt", em.val("ve_qt") - (1 if h + 1 < NH else 0))
            em.wait("pe", "gp_qt%d" % par, em.val("gp_qt%d" % par))
            em.wait("pe", "gp_vh%d" % par, em.val("gp_vh%d" % par))
            em.wait("pe", "sp_vh%d" % par, em.val("sp_vh%d" % par))
            em.wait("pe", "sp_kt%d" % par, em.val("sp_kt%d" % par))
            em.wait("pe", "gp_kt%d" % par, em.val("gp_kt%d" % par))

            evac_at_qc = [0] * NQC

            def epilogue_block(b, h):
                em.wait("pe", "ve_ctx", evac_at_qc[b])
                em.wait("pe", "ve_cs", em.val("ve_cs"))
                for j in range(4):
                    tt = b * 4 + j
                    em.op("pe", lambda E, j=j, tt=tt: E.transpose(
                        psT[:, j * 80:j * 80 + 65], ctxT[:, tt * 128:tt * 128 + 128],
                        ident[0:65, 0:65]), inc=("pe_ct" if j == 3 else None))
                ct_at_qc[b] = em.val("pe_ct")
                em.wait("ve", "pe_ct", em.val("pe_ct"))
                if b == 0:
                    em.wait("ve", "gp_cs", em.val("gp_cs"))
                em.op("ve", lambda E, b=b: E.tensor_copy(
                    cstage[:, b * 4 * 65:(b + 1) * 4 * 65].rearrange(
                        "p (t j) -> p t j", j=65),
                    psT[:, 0:320].rearrange("p (t j) -> p t j", j=80)[:, :, 0:65]),
                    inc="ve_cs")

            pending_tail = [None]

            def av_group(qc_ctx, j):
                exp_at_g, vbx = qc_ctx
                k0, size = groups[j]
                ptb = pt4[:, (j % 4) * 1536:(j % 4) * 1536 + 1536]
                em.wait("pe", "act_e", exp_at_g[j])
                if k0 == 0:
                    em.wait("pe", "ve_ctx", state["ctx_evac"])
                for j2 in range(size):
                    kt = k0 + j2
                    em.op("pe", lambda E, v=vbx, kt=kt, p=ptb, j2=j2: E.matmul(
                        psC[0:65, :], v[:, kt * 65:kt * 65 + 65],
                        p[:, j2 * 512:j2 * 512 + 512],
                        start=(kt == 0), stop=(kt == NKT - 1),
                        skip_group_check=True),
                        inc=("pe_avg" if j2 == size - 1 else None))
                state["av_done"][j % 4] = em.val("pe_avg")

            def flush_tail(stage):
                if pending_tail[0] is None:
                    return
                qc_ctx, tqc = pending_tail[0]
                if stage == 0:
                    av_group(qc_ctx, len(groups) - 2)
                    return
                pending_tail[0] = None
                av_group(qc_ctx, len(groups) - 1)
                em.wait("ve", "pe_avg", em.val("pe_avg"))
                em.wait("ve", "pe_ct", ct_at_qc[tqc])
                em.op("ve", lambda E, tqc=tqc: E.tensor_copy(
                    ctxT[:, tqc * 512:tqc * 512 + 512], psC[0:65, :]), inc="ve_ctx")
                state["ctx_evac"] = em.val("ve_ctx")
                evac_at_qc[tqc] = em.val("ve_ctx")

            for qc in range(NQC):
                exp_at_g = []
                qc_ctx = (exp_at_g, vb)
                for g, (k0, size) in enumerate(groups):
                    bank = psA if g % 2 == 0 else psB
                    bpar = g % 2
                    ptb = pt4[:, (g % 4) * 1536:(g % 4) * 1536 + 1536]
                    em.wait("pe", "act_e", state["exp_at_bank"][bpar])
                    for j in range(size):
                        kt = k0 + j
                        em.op("pe", lambda E, b=bank, j=j, kt=kt, kth=kth, qc=qc,
                              qtb=qtb: E.matmul(
                            b[:, j * 512:j * 512 + 512],
                            kth[:, kt * 128:kt * 128 + 128],
                            qtb[:, qc * 512:qc * 512 + 512],
                            start=True, stop=True),
                            inc=("pe_s" if j == size - 1 else None))
                    if g == 0:
                        flush_tail(0)
                    if g == 1:
                        flush_tail(1)
                    em.wait("act", "pe_avg", state["av_done"][g % 4])
                    em.wait("act", "pe_s", em.val("pe_s"))
                    em.op("act", lambda E, b=bank, p=ptb, w=size * 512: E.activation(
                        p[:, 0:w], b[:, 0:w], AF.Exp, bias=expb[:, 0:1], scale=1.0),
                        inc="act_e")
                    state["exp_at_bank"][bpar] = em.val("act_e")
                    exp_at_g.append(em.val("act_e"))
                    if g >= 2:
                        av_group(qc_ctx, g - 2)
                    if g == 1 and qc >= 1:
                        epilogue_block(qc - 1, h)
                pending_tail[0] = (qc_ctx, qc)
            flush_tail(0)
            flush_tail(1)
            epilogue_block(NQC - 1, h)
            em.wait("gp", "ve_cs", em.val("ve_cs"))
            em.op("gp", lambda E, h=h: E.dma_start(
                ctx_w[h], cstage[:, :].rearrange("p (t j) -> p t j", j=65)),
                inc="gp_cs", n=16)
            if h + 2 < NH:
                em.wait("sp", "pe_s", em.val("pe_s"))
                em.wait("gp", "pe_s", em.val("pe_s"))
                em.wait("gp", "pe_avg", em.val("pe_avg"))
                em.wait("sp", "pe_avg", em.val("pe_avg"))
                prefetch(h + 2, par)
            if h == 1:
                em.op("gp", lambda E: E.dma_start(
                    w1_sb[:, :].rearrange("p (dc h) -> p dc h", h=HF),
                    w1_d.rearrange("(dc p) h -> p dc h", p=128)), inc="gp_wffn", n=16)
                em.op("gp", lambda E: E.dma_start(
                    w2_sb[:, :].rearrange("p (ht d) -> p ht d", d=D),
                    w2_d.rearrange("(ht p) d -> p ht d", p=128)), inc="gp_wffn", n=16)
        if "ctx" in debug:
            dbc = dbufs["dbc"]
            dbg_w = dbg["ctx"].rearrange("(h t p) j -> h p t j", p=128, t=NTT)
            for h in range(NH):
                em.wait("gp", "gp_cs", em.val("gp_cs"))
                em.wait("gp", "sp_dbgc", em.val("sp_dbgc"))
                em.op("gp", lambda E, h=h: E.dma_start(
                    dbc[:, :].rearrange("p (t j) -> p t j", j=65), ctx_w[h]),
                    inc="sp_dbgc", n=16)
                em.wait("gp", "sp_dbgc")
                em.op("gp", lambda E, h=h: E.dma_start(
                    dbg_w[h], dbc[:, :].rearrange("p (t j) -> p t j", j=65)),
                    inc="sp_dbgc", n=16)

    FREE(["KT", "QTb", "vhb", "ctxT", "pt4", "cstage"])
    xres = AL("xres", [128, 2 * D], F32)
    cres = AL("cres", [128, 2 * NH * 65], BF16)
    cres2 = AL("cres2", [128, 2 * D], BF16)
    rcpt = AL("rcpt", [128, 2 * NH], F32)
    zbuf = AL("zbuf", [128, 4 * D], F32)
    zzbuf = AL("zzbuf", [128, 2 * D], F32)
    lns = AL("lns", [128, 40], F32)
    xnT = AL("xnT", [128, DC * NQ], BF16)
    hT = AL("hT", [128, NHT * 512], BF16)
    z2T = AL("z2T", [128, 2 * 512], F32)
    z2n = AL("z2n", [128, 4 * D], BF16)
    ostage = AL("ostage", [128, 2 * D], F32)

    ctx_r = ctx_s.rearrange("(h t p) j -> t p h j", p=128, t=NTT)

    def _ln_cs(zios, colbase):
        return [(lns[:, colbase + 4 * i:colbase + 4 * i + 1],
                 lns[:, colbase + 4 * i + 1:colbase + 4 * i + 2],
                 lns[:, colbase + 4 * i + 2:colbase + 4 * i + 3])
                for i in range(len(zios))]

    def emit_ln_stats(zios, colbase):
        W = lambda: em.wait("ve", "ve_ln", em.val("ve_ln"))
        cs = _ln_cs(zios, colbase)
        W()
        for i, (zin, _) in enumerate(zios):
            em.op("ve", lambda E, z=zin, c=cs[i]: E.reduce_sum(c[0], z, axis=AX),
                  inc="ve_ln")
        for i, (zin, _) in enumerate(zios):
            if i == 2:
                W()
            em.op("ve", lambda E, z=zin, c=cs[i], i=i: E.scalar_tensor_tensor(
                zzbuf[:, (i % 2) * D:(i % 2) * D + D], z, 1.0, z, ALU.mult, ALU.mult,
                accum_out=c[1]), inc="ve_ln")
        W()
        for i in range(len(zios)):
            em.op("ve", lambda E, c=cs[i]: E.tensor_single_scalar(
                c[0], c[0], 1.0 / D, ALU.mult), inc="ve_ln")
            em.op("ve", lambda E, c=cs[i]: E.tensor_single_scalar(
                c[1], c[1], 1.0 / D, ALU.mult), inc="ve_ln")
        W()
        for i in range(len(zios)):
            em.op("ve", lambda E, c=cs[i]: E.scalar_tensor_tensor(
                c[2], c[0], -1.0, c[0], ALU.mult, ALU.mult), inc="ve_ln")
        W()
        for i in range(len(zios)):
            em.op("ve", lambda E, c=cs[i]: E.tensor_add(c[1], c[1], c[2]), inc="ve_ln")

    def emit_ln_finish(zios, colbase):
        W = lambda: em.wait("ve", "ve_ln", em.val("ve_ln"))
        cs = _ln_cs(zios, colbase)
        em.wait("act", "ve_ln", em.val("ve_ln"))
        for i in range(len(zios)):
            em.op("act", lambda E, c=cs[i]: E.activation(
                c[2], c[1], AF.Sqrt, bias=lneps[0:NHX, 0:1]), inc="act_ln")
        em.wait("ve", "act_ln", em.val("act_ln"))
        for i in range(len(zios)):
            em.op("ve", lambda E, c=cs[i]: E.reciprocal(c[1], c[2]), inc="ve_ln")
        W()
        for i, (zin, outbuf) in enumerate(zios):
            if outbuf is not None:
                em.op("ve", lambda E, z=zin, o=outbuf, c=cs[i]: E.tensor_scalar(
                    o, z, c[0], c[1], ALU.subtract, ALU.mult), inc="ve_ln")

    def emit_ln_batch(zios, colbase):
        emit_ln_stats(zios, colbase)
        emit_ln_finish(zios, colbase)

    def emit_phase_f():
        em.wait("act", "ve_w", em.val("ve_w"))
        em.wait("act", "sp_w", em.val("sp_w"))
        em.wait("pe", "act_e", em.val("act_e"))
        for eng in ("sp", "gp", "ve", "act"):
            em.wait(eng, "pe_s", em.val("pe_s"))
            em.wait(eng, "pe_av", em.val("pe_av"))
            em.wait(eng, "pe_ct", em.val("pe_ct"))
            em.wait(eng, "ve_cs", em.val("ve_cs"))
            em.wait(eng, "ve_ct2", em.val("ve_ct2"))
            for nm in ("sp_kt0", "sp_kt1", "gp_kt0", "gp_kt1", "gp_vh0", "gp_vh1",
                       "gp_qt", "gp_cs"):
                em.wait(eng, nm, em.val(nm))
        em.wait("pe", "gp_wffn", em.val("gp_wffn"))
        tp_slots = [psA[:, 0:128], psA[:, 512:640]]

        def ln1_ve(t):
                xb = xres[:, (t % 2) * D:(t % 2) * D + D]
                cb = cres[:, (t % 2) * NH * 65:(t % 2) * NH * 65 + NH * 65]
                c2 = cres2[:, (t % 2) * D:(t % 2) * D + D]
                rc = rcpt[:, (t % 2) * NH:(t % 2) * NH + NH]
                if t >= 2:
                    em.wait("sp", "ve_z", em.val("ve_z") - 1)
                em.op("sp", lambda E, xb=xb, t=t: E.dma_start(
                    xb, x_d[t * 128:t * 128 + 128, :]), inc="sp_xr%d" % (t % 2), n=16)
                if t >= 2:
                    em.wait("gp", "ve_rcp", em.val("ve_rcp") - 1)
                em.op("gp", lambda E, cb=cb, t=t: E.dma_start(
                    cb.rearrange("p (h j) -> p h j", j=65), ctx_r[t]),
                    inc="gp_cr%d" % (t % 2), n=16)
                em.wait("ve", "sp_xr%d" % (t % 2), em.val("sp_xr%d" % (t % 2)))
                em.wait("ve", "gp_cr%d" % (t % 2), em.val("gp_cr%d" % (t % 2)))
                cbh = cb.rearrange("p (h j) -> p h j", j=65)
                if t >= 2:
                    em.wait("ve", "ve_z", em.val("ve_z") - 1)
                em.op("ve", lambda E, cbh=cbh, rc=rc: E.reciprocal(
                    rc.rearrange("p (h o) -> p h o", o=1), cbh[:, :, 64:65]),
                    inc="ve_rcp")
                em.wait("ve", "ve_rcp", em.val("ve_rcp"))
                rc_b = bass.AP(rcpt, (t % 2) * NH, [[2 * NH, 128], [1, NH], [0, 64]])
                c2h = c2.rearrange("p (h j) -> p h j", j=64)
                em.op("ve", lambda E, c2h=c2h, cbh=cbh, rb=rc_b: E.tensor_mul(
                    c2h, cbh[:, :, 0:64], rb), inc="ve_rcp")
                em.wait("ve", "ve_rcp", em.val("ve_rcp"))
                zo = zbuf[:, (t % 4) * D:(t % 4) * D + D]
                em.op("ve", lambda E, xb=xb, c2=c2, zo=zo: E.scalar_tensor_tensor(
                    zo, xb, 1.0, c2, ALU.mult, ALU.add), inc="ve_z")

        def ln1_pe(t):
                xo = zbuf[:, (t % 4) * D:(t % 4) * D + D]
                em.wait("pe", "ve_ln", lnval_at[t])
                for dc in range(DC):
                    k = t * DC + dc
                    ecnt = "evac_xnv" if k % 2 == 0 else "evac_xna"
                    if k >= 2:
                        em.wait("pe", ecnt, em.val(ecnt))
                    em.op("pe", lambda E, s=tp_slots[k % 2], dc=dc, xo=xo: E.transpose(
                        s, xo[:, dc * 128:dc * 128 + 128], ident[:, :]), inc="pe_xn")
                    eng = "ve" if dc % 2 == 0 else "act"
                    em.wait(eng, "pe_xn", em.val("pe_xn"))
                    dst = xnT[:, dc * NQ + t * 128: dc * NQ + t * 128 + 128]
                    if eng == "ve":
                        em.op("ve", lambda E, s=tp_slots[k % 2], d=dst: E.tensor_copy(d, s),
                              inc=ecnt)
                    else:
                        em.op("act", lambda E, s=tp_slots[k % 2], d=dst: E.copy(d, s),
                              inc=ecnt)

        lnval_at = {}

        def ln1_zios(b):
            tiles = list(range(4 * b, min(4 * b + 4, NTT)))
            return tiles, [(zbuf[:, (t % 4) * D:(t % 4) * D + D],
                            zbuf[:, (t % 4) * D:(t % 4) * D + D]) for t in tiles]

        def ln1_batch_ve(b):
            tiles, zios = ln1_zios(b)
            em.wait("ve", "pe_xn", em.val("pe_xn"))
            for t in tiles:
                ln1_ve(t)
            em.wait("ve", "ve_z", em.val("ve_z"))
            emit_ln_stats(zios, 0)

        def ln1_batch_finish(b):
            tiles, zios = ln1_zios(b)
            emit_ln_finish(zios, 0)
            for t in tiles:
                lnval_at[t] = em.val("ve_ln")

        def ln1_batch_pe(b):
            for t in range(4 * b, min(4 * b + 4, NTT)):
                ln1_pe(t)

        NTC = NQ // 512
        mm_slots = [psB[:, 0:512], psB[:, 512:1024]]
        tslots = [psA[:, 1024:1152], psT[:, 0:128]]
        ln1_batch_ve(0)
        ln1_batch_finish(0)
        ln1_batch_pe(0)
        for tc in range(NTC):
            if tc + 1 < NTC:
                ln1_batch_ve(tc + 1)
            em.wait("pe", "evac_xnv", em.val("evac_xnv"))
            em.wait("pe", "evac_xna", em.val("evac_xna"))
            em.wait("pe", "act_y", em.val("act_y"))
            for ht in range(NHT):
                mi = tc * NHT + ht
                m = mm_slots[mi % 2]
                if mi >= 2:
                    em.wait("pe", "act_h", em.val("act_h") - 1)
                for dc in range(DC):
                    em.op("pe", lambda E, m=m, dc=dc, ht=ht, tc=tc: E.matmul(
                        m, w1_sb[:, dc * HF + ht * 128: dc * HF + ht * 128 + 128],
                        xnT[:, dc * NQ + tc * 512: dc * NQ + tc * 512 + 512],
                        start=(dc == 0), stop=(dc == DC - 1)),
                        inc=("pe_h" if dc == DC - 1 else None))
                em.wait("act", "pe_h", em.val("pe_h"))
                em.op("act", lambda E, m=m, ht=ht: E.activation(
                    hT[:, ht * 512:ht * 512 + 512], m, AF.Relu,
                    bias=b1_sb[:, ht:ht + 1], scale=1.0), inc="act_h")
            if tc + 1 < NTC:
                ln1_batch_finish(tc + 1)

            def z2_transposes(dt, zslot):
                em.wait("pe", "ve_y", em.val("ve_y"))
                for st4 in range(4):
                    k = dt * 4 + st4
                    sl = tslots[k % 2]
                    if True:
                        em.wait("pe", "ve_z2", max(0, em.val("ve_z2") - 1))
                    em.op("pe", lambda E, sl=sl, st4=st4, z=zslot: E.transpose(
                        sl, z[:, st4 * 128:st4 * 128 + 128], ident[:, :]), inc="pe_z2")
                    em.wait("ve", "pe_z2", em.val("pe_z2"))
                    em.op("ve", lambda E, sl=sl, st4=st4, dt=dt: E.tensor_copy(
                        z2n[:, st4 * D + dt * 128: st4 * D + dt * 128 + 128], sl),
                        inc="ve_z2")

            pending_z = None
            for dt in range(NJT):
                mi2 = tc * NJT + dt
                m = mm_slots[mi2 % 2]
                zslot = z2T[:, (dt % 2) * 512:(dt % 2) * 512 + 512]
                em.wait("pe", "act_h", em.val("act_h"))
                if mi2 >= 2:
                    em.wait("pe", "act_y", em.val("act_y") - 1)
                for ht in range(NHT):
                    em.op("pe", lambda E, m=m, ht=ht, dt=dt: E.matmul(
                        m, w2_sb[:, ht * D + dt * 128: ht * D + dt * 128 + 128],
                        hT[:, ht * 512:ht * 512 + 512],
                        start=(ht == 0), stop=(ht == NHT - 1)),
                        inc=("pe_y" if ht == NHT - 1 else None))
                if pending_z is not None:
                    z2_transposes(*pending_z)
                em.wait("act", "pe_y", em.val("pe_y"))
                em.wait("act", "pe_z2", max(0, em.val("pe_z2") - 4))
                em.op("act", lambda E, m=m, dt=dt, z=zslot: E.activation(
                    z, m, AF.Identity, bias=b2_sb[:, dt:dt + 1], scale=1.0),
                    inc="act_y")
                em.wait("ve", "act_y", em.val("act_y"))
                em.op("ve", lambda E, dt=dt, tc=tc, z=zslot: E.tensor_tensor(
                    z, z, xnT[:, dt * NQ + tc * 512: dt * NQ + tc * 512 + 512], ALU.add),
                    inc="ve_y")
                pending_z = (dt, zslot)
            z2_transposes(*pending_z)
            em.wait("ve", "ve_z2", em.val("ve_z2"))
            emit_ln_batch([(z2n[:, st4 * D:st4 * D + D], None) for st4 in range(4)], 16)
            for st4 in range(4):
                oi = tc * 4 + st4
                c0 = lns[:, 16 + 4 * st4:16 + 4 * st4 + 1]
                c1 = lns[:, 16 + 4 * st4 + 1:16 + 4 * st4 + 2]
                ob = ostage[:, (oi % 2) * D:(oi % 2) * D + D]
                if oi >= 2:
                    em.wait("ve", "sp_out%d" % (oi % 2), em.val("sp_out%d" % (oi % 2)))
                em.op("ve", lambda E, st4=st4, ob=ob, c0=c0, c1=c1: E.tensor_scalar(
                    ob, z2n[:, st4 * D:st4 * D + D], c0, c1,
                    ALU.subtract, ALU.mult), inc="ve_ln")
                em.wait("sp", "ve_ln", em.val("ve_ln"))
                em.op("sp", lambda E, ob=ob, tc=tc, st4=st4: E.dma_start(
                    out_d[tc * 512 + st4 * 128: tc * 512 + st4 * 128 + 128, :], ob),
                    inc="sp_out%d" % (oi % 2), n=16)
            if tc + 1 < NTC:
                ln1_batch_pe(tc + 1)
        if "xn" in debug:
            for dc in range(DC):
                em.wait("gp", "evac_xnv", em.val("evac_xnv"))
                em.wait("gp", "evac_xna", em.val("evac_xna"))
                em.op("gp", lambda E, dc=dc: E.dma_start(
                    dbg["xn"][dc * 128:(dc + 1) * 128, :],
                    xnT[:, dc * NQ:(dc + 1) * NQ]), inc="sp_dbgx", n=16)
        em.wait("sp", "sp_out0", em.val("sp_out0"))
        em.wait("sp", "sp_out1", em.val("sp_out1"))

    emit_phase_w()
    emit_phase_p()
    emit_phase_s()
    emit_phase_a()
    emit_phase_f()

    with nc.Block() as blk:
        em.run(blk)
    return nc


_NC_CACHE = {}


def _get_nc(key, cfg, debug=()):
    if key not in _NC_CACHE:
        _NC_CACHE[key] = build(cfg, debug=debug)
    return _NC_CACHE[key]


def kernel(L0_attn_hidden_states, L0_attn_qk_w, L0_attn_qk_b, L0_attn_v_w,
           L0_attn_v_b, L0_attn_hash_a, L0_ffn_w_gate, L0_ffn_w_noise,
           L0_ffn_W1, L0_ffn_b1, L0_ffn_W2, L0_ffn_b2):
    x = np.ascontiguousarray(np.asarray(L0_attn_hidden_states, dtype=np.float32))
    Bsz, L, D = x.shape
    cfg = CFG(L=L, NH=D // 64, HF=np.asarray(L0_ffn_W1).shape[2])
    nc = _get_nc(("full", L, D), cfg)

    com = {
        "qk_w": np.ascontiguousarray(L0_attn_qk_w, dtype=np.float32),
        "qk_b": np.ascontiguousarray(L0_attn_qk_b, dtype=np.float32),
        "v_w": np.ascontiguousarray(L0_attn_v_w, dtype=np.float32),
        "v_b": np.ascontiguousarray(L0_attn_v_b, dtype=np.float32),
        "hash_a": np.ascontiguousarray(L0_attn_hash_a, dtype=np.float32),
        "w1": np.ascontiguousarray(np.asarray(L0_ffn_W1)[1], dtype=np.float32),
        "b1": np.ascontiguousarray(np.asarray(L0_ffn_b1)[1], dtype=np.float32),
        "w2": np.ascontiguousarray(np.asarray(L0_ffn_W2)[1], dtype=np.float32),
        "b2": np.ascontiguousarray(np.asarray(L0_ffn_b2)[1], dtype=np.float32),
    }
    NQ = L // 2
    in_maps = []
    for c in range(8):
        b = c // 2
        if c % 2 == 0:
            xb = x[b]
        else:
            xb = np.concatenate([x[b, NQ:], x[b, :NQ]], axis=0)
        m = dict(com)
        m["x"] = np.ascontiguousarray(xb)
        in_maps.append(m)

    res = run_bass_kernel_spmd(nc, in_maps, core_ids=list(range(8)))
    out = np.empty((Bsz, L, D), dtype=np.float32)
    for c in range(8):
        b = c // 2
        r = res.results[c]["out"]
        if c % 2 == 0:
            out[b, :NQ] = r
        else:
            out[b, NQ:] = r
    return out


# version 08
# speedup vs baseline: 1.1443x; 1.0190x over previous
import numpy as np
import concourse.bass as bass
import concourse.mybir as mybir
from concourse.bass_utils import run_bass_kernel_spmd

F32 = mybir.dt.float32
BF16 = mybir.dt.bfloat16
AF = mybir.ActivationFunctionType
ALU = mybir.AluOpType
AX = mybir.AxisListType.X

CBIT = 4992.0
EXP_BIAS = -(CBIT + 5.0)
LN_EPS = 1e-5

ENGS = ("pe", "act", "ve", "sp", "gp")


class CFG:
    def __init__(self, L=8192, NH=16, HF=2048):
        self.L = L
        self.NH = NH
        self.D = NH * 64
        self.NQ = L // 2
        self.HF = HF
        self.DC = self.D // 128
        self.NJT = self.D // 128
        self.NKT = L // 128
        self.NQC = self.NQ // 512
        self.NTT = self.NQ // 128


class Emitter:
    def __init__(self, nc):
        self.nc = nc
        self.streams = {e: [] for e in ENGS}
        self.sems = {}
        self.cnt = {}

    def sem(self, name):
        if name not in self.sems:
            self.sems[name] = self.nc.alloc_semaphore(name)
            self.cnt[name] = 0
        return self.sems[name]

    def val(self, name):
        self.sem(name)
        return self.cnt[name]

    def op(self, eng, fn, inc=None, n=1):
        if inc is not None:
            h = self.sem(inc)
            self.cnt[inc] += n

            def wrapped(E, fn=fn, h=h, n=n):
                fn(E).then_inc(h, n)
            self.streams[eng].append(wrapped)
        else:
            self.streams[eng].append(fn)

    def wait(self, eng, name, value=None):
        if value is None:
            value = self.val(name)
        if value <= 0:
            return
        h = self.sem(name)
        self.streams[eng].append(lambda E, h=h, v=value: E.wait_ge(h, v))

    def run(self, block):
        for eng, meth in (("pe", block.tensor), ("act", block.scalar),
                          ("ve", block.vector), ("sp", block.sync),
                          ("gp", block.gpsimd)):
            fns = self.streams[eng]
            if fns:
                meth(lambda E, fns=fns: [f(E) for f in fns])
            self.streams[eng] = []


def build(cfg, debug=()):
    nc = bass.Bass("TRN2", target_bir_lowering=False)
    em = Emitter(nc)
    L, NH, D, NQ, HF = cfg.L, cfg.NH, cfg.D, cfg.NQ, cfg.HF
    DC, NJT, NKT, NQC, NTT = cfg.DC, cfg.NJT, cfg.NKT, cfg.NQC, cfg.NTT
    NHT = HF // 128
    NQUARTER = 4
    QL = L // NQUARTER
    QT_TILES = QL // 128
    QTC = QL // 512
    assert QTC >= 1 and NQC >= 1
    NHX = 128
    VHW = min(512, D)
    NVH = D // VHW
    BCW = min(512, D)

    x_d = nc.dram_tensor("x", [L, D], F32, kind="ExternalInput").ap()
    qkw_d = nc.dram_tensor("qk_w", [D, D], F32, kind="ExternalInput").ap()
    qkb_d = nc.dram_tensor("qk_b", [D], F32, kind="ExternalInput").ap()
    vw_d = nc.dram_tensor("v_w", [D, D], F32, kind="ExternalInput").ap()
    vb_d = nc.dram_tensor("v_b", [D], F32, kind="ExternalInput").ap()
    ha_d = nc.dram_tensor("hash_a", [66], F32, kind="ExternalInput").ap()
    w1_d = nc.dram_tensor("w1", [D, HF], F32, kind="ExternalInput").ap()
    b1_d = nc.dram_tensor("b1", [HF], F32, kind="ExternalInput").ap()
    w2_d = nc.dram_tensor("w2", [HF, D], F32, kind="ExternalInput").ap()
    b2_d = nc.dram_tensor("b2", [D], F32, kind="ExternalInput").ap()
    out_d = nc.dram_tensor("out", [NQ, D], F32, kind="ExternalOutput").ap()

    qkT_s = nc.dram_tensor("qkT_s", [D, L], BF16, kind="Internal").ap()
    xbf_s = nc.dram_tensor("xbf_s", [L, D], BF16, kind="Internal").ap()
    qkw_bf = nc.dram_tensor("qkw_bf", [D, D], BF16, kind="Internal").ap()
    vw_bf = nc.dram_tensor("vw_bf", [D, D], BF16, kind="Internal").ap()
    v_s = nc.dram_tensor("v_s", [L, D], BF16, kind="Internal").ap()
    ctx_s = nc.dram_tensor("ctx_s", [NH * NQ, 65], BF16, kind="Internal").ap()
    bits_s = nc.dram_tensor("bits_s", [NH, L + NQ], BF16, kind="Internal").ap()

    dbg = {}
    if "qkT" in debug:
        dbg["qkT"] = nc.dram_tensor("dbg_qkT", [D, L], F32, kind="ExternalOutput").ap()
    if "stats" in debug:
        dbg["dot1"] = nc.dram_tensor("dbg_dot1", [NH, L], F32, kind="ExternalOutput").ap()
        dbg["n2"] = nc.dram_tensor("dbg_n2", [NH, L], F32, kind="ExternalOutput").ap()
        dbg["bits"] = nc.dram_tensor("dbg_bits", [NH, L + NQ], F32, kind="ExternalOutput").ap()
    if "v" in debug:
        dbg["v"] = nc.dram_tensor("dbg_v", [L, D], F32, kind="ExternalOutput").ap()
    if "ctx" in debug:
        dbg["ctx"] = nc.dram_tensor("dbg_ctx", [NH * NQ, 65], F32, kind="ExternalOutput").ap()
    if "xn" in debug:
        dbg["xn"] = nc.dram_tensor("dbg_xn", [D, NQ], F32, kind="ExternalOutput").ap()

    guards = []

    def AL(name, shape, dtype, side="left"):
        g = nc.sbuf_tensor(name, shape, dtype, side=side)
        t = g.__enter__()
        guards.append((name, g))
        return t

    def FREE(names):
        for name in reversed(names):
            for i in range(len(guards) - 1, -1, -1):
                if guards[i][0] == name:
                    guards[i][1].__exit__(None, None, None)
                    guards.pop(i)
                    break

    ident = AL("ident", [128, 128], F32)
    expb = AL("expb", [128, 1], F32)
    lneps = AL("lneps", [128, 1], F32)
    st = AL("st", [NH, 16], F32)
    a23 = AL("a23", [NH, 2], F32)
    qkb_sb = AL("qkb_sb", [128, NJT], F32)
    b1_sb = AL("b1_sb", [128, NHT], F32)
    b2_sb = AL("b2_sb", [128, NJT], F32)

    dbufs = {}
    if "qkT" in debug:
        dbufs["dbq"] = AL("dbq", [128, L], F32)
    if "v" in debug:
        dbufs["dbv"] = AL("dbv", [128, D], F32)
    if "ctx" in debug:
        dbufs["dbc"] = AL("dbc", [128, NTT * 65], F32)

    dot1 = AL("dot1", [NH, L], F32)
    n2 = AL("n2", [NH, L], BF16)
    bits = AL("bits", [NH, L + NQ], BF16)

    qk_wT = AL("qk_wT", [128, DC * D], BF16, side="right")
    v_wT = AL("v_wT", [128, DC * D], BF16, side="right")
    vbb_sb = AL("vbb_sb", [128, D], BF16, side="right")
    a1sel = AL("a1sel", [128, NJT * NH], F32, side="right")
    n1sel = AL("n1sel", [128, NJT * NH], BF16, side="right")
    onesb = AL("onesb", [1, 128], BF16, side="right")
    vbrow = AL("vbrow", [1, D], BF16, side="right")
    onesf = AL("onesf", [1, NH], F32, side="right")

    psA = nc.alloc_psum_tensor("psA", [128, 1536], F32)
    psB = nc.alloc_psum_tensor("psB", [128, 1536], F32)
    psC = nc.alloc_psum_tensor("psC", [128, 512], F32)
    psT = nc.alloc_psum_tensor("psT", [128, 512], F32)


    def emit_phase_w():
        em.op("gp", lambda E: E.memset(ident[:, :], 0.0), inc="gp_i0")
        em.wait("gp", "gp_i0")
        em.op("gp", lambda E: E.affine_select(
            out=ident[:, :], in_=ident[:, :], compare_op=ALU.not_equal,
            fill=1.0, base=0, pattern=[[-1, 128]], channel_multiplier=1), inc="gp_init")
        em.op("sp", lambda E: E.dma_start(
            qkb_sb[:, 0:NJT], qkb_d.rearrange("(j p) -> p j", p=128),
            allow_slow_non_contiguous=True), inc="sp_w", n=16)
        em.op("sp", lambda E: E.dma_start(
            b1_sb[:, 0:NHT], b1_d.rearrange("(t p) -> p t", p=128),
            allow_slow_non_contiguous=True), inc="sp_w", n=16)
        em.op("sp", lambda E: E.dma_start(
            b2_sb[:, 0:NJT], b2_d.rearrange("(t p) -> p t", p=128),
            allow_slow_non_contiguous=True), inc="sp_w", n=16)
        em.op("ve", lambda E: E.memset(expb[:, :], EXP_BIAS), inc="ve_w")
        em.op("ve", lambda E: E.memset(lneps[:, :], LN_EPS), inc="ve_w")
        em.op("ve", lambda E: E.memset(a1sel[:, :], 0.0), inc="ve_w")
        em.op("ve", lambda E: E.memset(n1sel[:, :], 0.0), inc="ve_w")
        em.wait("ve", "ve_w", em.val("ve_w"))
        for jt in range(NJT):
            c0 = jt * NH + 2 * jt
            em.wait("sp", "ve_w", em.val("ve_w"))
            em.op("sp", lambda E, c0=c0: E.dma_start(
                a1sel[0:64, c0:c0 + 1],
                ha_d[0:64].rearrange("(p o) -> p o", o=1),
                allow_slow_non_contiguous=True), inc="sp_w", n=16)
            em.op("sp", lambda E, c0=c0: E.dma_start(
                a1sel[64:128, c0 + 1:c0 + 2],
                ha_d[0:64].rearrange("(p o) -> p o", o=1),
                allow_slow_non_contiguous=True), inc="sp_w", n=16)
            em.op("ve", lambda E, c0=c0: E.memset(n1sel[0:64, c0:c0 + 1], 1.0))
            em.op("ve", lambda E, c0=c0: E.memset(n1sel[64:128, c0 + 1:c0 + 2], 1.0),
                  inc="ve_w")
        em.op("ve", lambda E: E.memset(onesb[:, :], 1.0), inc="ve_w")
        em.op("sp", lambda E: E.dma_start(
            st[0:1, 0:2], ha_d[64:66].rearrange("(o c) -> o c", o=1)), inc="sp_w", n=16)
        em.op("gp", lambda E: E.dma_start(
            vbrow[0:1, :], vb_d.rearrange("(o c) -> o c", o=1)), inc="gp_w", n=16)
        em.wait("pe", "gp_init")
        em.wait("pe", "ve_w")
        em.wait("pe", "sp_w")
        em.wait("pe", "gp_w")
        em.op("ve", lambda E: E.memset(onesf[:, :], 1.0), inc="ve_w")
        em.wait("pe", "ve_w")
        em.op("pe", lambda E: E.matmul(psC[0:NH, 0:2], onesf[0:1, 0:NH],
                                       st[0:1, 0:2], start=True, stop=True), inc="pe_w")
        em.wait("ve", "pe_w", em.val("pe_w"))
        em.op("ve", lambda E: E.tensor_copy(a23[:, :], psC[0:NH, 0:2]), inc="ve_a23")
        for half in range(D // BCW):
            em.wait("pe", "ve_vbb", em.val("ve_vbb") if half > 0 else 0)
            em.wait("pe", "ve_a23", em.val("ve_a23"))
            em.op("pe", lambda E, h=half: E.matmul(
                psC[:, 0:BCW], onesb[0:1, 0:128],
                vbrow[0:1, h * BCW:(h + 1) * BCW], start=True, stop=True), inc="pe_w")
            em.wait("ve", "pe_w", em.val("pe_w"))
            em.op("ve", lambda E, h=half: E.tensor_copy(
                vbb_sb[:, h * BCW:(h + 1) * BCW], psC[:, 0:BCW]), inc="ve_vbb")
        em.op("gp", lambda E: E.dma_start(qkw_bf, qkw_d), inc="gp_wbf", n=16)
        em.op("gp", lambda E: E.dma_start(vw_bf, vw_d), inc="gp_wbf", n=16)
        em.op("gp", lambda E: E.dma_start(xbf_s, x_d), inc="gp_xbf", n=16)
        em.wait("sp", "gp_wbf", em.val("gp_wbf"))
        for wi, (src, dstT) in enumerate(((qkw_bf, qk_wT), (vw_bf, v_wT))):
            for dc in range(DC):
                em.op("sp", lambda E, s=src, d=dstT, dc=dc: E.dma_start(
                    d[:, dc * D:(dc + 1) * D], s[:, dc * 128:(dc + 1) * 128],
                    transpose=True), inc="sp_wt", n=16)

    xT = AL("xT", [128, 2 * DC * QL], BF16, side="right")
    qkf = AL("qkf", [128, 2 * 512], F32, side="right")
    sqb = AL("sqb", [128, 2 * 512], BF16, side="right")
    qstage = AL("qstage", [128, 2 * NJT * 512], BF16, side="right")
    vstage = AL("vstage", [128, 2 * D], BF16, side="right")

    def emit_phase_p():
        em.wait("act", "sp_w", em.val("sp_w"))
        em.wait("act", "ve_w", em.val("ve_w"))
        mm_slots = [psA[:, 1024:1536], psB[:, 0:512]]
        d1_slots = [psB[:, 512:1024], psB[:, 1024:1536]]
        n2_slots = [psC[:, 0:512], psT[:, 0:512]]
        em.wait("pe", "ve_vbb", em.val("ve_vbb"))
        em.wait("sp", "gp_xbf", em.val("gp_xbf"))
        def issue_xt(q):
            par = q % 2
            for dc in range(DC):
                em.op("sp", lambda E, q=q, dc=dc, par=par: E.dma_start(
                    xT[:, par * DC * QL + dc * QL: par * DC * QL + (dc + 1) * QL],
                    xbf_s[q * QL:(q + 1) * QL, dc * 128:(dc + 1) * 128],
                    transpose=True), inc="sp_xt%d" % par, n=16)

        issue_xt(0)
        for q in range(NQUARTER):
            par = q % 2
            xTq = xT[:, par * DC * QL:(par + 1) * DC * QL]
            if q + 1 < NQUARTER:
                em.wait("sp", "pe_qk", em.val("pe_qk"))
                em.wait("sp", "pe_v", em.val("pe_v"))
                issue_xt(q + 1)
            em.wait("pe", "sp_xt%d" % par, em.val("sp_xt%d" % par))
            em.wait("pe", "sp_wt", em.val("sp_wt"))
            em.wait("pe", "ve_v", em.val("ve_v"))
            for tc in range(QTC):
                gi = q * QTC + tc
                d1b = d1_slots[gi % 2]
                n2b = n2_slots[gi % 2]
                def stats_mms(jt, d1b, n2b, qf, sq):
                    em.wait("pe", "act_qk", em.val("act_qk"))
                    em.op("pe", lambda E, d1b=d1b, jt=jt, qf=qf: E.matmul(
                        d1b[0:NH, :], a1sel[:, jt * NH:jt * NH + NH], qf,
                        start=(jt == 0), stop=(jt == NJT - 1)), inc="pe_d1")
                    em.wait("pe", "ve_sq", em.val("ve_sq"))
                    em.op("pe", lambda E, n2b=n2b, jt=jt, sq=sq: E.matmul(
                        n2b[0:NH, :], n1sel[:, jt * NH:jt * NH + NH], sq,
                        start=(jt == 0), stop=(jt == NJT - 1)), inc="pe_n2")

                pending = None
                for jt in range(NJT):
                    mi = gi * NJT + jt
                    mmb = mm_slots[mi % 2]
                    if mi >= 2:
                        em.wait("pe", "act_qk", em.val("act_qk") - 1)
                    if jt == 0 and gi >= 2:
                        em.wait("pe", "ve_d1", em.val("ve_d1") - 1)
                        em.wait("pe", "ve_n2", em.val("ve_n2") - 1)
                    for dc in range(DC):
                        em.op("pe", lambda E, m=mmb, dc=dc, jt=jt, tc=tc, xTq=xTq: E.matmul(
                            m, qk_wT[:, (dc * NJT + jt) * 128:(dc * NJT + jt) * 128 + 128],
                            xTq[:, dc * QL + tc * 512: dc * QL + tc * 512 + 512],
                            start=(dc == 0), stop=(dc == DC - 1)),
                            inc=("pe_qk" if dc == DC - 1 else None))
                    if pending is not None:
                        stats_mms(*pending)
                    qf = qkf[:, (mi % 2) * 512:(mi % 2) * 512 + 512]
                    sq = sqb[:, (mi % 2) * 512:(mi % 2) * 512 + 512]
                    em.wait("act", "pe_qk", em.val("pe_qk"))
                    if mi >= 2:
                        em.wait("act", "pe_d1", em.val("pe_d1"))
                    em.op("act", lambda E, m=mmb, qf=qf, jt=jt: E.activation(
                        qf, m, AF.Identity, bias=qkb_sb[:, jt:jt + 1], scale=1.0),
                        inc="act_qk")
                    em.wait("ve", "act_qk", em.val("act_qk"))
                    if mi >= 2:
                        em.wait("ve", "pe_n2", em.val("pe_n2"))
                    em.op("ve", lambda E, sq=sq, qf=qf: E.tensor_mul(sq, qf, qf),
                          inc="ve_sq")
                    stg = qstage[:, (gi % 2) * NJT * 512 + jt * 512:
                                 (gi % 2) * NJT * 512 + jt * 512 + 512]
                    if gi >= 2 and jt == 0:
                        em.wait("ve", "gp_qs%d" % (gi % 2), em.val("gp_qs%d" % (gi % 2)))
                    em.op("ve", lambda E, stg=stg, qf=qf: E.tensor_copy(stg, qf),
                          inc="ve_qs")
                    pending = (jt, d1b, n2b, qf, sq)
                stats_mms(*pending)
                em.wait("ve", "pe_d1", em.val("pe_d1"))
                em.wait("ve", "pe_n2", em.val("pe_n2"))
                cs = q * QL + tc * 512
                em.op("ve", lambda E, d1b=d1b, cs=cs: E.tensor_copy(
                    dot1[:, cs:cs + 512], d1b[0:NH, :]), inc="ve_d1")
                em.op("ve", lambda E, n2b=n2b, cs=cs: E.tensor_copy(
                    n2[:, cs:cs + 512], n2b[0:NH, :]), inc="ve_n2")
                em.wait("gp", "ve_qs", em.val("ve_qs"))
                src = qstage[:, (gi % 2) * NJT * 512:(gi % 2) * NJT * 512 + NJT * 512]
                dst = qkT_s.rearrange("(jt p) Lx -> p jt Lx", p=128)[:, :, cs:cs + 512]
                em.op("gp", lambda E, s=src, d=dst: E.dma_start(
                    d, s.rearrange("p (jt c) -> p jt c", c=512)),
                    inc="gp_qs%d" % (gi % 2), n=16)
            em.wait("pe", "act_qk", em.val("act_qk"))
            for t in range(QT_TILES):
                gt = q * QT_TILES + t
                vs = vstage[:, (gt % 2) * D:(gt % 2) * D + D]
                if gt >= 2:
                    em.wait("ve", "gp_vs%d" % (gt % 2), em.val("gp_vs%d" % (gt % 2)))
                for half in range(NVH):
                    vi = gt * NVH + half
                    mmb = mm_slots[vi % 2]
                    em.wait("pe", "ve_v", max(0, em.val("ve_v") - 1))
                    for dc in range(DC):
                        em.op("pe", lambda E, m=mmb, dc=dc, t=t, h=half, xTq=xTq: E.matmul(
                            m[:, 0:VHW],
                            xTq[:, dc * QL + t * 128: dc * QL + t * 128 + 128],
                            v_wT[:, dc * D + h * VHW: dc * D + h * VHW + VHW],
                            start=(dc == 0), stop=(dc == DC - 1)),
                            inc=("pe_v" if dc == DC - 1 else None))
                    em.wait("ve", "pe_v", em.val("pe_v"))
                    em.op("ve", lambda E, m=mmb, vs=vs, h=half: E.tensor_tensor(
                        vs[:, h * VHW:h * VHW + VHW], m[:, 0:VHW],
                        vbb_sb[:, h * VHW:h * VHW + VHW], ALU.add), inc="ve_v")
                em.wait("gp", "ve_v", em.val("ve_v"))
                em.op("gp", lambda E, vs=vs, q=q, t=t: E.dma_start(
                    v_s[q * QL + t * 128: q * QL + t * 128 + 128, :], vs),
                    inc="gp_vs%d" % (gt % 2), n=16)
        if "qkT" in debug:
            db = dbufs["dbq"]
            for jt in range(NJT):
                em.wait("gp", "gp_qs0", em.val("gp_qs0"))
                em.wait("gp", "gp_qs1", em.val("gp_qs1"))
                em.wait("gp", "sp_dbg", em.val("sp_dbg"))
                em.op("gp", lambda E, jt=jt: E.dma_start(
                    db[:, :], qkT_s[jt * 128:(jt + 1) * 128, :]), inc="sp_dbg", n=16)
                em.wait("gp", "sp_dbg")
                em.op("gp", lambda E, jt=jt: E.dma_start(
                    dbg["qkT"][jt * 128:(jt + 1) * 128, :], db[:, :]), inc="sp_dbg", n=16)
        if "v" in debug:
            dbv = dbufs["dbv"]
            for t in range(L // 128):
                em.wait("gp", "gp_vs0", em.val("gp_vs0"))
                em.wait("gp", "gp_vs1", em.val("gp_vs1"))
                em.wait("gp", "sp_dbgv", em.val("sp_dbgv"))
                em.op("gp", lambda E, t=t: E.dma_start(
                    dbv[:, :], v_s[t * 128:(t + 1) * 128, :]), inc="sp_dbgv", n=16)
                em.wait("gp", "sp_dbgv")
                em.op("gp", lambda E, t=t: E.dma_start(
                    dbg["v"][t * 128:(t + 1) * 128, :], dbv[:, :]), inc="sp_dbgv", n=16)

    FREE(["xT", "qkf", "sqb", "qstage", "vstage"])
    FREE(["onesf"])
    FREE(["qk_wT", "v_wT", "vbb_sb", "a1sel", "n1sel", "onesb", "vbrow"])
    hkv = AL("hkv", [NH, L], BF16)

    def emit_phase_s():
        V = "ve"
        for nm in ("pe_qk", "pe_v", "pe_d1", "pe_n2", "gp_qs0", "gp_qs1",
                   "gp_vs0", "gp_vs1", "act_qk", "ve_qs", "ve_v", "ve_sq"):
            em.wait(V, nm, em.val(nm))
        em.wait(V, "ve_a23", em.val("ve_a23"))
        c = lambda i: st[:, i:i + 1]
        em.wait(V, "ve_s", em.val("ve_s"))
        em.op(V, lambda E: E.reduce_max(c(0), n2[:, :], axis=AX), inc="ve_s")
        em.wait("act", "ve_s", em.val("ve_s"))
        em.op("act", lambda E: E.activation(c(1), c(0), AF.Sqrt), inc="act_s")
        em.wait(V, "act_s", em.val("act_s"))
        em.wait(V, "ve_s", em.val("ve_s"))
        em.op(V, lambda E: E.tensor_single_scalar(c(2), c(1), 1e-12, ALU.max), inc="ve_s")
        em.wait(V, "ve_s", em.val("ve_s"))
        em.op(V, lambda E: E.reciprocal(c(3), c(2)), inc="ve_s")
        em.wait(V, "ve_s", em.val("ve_s"))
        em.op(V, lambda E: E.tensor_single_scalar(c(2), c(3), 0.75, ALU.mult), inc="ve_s")
        em.wait(V, "ve_s", em.val("ve_s"))
        em.op(V, lambda E: E.tensor_mul(c(4), c(2), c(2)), inc="ve_s")
        em.wait(V, "ve_s", em.val("ve_s"))
        em.op(V, lambda E: E.tensor_copy(bits[:, 0:L], dot1[:, :]), inc="ve_s")
        em.wait(V, "ve_s", em.val("ve_s"))
        em.op(V, lambda E: E.tensor_scalar_mul(hkv[:, :], n2[:, :], c(4)), inc="ve_s")
        em.wait(V, "ve_s", em.val("ve_s"))
        em.op(V, lambda E: E.tensor_scalar(n2[:, :], hkv[:, :], -1.0, 0.5,
                                           ALU.mult, ALU.add), inc="ve_s")
        em.wait(V, "ve_s", em.val("ve_s"))
        em.op(V, lambda E: E.tensor_mul(hkv[:, :], hkv[:, :], hkv[:, :]), inc="ve_s")
        em.wait(V, "ve_s", em.val("ve_s"))
        em.op(V, lambda E: E.tensor_scalar(hkv[:, :], hkv[:, :], -1.0, 0.5,
                                           ALU.mult, ALU.add), inc="ve_s")
        em.wait(V, "ve_s", em.val("ve_s"))
        em.op(V, lambda E: E.tensor_scalar_mul(hkv[:, :], hkv[:, :], a23[:, 1:2]), inc="ve_s")
        em.wait(V, "ve_s", em.val("ve_s"))
        em.op(V, lambda E: E.scalar_tensor_tensor(
            hkv[:, :], n2[:, :], a23[:, 0:1], hkv[:, :], ALU.mult, ALU.add), inc="ve_s")
        em.wait(V, "ve_s", em.val("ve_s"))
        em.op(V, lambda E: E.scalar_tensor_tensor(
            hkv[:, :], bits[:, 0:L], c(2), hkv[:, :], ALU.mult, ALU.add), inc="ve_s")
        em.wait(V, "ve_s", em.val("ve_s"))
        em.op(V, lambda E: E.tensor_single_scalar(hkv[:, :], hkv[:, :], 0.0, ALU.is_ge),
              inc="ve_s")
        em.wait(V, "ve_s", em.val("ve_s"))
        em.op(V, lambda E: E.tensor_scalar(bits[:, 0:L], hkv[:, :], 128.0, -64.0,
                                           ALU.mult, ALU.add), inc="ve_s")
        em.wait(V, "ve_s", em.val("ve_s"))
        em.op(V, lambda E: E.reduce_max(c(5), hkv[:, :], axis=AX), inc="ve_s")
        em.wait(V, "ve_s", em.val("ve_s"))
        em.op(V, lambda E: E.tensor_reduce(c(6), hkv[:, :], axis=AX, op=ALU.min), inc="ve_s")
        em.wait(V, "ve_s", em.val("ve_s"))
        em.op(V, lambda E: E.tensor_sub(c(7), c(5), c(6)), inc="ve_s")
        em.wait(V, "ve_s", em.val("ve_s"))
        em.op(V, lambda E: E.tensor_add(c(8), c(5), c(6)), inc="ve_s")
        em.wait(V, "ve_s", em.val("ve_s"))
        em.op(V, lambda E: E.tensor_single_scalar(c(8), c(8), 1.0, ALU.subtract),
              inc="ve_s")
        em.wait(V, "ve_s", em.val("ve_s"))
        em.op(V, lambda E: E.tensor_single_scalar(c(9), c(7), 156.0, ALU.mult), inc="ve_s")
        em.wait(V, "ve_s", em.val("ve_s"))
        em.op(V, lambda E: E.tensor_sub(c(10), c(8), c(7)), inc="ve_s")
        em.wait(V, "ve_s", em.val("ve_s"))
        em.op(V, lambda E: E.tensor_single_scalar(c(10), c(10), 78.0, ALU.mult), inc="ve_s")
        em.wait(V, "ve_s", em.val("ve_s"))
        em.op(V, lambda E: E.tensor_single_scalar(dot1[:, 0:NQ], dot1[:, 0:NQ], 0.0,
                                                  ALU.is_ge), inc="ve_s")
        em.wait(V, "ve_s", em.val("ve_s"))
        em.op(V, lambda E: E.tensor_scalar(bits[:, L:L + NQ], dot1[:, 0:NQ], c(9), c(10),
                                           ALU.mult, ALU.add), inc="ve_s")
        em.wait("gp", "ve_s", em.val("ve_s"))
        em.op("gp", lambda E: E.dma_start(bits_s, bits[:, :]), inc="gp_bits", n=16)
        if "stats" in debug:
            em.wait("gp", "ve_s", em.val("ve_s"))
            em.op("gp", lambda E: E.dma_start(dbg["dot1"], dot1[:, :]), inc="sp_dbgs", n=16)
            em.op("gp", lambda E: E.dma_start(dbg["n2"], n2[:, :]), inc="sp_dbgs", n=16)
            em.op("gp", lambda E: E.dma_start(dbg["bits"], bits[:, :]), inc="sp_dbgs", n=16)

    FREE(["hkv"])
    FREE(["dot1", "n2", "bits"])
    w1_sb = AL("w1_sb", [128, DC * HF], BF16)
    w2_sb = AL("w2_sb", [128, NHT * D], BF16)
    KT = AL("KT", [65, 2 * L], BF16)
    QTb = AL("QTb", [65, 2 * NQ], BF16)
    vhb = AL("vhb", [128, 2 * NKT * 65], BF16)
    ctxT = AL("ctxT", [65, NQ], F32)
    pt4 = AL("pt4", [128, 4 * 1536], BF16)
    cstage = AL("cstage", [128, NTT * 65], BF16)

    groups = []
    k = 0
    while k < NKT:
        size = min(3, NKT - k)
        groups.append((k, size))
        k += size

    ctx_w = ctx_s.rearrange("(h t p) j -> h p t j", p=128, t=NTT)

    def emit_phase_a():
        em.wait("pe", "ve_d1", em.val("ve_d1"))
        em.wait("pe", "ve_n2", em.val("ve_n2"))
        em.wait("pe", "ve_v", em.val("ve_v"))
        em.wait("pe", "act_qk", em.val("act_qk"))
        em.wait("pe", "evac_xp", em.val("evac_xp"))

        em.wait("act", "ve_w", em.val("ve_w"))
        state = {"exp_at_bank": {0: 0, 1: 0}, "ctx_evac": 0,
                 "av_done": {0: 0, 1: 0, 2: 0, 3: 0}}

        def prefetch(h, into):
            kth = KT[:, into * L: into * L + L]
            em.op("sp", lambda E, h=h, kth=kth: E.dma_start(
                kth[0:64, :], qkT_s[h * 64:(h + 1) * 64, :]),
                inc="sp_kt%d" % into, n=16)
            em.op("gp", lambda E, h=h, kth=kth: E.dma_start(
                kth[64:65, :], bits_s[h:h + 1, 0:L]), inc="gp_kt%d" % into, n=16)
            vb = vhb[:, into * NKT * 65: into * NKT * 65 + NKT * 65]
            dstv = vb.rearrange("p (k j) -> p k j", j=65)[:, :, 0:64]
            srcv = v_s[:, h * 64:(h + 1) * 64].rearrange("(k p) j -> p k j", p=128)
            KH = NKT // 2
            em.op("sp", lambda E, d=dstv, s=srcv, KH=KH: E.dma_start(
                d[:, 0:KH, :], s[:, 0:KH, :]), inc="sp_vh%d" % into, n=16)
            em.op("gp", lambda E, d=dstv, s=srcv, KH=KH: E.dma_start(
                d[:, KH:NKT, :], s[:, KH:NKT, :]), inc="gp_vh%d" % into, n=16)

        em.wait("ve", "sp_dbgs", em.val("sp_dbgs"))
        em.wait("ve", "gp_bits", em.val("gp_bits"))
        em.op("ve", lambda E: E.memset(vhb[:, :], 1.0), inc="ve_ones")
        em.wait("gp", "ve_ones", em.val("ve_ones"))
        em.wait("sp", "ve_ones", em.val("ve_ones"))
        em.wait("gp", "gp_bits", em.val("gp_bits"))
        em.wait("sp", "ve_s", em.val("ve_s"))
        em.wait("sp", "gp_bits", em.val("gp_bits"))
        em.wait("sp", "sp_dbgs", em.val("sp_dbgs"))
        em.wait("gp", "sp_dbgs", em.val("sp_dbgs"))
        em.wait("sp", "gp_qs0", em.val("gp_qs0"))
        em.wait("sp", "gp_qs1", em.val("gp_qs1"))
        em.wait("gp", "gp_vs0", em.val("gp_vs0"))
        em.wait("gp", "gp_vs1", em.val("gp_vs1"))
        prefetch(0, 0)
        prefetch(1, 1)

        ct_at_qc = [0] * NQC

        def build_qt(h):
            par = h % 2
            kth = KT[:, par * L: par * L + L]
            qtb = QTb[:, par * NQ: par * NQ + NQ]
            em.wait("ve", "sp_kt%d" % par, em.val("sp_kt%d" % par))
            em.wait("ve", "gp_kt%d" % par, em.val("gp_kt%d" % par))
            em.wait("ve", "pe_s", em.val("pe_s"))
            em.op("ve", lambda E, kth=kth, q=qtb: E.tensor_scalar_mul(
                q[0:64, :], kth[0:64, 0:NQ], 0.125), inc="ve_qt")
            em.wait("gp", "pe_s", em.val("pe_s"))
            em.op("gp", lambda E, h=h, q=qtb: E.dma_start(
                q[64:65, :], bits_s[h:h + 1, L:L + NQ]), inc="gp_qt%d" % par, n=16)

        build_qt(0)
        for h in range(NH):
            par = h % 2
            kth = KT[:, par * L: par * L + L]
            vb = vhb[:, par * NKT * 65: par * NKT * 65 + NKT * 65]
            qtb = QTb[:, par * NQ: par * NQ + NQ]
            if h + 1 < NH:
                build_qt(h + 1)
            em.wait("pe", "ve_qt", em.val("ve_qt") - (1 if h + 1 < NH else 0))
            em.wait("pe", "gp_qt%d" % par, em.val("gp_qt%d" % par))
            em.wait("pe", "gp_vh%d" % par, em.val("gp_vh%d" % par))
            em.wait("pe", "sp_vh%d" % par, em.val("sp_vh%d" % par))
            em.wait("pe", "sp_kt%d" % par, em.val("sp_kt%d" % par))
            em.wait("pe", "gp_kt%d" % par, em.val("gp_kt%d" % par))

            evac_at_qc = [0] * NQC

            def epilogue_block(b, h):
                em.wait("pe", "ve_ctx", evac_at_qc[b])
                em.wait("pe", "ve_cs", em.val("ve_cs"))
                for j in range(4):
                    tt = b * 4 + j
                    em.op("pe", lambda E, j=j, tt=tt: E.transpose(
                        psT[:, j * 80:j * 80 + 65], ctxT[:, tt * 128:tt * 128 + 128],
                        ident[0:65, 0:65]), inc=("pe_ct" if j == 3 else None))
                ct_at_qc[b] = em.val("pe_ct")
                em.wait("ve", "pe_ct", em.val("pe_ct"))
                if b == 0:
                    em.wait("ve", "gp_cs", em.val("gp_cs"))
                em.op("ve", lambda E, b=b: E.tensor_copy(
                    cstage[:, b * 4 * 65:(b + 1) * 4 * 65].rearrange(
                        "p (t j) -> p t j", j=65),
                    psT[:, 0:320].rearrange("p (t j) -> p t j", j=80)[:, :, 0:65]),
                    inc="ve_cs")

            pending_tail = [None]

            def av_group(qc_ctx, j):
                exp_at_g, vbx = qc_ctx
                k0, size = groups[j]
                ptb = pt4[:, (j % 4) * 1536:(j % 4) * 1536 + 1536]
                em.wait("pe", "act_e", exp_at_g[j])
                if k0 == 0:
                    em.wait("pe", "ve_ctx", state["ctx_evac"])
                for j2 in range(size):
                    kt = k0 + j2
                    em.op("pe", lambda E, v=vbx, kt=kt, p=ptb, j2=j2: E.matmul(
                        psC[0:65, :], v[:, kt * 65:kt * 65 + 65],
                        p[:, j2 * 512:j2 * 512 + 512],
                        start=(kt == 0), stop=(kt == NKT - 1),
                        skip_group_check=True),
                        inc=("pe_avg" if j2 == size - 1 else None))
                state["av_done"][j % 4] = em.val("pe_avg")

            def flush_tail(stage):
                if pending_tail[0] is None:
                    return
                qc_ctx, tqc = pending_tail[0]
                if stage == 0:
                    av_group(qc_ctx, len(groups) - 2)
                    return
                pending_tail[0] = None
                av_group(qc_ctx, len(groups) - 1)
                em.wait("ve", "pe_avg", em.val("pe_avg"))
                em.wait("ve", "pe_ct", ct_at_qc[tqc])
                em.op("ve", lambda E, tqc=tqc: E.tensor_copy(
                    ctxT[:, tqc * 512:tqc * 512 + 512], psC[0:65, :]), inc="ve_ctx")
                state["ctx_evac"] = em.val("ve_ctx")
                evac_at_qc[tqc] = em.val("ve_ctx")

            for qc in range(NQC):
                exp_at_g = []
                qc_ctx = (exp_at_g, vb)
                for g, (k0, size) in enumerate(groups):
                    bank = psA if g % 2 == 0 else psB
                    bpar = g % 2
                    ptb = pt4[:, (g % 4) * 1536:(g % 4) * 1536 + 1536]
                    em.wait("pe", "act_e", state["exp_at_bank"][bpar])
                    for j in range(size):
                        kt = k0 + j
                        em.op("pe", lambda E, b=bank, j=j, kt=kt, kth=kth, qc=qc,
                              qtb=qtb: E.matmul(
                            b[:, j * 512:j * 512 + 512],
                            kth[:, kt * 128:kt * 128 + 128],
                            qtb[:, qc * 512:qc * 512 + 512],
                            start=True, stop=True),
                            inc=("pe_s" if j == size - 1 else None))
                    if g == 0:
                        flush_tail(0)
                    if g == 1:
                        flush_tail(1)
                    em.wait("act", "pe_avg", state["av_done"][g % 4])
                    em.wait("act", "pe_s", em.val("pe_s"))
                    em.op("act", lambda E, b=bank, p=ptb, w=size * 512: E.activation(
                        p[:, 0:w], b[:, 0:w], AF.Exp, bias=expb[:, 0:1], scale=1.0),
                        inc="act_e")
                    state["exp_at_bank"][bpar] = em.val("act_e")
                    exp_at_g.append(em.val("act_e"))
                    if g >= 2:
                        av_group(qc_ctx, g - 2)
                    if g == 1 and qc >= 1:
                        epilogue_block(qc - 1, h)
                pending_tail[0] = (qc_ctx, qc)
            flush_tail(0)
            flush_tail(1)
            epilogue_block(NQC - 1, h)
            em.wait("gp", "ve_cs", em.val("ve_cs"))
            em.op("gp", lambda E, h=h: E.dma_start(
                ctx_w[h], cstage[:, :].rearrange("p (t j) -> p t j", j=65)),
                inc="gp_cs", n=16)
            if h + 2 < NH:
                em.wait("sp", "pe_s", em.val("pe_s"))
                em.wait("gp", "pe_s", em.val("pe_s"))
                em.wait("gp", "pe_avg", em.val("pe_avg"))
                em.wait("sp", "pe_avg", em.val("pe_avg"))
                prefetch(h + 2, par)
            if h == 1:
                em.op("gp", lambda E: E.dma_start(
                    w1_sb[:, :].rearrange("p (dc h) -> p dc h", h=HF),
                    w1_d.rearrange("(dc p) h -> p dc h", p=128)), inc="gp_wffn", n=16)
                em.op("gp", lambda E: E.dma_start(
                    w2_sb[:, :].rearrange("p (ht d) -> p ht d", d=D),
                    w2_d.rearrange("(ht p) d -> p ht d", p=128)), inc="gp_wffn", n=16)
        if "ctx" in debug:
            dbc = dbufs["dbc"]
            dbg_w = dbg["ctx"].rearrange("(h t p) j -> h p t j", p=128, t=NTT)
            for h in range(NH):
                em.wait("gp", "gp_cs", em.val("gp_cs"))
                em.wait("gp", "sp_dbgc", em.val("sp_dbgc"))
                em.op("gp", lambda E, h=h: E.dma_start(
                    dbc[:, :].rearrange("p (t j) -> p t j", j=65), ctx_w[h]),
                    inc="sp_dbgc", n=16)
                em.wait("gp", "sp_dbgc")
                em.op("gp", lambda E, h=h: E.dma_start(
                    dbg_w[h], dbc[:, :].rearrange("p (t j) -> p t j", j=65)),
                    inc="sp_dbgc", n=16)

    FREE(["KT", "QTb", "vhb", "ctxT", "pt4", "cstage"])
    xres = AL("xres", [128, 2 * D], F32)
    cres = AL("cres", [128, 2 * NH * 65], BF16)
    cres2 = AL("cres2", [128, 2 * D], BF16)
    rcpt = AL("rcpt", [128, 2 * NH], F32)
    zbuf = AL("zbuf", [128, 4 * D], F32)
    zzbuf = AL("zzbuf", [128, 2 * D], F32)
    lns = AL("lns", [128, 40], F32)
    xnT = AL("xnT", [128, DC * NQ], BF16)
    hT = AL("hT", [128, NHT * 512], BF16)
    z2T = AL("z2T", [128, 2 * 512], F32)
    z2n = AL("z2n", [128, 4 * D], BF16)
    ostage = AL("ostage", [128, 2 * D], F32)

    ctx_r = ctx_s.rearrange("(h t p) j -> t p h j", p=128, t=NTT)

    def _ln_cs(zios, colbase):
        return [(lns[:, colbase + 4 * i:colbase + 4 * i + 1],
                 lns[:, colbase + 4 * i + 1:colbase + 4 * i + 2],
                 lns[:, colbase + 4 * i + 2:colbase + 4 * i + 3])
                for i in range(len(zios))]

    def emit_ln_stats(zios, colbase):
        W = lambda: em.wait("ve", "ve_ln", em.val("ve_ln"))
        cs = _ln_cs(zios, colbase)
        W()
        for i, (zin, _) in enumerate(zios):
            em.op("ve", lambda E, z=zin, c=cs[i]: E.reduce_sum(c[0], z, axis=AX),
                  inc="ve_ln")
        for i, (zin, _) in enumerate(zios):
            if i == 2:
                W()
            em.op("ve", lambda E, z=zin, c=cs[i], i=i: E.scalar_tensor_tensor(
                zzbuf[:, (i % 2) * D:(i % 2) * D + D], z, 1.0, z, ALU.mult, ALU.mult,
                accum_out=c[1]), inc="ve_ln")
        W()
        for i in range(len(zios)):
            em.op("ve", lambda E, c=cs[i]: E.tensor_single_scalar(
                c[0], c[0], 1.0 / D, ALU.mult), inc="ve_ln")
            em.op("ve", lambda E, c=cs[i]: E.tensor_single_scalar(
                c[1], c[1], 1.0 / D, ALU.mult), inc="ve_ln")
        W()
        for i in range(len(zios)):
            em.op("ve", lambda E, c=cs[i]: E.scalar_tensor_tensor(
                c[2], c[0], -1.0, c[0], ALU.mult, ALU.mult), inc="ve_ln")
        W()
        for i in range(len(zios)):
            em.op("ve", lambda E, c=cs[i]: E.tensor_add(c[1], c[1], c[2]), inc="ve_ln")

    def emit_ln_finish(zios, colbase):
        W = lambda: em.wait("ve", "ve_ln", em.val("ve_ln"))
        cs = _ln_cs(zios, colbase)
        em.wait("act", "ve_ln", em.val("ve_ln"))
        for i in range(len(zios)):
            em.op("act", lambda E, c=cs[i]: E.activation(
                c[2], c[1], AF.Sqrt, bias=lneps[0:NHX, 0:1]), inc="act_ln")
        em.wait("ve", "act_ln", em.val("act_ln"))
        for i in range(len(zios)):
            em.op("ve", lambda E, c=cs[i]: E.reciprocal(c[1], c[2]), inc="ve_ln")
        W()
        for i, (zin, outbuf) in enumerate(zios):
            if outbuf is not None:
                em.op("ve", lambda E, z=zin, o=outbuf, c=cs[i]: E.tensor_scalar(
                    o, z, c[0], c[1], ALU.subtract, ALU.mult), inc="ve_ln")

    def emit_ln_batch(zios, colbase):
        emit_ln_stats(zios, colbase)
        emit_ln_finish(zios, colbase)

    def emit_phase_f():
        em.wait("act", "ve_w", em.val("ve_w"))
        em.wait("act", "sp_w", em.val("sp_w"))
        em.wait("pe", "act_e", em.val("act_e"))
        for eng in ("sp", "gp", "ve", "act"):
            em.wait(eng, "pe_s", em.val("pe_s"))
            em.wait(eng, "pe_av", em.val("pe_av"))
            em.wait(eng, "pe_ct", em.val("pe_ct"))
            em.wait(eng, "ve_cs", em.val("ve_cs"))
            em.wait(eng, "ve_ct2", em.val("ve_ct2"))
            for nm in ("sp_kt0", "sp_kt1", "gp_kt0", "gp_kt1", "gp_vh0", "gp_vh1",
                       "gp_qt", "gp_cs"):
                em.wait(eng, nm, em.val(nm))
        em.wait("pe", "gp_wffn", em.val("gp_wffn"))
        xn_slots = [psA[:, 0:128], psA[:, 512:640], psB[:, 1024:1152], psC[:, 0:128]]

        def ln1_ve(t):
                xb = xres[:, (t % 2) * D:(t % 2) * D + D]
                cb = cres[:, (t % 2) * NH * 65:(t % 2) * NH * 65 + NH * 65]
                c2 = cres2[:, (t % 2) * D:(t % 2) * D + D]
                rc = rcpt[:, (t % 2) * NH:(t % 2) * NH + NH]
                if t >= 2:
                    em.wait("sp", "ve_z", em.val("ve_z") - 1)
                em.op("sp", lambda E, xb=xb, t=t: E.dma_start(
                    xb, x_d[t * 128:t * 128 + 128, :]), inc="sp_xr%d" % (t % 2), n=16)
                if t >= 2:
                    em.wait("gp", "ve_rcp", em.val("ve_rcp") - 1)
                em.op("gp", lambda E, cb=cb, t=t: E.dma_start(
                    cb.rearrange("p (h j) -> p h j", j=65), ctx_r[t]),
                    inc="gp_cr%d" % (t % 2), n=16)
                em.wait("ve", "sp_xr%d" % (t % 2), em.val("sp_xr%d" % (t % 2)))
                em.wait("ve", "gp_cr%d" % (t % 2), em.val("gp_cr%d" % (t % 2)))
                cbh = cb.rearrange("p (h j) -> p h j", j=65)
                if t >= 2:
                    em.wait("ve", "ve_z", em.val("ve_z") - 1)
                em.op("ve", lambda E, cbh=cbh, rc=rc: E.reciprocal(
                    rc.rearrange("p (h o) -> p h o", o=1), cbh[:, :, 64:65]),
                    inc="ve_rcp")
                em.wait("ve", "ve_rcp", em.val("ve_rcp"))
                rc_b = bass.AP(rcpt, (t % 2) * NH, [[2 * NH, 128], [1, NH], [0, 64]])
                c2h = c2.rearrange("p (h j) -> p h j", j=64)
                em.op("ve", lambda E, c2h=c2h, cbh=cbh, rb=rc_b: E.tensor_mul(
                    c2h, cbh[:, :, 0:64], rb), inc="ve_rcp")
                em.wait("ve", "ve_rcp", em.val("ve_rcp"))
                zo = zbuf[:, (t % 4) * D:(t % 4) * D + D]
                em.op("ve", lambda E, xb=xb, c2=c2, zo=zo: E.scalar_tensor_tensor(
                    zo, xb, 1.0, c2, ALU.mult, ALU.add), inc="ve_z")

        def ln1_pe(t):
                xo = zbuf[:, (t % 4) * D:(t % 4) * D + D]
                em.wait("pe", "ve_ln", lnval_at[t])
                for dc in range(DC):
                    k = t * DC + dc
                    sl = xn_slots[k % 4]
                    ecnt = "evac_xnv" if k % 2 == 0 else "evac_xna"
                    if k >= 4:
                        em.wait("pe", ecnt, em.val(ecnt) - 1)
                    em.op("pe", lambda E, s=sl, dc=dc, xo=xo: E.transpose(
                        s, xo[:, dc * 128:dc * 128 + 128], ident[:, :]), inc="pe_xn")
                    eng = "ve" if dc % 2 == 0 else "act"
                    em.wait(eng, "pe_xn", em.val("pe_xn"))
                    dst = xnT[:, dc * NQ + t * 128: dc * NQ + t * 128 + 128]
                    if eng == "ve":
                        em.op("ve", lambda E, s=sl, d=dst: E.tensor_copy(d, s),
                              inc=ecnt)
                    else:
                        em.op("act", lambda E, s=sl, d=dst: E.copy(d, s),
                              inc=ecnt)

        lnval_at = {}

        def ln1_zios(b):
            tiles = list(range(4 * b, min(4 * b + 4, NTT)))
            return tiles, [(zbuf[:, (t % 4) * D:(t % 4) * D + D],
                            zbuf[:, (t % 4) * D:(t % 4) * D + D]) for t in tiles]

        def ln1_batch_ve(b):
            tiles, zios = ln1_zios(b)
            em.wait("ve", "pe_xn", em.val("pe_xn"))
            for t in tiles:
                ln1_ve(t)
            em.wait("ve", "ve_z", em.val("ve_z"))
            emit_ln_stats(zios, 0)

        def ln1_batch_finish(b):
            tiles, zios = ln1_zios(b)
            emit_ln_finish(zios, 0)
            for t in tiles:
                lnval_at[t] = em.val("ve_ln")

        def ln1_batch_pe(b):
            for t in range(4 * b, min(4 * b + 4, NTT)):
                ln1_pe(t)

        NTC = NQ // 512
        mm_slots = [psB[:, 0:512], psB[:, 512:1024]]
        tslots = [psA[:, 1024:1152], psT[:, 0:128]]
        ln1_batch_ve(0)
        ln1_batch_finish(0)
        ln1_batch_pe(0)
        for tc in range(NTC):
            if tc + 1 < NTC:
                ln1_batch_ve(tc + 1)
            em.wait("pe", "evac_xnv", em.val("evac_xnv"))
            em.wait("pe", "evac_xna", em.val("evac_xna"))
            em.wait("pe", "act_y", em.val("act_y"))
            for ht in range(NHT):
                mi = tc * NHT + ht
                m = mm_slots[mi % 2]
                if mi >= 2:
                    em.wait("pe", "act_h", em.val("act_h") - 1)
                for dc in range(DC):
                    em.op("pe", lambda E, m=m, dc=dc, ht=ht, tc=tc: E.matmul(
                        m, w1_sb[:, dc * HF + ht * 128: dc * HF + ht * 128 + 128],
                        xnT[:, dc * NQ + tc * 512: dc * NQ + tc * 512 + 512],
                        start=(dc == 0), stop=(dc == DC - 1)),
                        inc=("pe_h" if dc == DC - 1 else None))
                em.wait("act", "pe_h", em.val("pe_h"))
                em.op("act", lambda E, m=m, ht=ht: E.activation(
                    hT[:, ht * 512:ht * 512 + 512], m, AF.Relu,
                    bias=b1_sb[:, ht:ht + 1], scale=1.0), inc="act_h")
            if tc + 1 < NTC:
                ln1_batch_finish(tc + 1)

            def z2_transposes(dt, zslot):
                em.wait("pe", "ve_y", em.val("ve_y"))
                for st4 in range(4):
                    k = dt * 4 + st4
                    sl = tslots[k % 2]
                    if True:
                        em.wait("pe", "ve_z2", max(0, em.val("ve_z2") - 1))
                    em.op("pe", lambda E, sl=sl, st4=st4, z=zslot: E.transpose(
                        sl, z[:, st4 * 128:st4 * 128 + 128], ident[:, :]), inc="pe_z2")
                    em.wait("ve", "pe_z2", em.val("pe_z2"))
                    em.op("ve", lambda E, sl=sl, st4=st4, dt=dt: E.tensor_copy(
                        z2n[:, st4 * D + dt * 128: st4 * D + dt * 128 + 128], sl),
                        inc="ve_z2")

            pending_z = None
            for dt in range(NJT):
                mi2 = tc * NJT + dt
                m = mm_slots[mi2 % 2]
                zslot = z2T[:, (dt % 2) * 512:(dt % 2) * 512 + 512]
                em.wait("pe", "act_h", em.val("act_h"))
                if mi2 >= 2:
                    em.wait("pe", "act_y", em.val("act_y") - 1)
                for ht in range(NHT):
                    em.op("pe", lambda E, m=m, ht=ht, dt=dt: E.matmul(
                        m, w2_sb[:, ht * D + dt * 128: ht * D + dt * 128 + 128],
                        hT[:, ht * 512:ht * 512 + 512],
                        start=(ht == 0), stop=(ht == NHT - 1)),
                        inc=("pe_y" if ht == NHT - 1 else None))
                if pending_z is not None:
                    z2_transposes(*pending_z)
                em.wait("act", "pe_y", em.val("pe_y"))
                em.wait("act", "pe_z2", max(0, em.val("pe_z2") - 4))
                em.op("act", lambda E, m=m, dt=dt, z=zslot: E.activation(
                    z, m, AF.Identity, bias=b2_sb[:, dt:dt + 1], scale=1.0),
                    inc="act_y")
                em.wait("ve", "act_y", em.val("act_y"))
                em.op("ve", lambda E, dt=dt, tc=tc, z=zslot: E.tensor_tensor(
                    z, z, xnT[:, dt * NQ + tc * 512: dt * NQ + tc * 512 + 512], ALU.add),
                    inc="ve_y")
                pending_z = (dt, zslot)
            z2_transposes(*pending_z)
            em.wait("ve", "ve_z2", em.val("ve_z2"))
            emit_ln_batch([(z2n[:, st4 * D:st4 * D + D], None) for st4 in range(4)], 16)
            for st4 in range(4):
                oi = tc * 4 + st4
                c0 = lns[:, 16 + 4 * st4:16 + 4 * st4 + 1]
                c1 = lns[:, 16 + 4 * st4 + 1:16 + 4 * st4 + 2]
                ob = ostage[:, (oi % 2) * D:(oi % 2) * D + D]
                if oi >= 2:
                    em.wait("ve", "sp_out%d" % (oi % 2), em.val("sp_out%d" % (oi % 2)))
                em.op("ve", lambda E, st4=st4, ob=ob, c0=c0, c1=c1: E.tensor_scalar(
                    ob, z2n[:, st4 * D:st4 * D + D], c0, c1,
                    ALU.subtract, ALU.mult), inc="ve_ln")
                em.wait("sp", "ve_ln", em.val("ve_ln"))
                em.op("sp", lambda E, ob=ob, tc=tc, st4=st4: E.dma_start(
                    out_d[tc * 512 + st4 * 128: tc * 512 + st4 * 128 + 128, :], ob),
                    inc="sp_out%d" % (oi % 2), n=16)
            if tc + 1 < NTC:
                ln1_batch_pe(tc + 1)
        if "xn" in debug:
            for dc in range(DC):
                em.wait("gp", "evac_xnv", em.val("evac_xnv"))
                em.wait("gp", "evac_xna", em.val("evac_xna"))
                em.op("gp", lambda E, dc=dc: E.dma_start(
                    dbg["xn"][dc * 128:(dc + 1) * 128, :],
                    xnT[:, dc * NQ:(dc + 1) * NQ]), inc="sp_dbgx", n=16)
        em.wait("sp", "sp_out0", em.val("sp_out0"))
        em.wait("sp", "sp_out1", em.val("sp_out1"))

    emit_phase_w()
    emit_phase_p()
    emit_phase_s()
    emit_phase_a()
    emit_phase_f()

    with nc.Block() as blk:
        em.run(blk)
    return nc


_NC_CACHE = {}


def _get_nc(key, cfg, debug=()):
    if key not in _NC_CACHE:
        _NC_CACHE[key] = build(cfg, debug=debug)
    return _NC_CACHE[key]


def kernel(L0_attn_hidden_states, L0_attn_qk_w, L0_attn_qk_b, L0_attn_v_w,
           L0_attn_v_b, L0_attn_hash_a, L0_ffn_w_gate, L0_ffn_w_noise,
           L0_ffn_W1, L0_ffn_b1, L0_ffn_W2, L0_ffn_b2):
    x = np.ascontiguousarray(np.asarray(L0_attn_hidden_states, dtype=np.float32))
    Bsz, L, D = x.shape
    cfg = CFG(L=L, NH=D // 64, HF=np.asarray(L0_ffn_W1).shape[2])
    nc = _get_nc(("full", L, D), cfg)

    com = {
        "qk_w": np.ascontiguousarray(L0_attn_qk_w, dtype=np.float32),
        "qk_b": np.ascontiguousarray(L0_attn_qk_b, dtype=np.float32),
        "v_w": np.ascontiguousarray(L0_attn_v_w, dtype=np.float32),
        "v_b": np.ascontiguousarray(L0_attn_v_b, dtype=np.float32),
        "hash_a": np.ascontiguousarray(L0_attn_hash_a, dtype=np.float32),
        "w1": np.ascontiguousarray(np.asarray(L0_ffn_W1)[1], dtype=np.float32),
        "b1": np.ascontiguousarray(np.asarray(L0_ffn_b1)[1], dtype=np.float32),
        "w2": np.ascontiguousarray(np.asarray(L0_ffn_W2)[1], dtype=np.float32),
        "b2": np.ascontiguousarray(np.asarray(L0_ffn_b2)[1], dtype=np.float32),
    }
    NQ = L // 2
    in_maps = []
    for c in range(8):
        b = c // 2
        if c % 2 == 0:
            xb = x[b]
        else:
            xb = np.concatenate([x[b, NQ:], x[b, :NQ]], axis=0)
        m = dict(com)
        m["x"] = np.ascontiguousarray(xb)
        in_maps.append(m)

    res = run_bass_kernel_spmd(nc, in_maps, core_ids=list(range(8)))
    out = np.empty((Bsz, L, D), dtype=np.float32)
    for c in range(8):
        b = c // 2
        r = res.results[c]["out"]
        if c % 2 == 0:
            out[b, :NQ] = r
        else:
            out[b, NQ:] = r
    return out


# version 09
# speedup vs baseline: 1.1487x; 1.0039x over previous
import numpy as np
import concourse.bass as bass
import concourse.mybir as mybir
from concourse.bass_utils import run_bass_kernel_spmd

F32 = mybir.dt.float32
BF16 = mybir.dt.bfloat16
AF = mybir.ActivationFunctionType
ALU = mybir.AluOpType
AX = mybir.AxisListType.X

CBIT = 4992.0
EXP_BIAS = -(CBIT + 5.0)
LN_EPS = 1e-5

ENGS = ("pe", "act", "ve", "sp", "gp")


class CFG:
    def __init__(self, L=8192, NH=16, HF=2048):
        self.L = L
        self.NH = NH
        self.D = NH * 64
        self.NQ = L // 2
        self.HF = HF
        self.DC = self.D // 128
        self.NJT = self.D // 128
        self.NKT = L // 128
        self.NQC = self.NQ // 512
        self.NTT = self.NQ // 128


class Emitter:
    def __init__(self, nc):
        self.nc = nc
        self.streams = {e: [] for e in ENGS}
        self.sems = {}
        self.cnt = {}

    def sem(self, name):
        if name not in self.sems:
            self.sems[name] = self.nc.alloc_semaphore(name)
            self.cnt[name] = 0
        return self.sems[name]

    def val(self, name):
        self.sem(name)
        return self.cnt[name]

    def op(self, eng, fn, inc=None, n=1):
        if inc is not None:
            h = self.sem(inc)
            self.cnt[inc] += n

            def wrapped(E, fn=fn, h=h, n=n):
                fn(E).then_inc(h, n)
            self.streams[eng].append(wrapped)
        else:
            self.streams[eng].append(fn)

    def wait(self, eng, name, value=None):
        if value is None:
            value = self.val(name)
        if value <= 0:
            return
        h = self.sem(name)
        self.streams[eng].append(lambda E, h=h, v=value: E.wait_ge(h, v))

    def run(self, block):
        for eng, meth in (("pe", block.tensor), ("act", block.scalar),
                          ("ve", block.vector), ("sp", block.sync),
                          ("gp", block.gpsimd)):
            fns = self.streams[eng]
            if fns:
                meth(lambda E, fns=fns: [f(E) for f in fns])
            self.streams[eng] = []


def build(cfg, debug=()):
    nc = bass.Bass("TRN2", target_bir_lowering=False)
    em = Emitter(nc)
    L, NH, D, NQ, HF = cfg.L, cfg.NH, cfg.D, cfg.NQ, cfg.HF
    DC, NJT, NKT, NQC, NTT = cfg.DC, cfg.NJT, cfg.NKT, cfg.NQC, cfg.NTT
    NHT = HF // 128
    NQUARTER = 4
    QL = L // NQUARTER
    QT_TILES = QL // 128
    QTC = QL // 512
    assert QTC >= 1 and NQC >= 1
    NHX = 128
    VHW = min(512, D)
    NVH = D // VHW
    BCW = min(512, D)

    x_d = nc.dram_tensor("x", [L, D], F32, kind="ExternalInput").ap()
    qkw_d = nc.dram_tensor("qk_w", [D, D], F32, kind="ExternalInput").ap()
    qkb_d = nc.dram_tensor("qk_b", [D], F32, kind="ExternalInput").ap()
    vw_d = nc.dram_tensor("v_w", [D, D], F32, kind="ExternalInput").ap()
    vb_d = nc.dram_tensor("v_b", [D], F32, kind="ExternalInput").ap()
    ha_d = nc.dram_tensor("hash_a", [66], F32, kind="ExternalInput").ap()
    w1_d = nc.dram_tensor("w1", [D, HF], F32, kind="ExternalInput").ap()
    b1_d = nc.dram_tensor("b1", [HF], F32, kind="ExternalInput").ap()
    w2_d = nc.dram_tensor("w2", [HF, D], F32, kind="ExternalInput").ap()
    b2_d = nc.dram_tensor("b2", [D], F32, kind="ExternalInput").ap()
    out_d = nc.dram_tensor("out", [NQ, D], F32, kind="ExternalOutput").ap()

    qkT_s = nc.dram_tensor("qkT_s", [D, L], BF16, kind="Internal").ap()
    xbf_s = nc.dram_tensor("xbf_s", [L, D], BF16, kind="Internal").ap()
    qkw_bf = nc.dram_tensor("qkw_bf", [D, D], BF16, kind="Internal").ap()
    vw_bf = nc.dram_tensor("vw_bf", [D, D], BF16, kind="Internal").ap()
    v_s = nc.dram_tensor("v_s", [L, D], BF16, kind="Internal").ap()
    ctx_s = nc.dram_tensor("ctx_s", [NH * NQ, 65], BF16, kind="Internal").ap()
    bits_s = nc.dram_tensor("bits_s", [NH, L + NQ], BF16, kind="Internal").ap()

    dbg = {}
    if "qkT" in debug:
        dbg["qkT"] = nc.dram_tensor("dbg_qkT", [D, L], F32, kind="ExternalOutput").ap()
    if "stats" in debug:
        dbg["dot1"] = nc.dram_tensor("dbg_dot1", [NH, L], F32, kind="ExternalOutput").ap()
        dbg["n2"] = nc.dram_tensor("dbg_n2", [NH, L], F32, kind="ExternalOutput").ap()
        dbg["bits"] = nc.dram_tensor("dbg_bits", [NH, L + NQ], F32, kind="ExternalOutput").ap()
    if "v" in debug:
        dbg["v"] = nc.dram_tensor("dbg_v", [L, D], F32, kind="ExternalOutput").ap()
    if "ctx" in debug:
        dbg["ctx"] = nc.dram_tensor("dbg_ctx", [NH * NQ, 65], F32, kind="ExternalOutput").ap()
    if "xn" in debug:
        dbg["xn"] = nc.dram_tensor("dbg_xn", [D, NQ], F32, kind="ExternalOutput").ap()

    guards = []

    def AL(name, shape, dtype, side="left"):
        g = nc.sbuf_tensor(name, shape, dtype, side=side)
        t = g.__enter__()
        guards.append((name, g))
        return t

    def FREE(names):
        for name in reversed(names):
            for i in range(len(guards) - 1, -1, -1):
                if guards[i][0] == name:
                    guards[i][1].__exit__(None, None, None)
                    guards.pop(i)
                    break

    ident = AL("ident", [128, 128], F32)
    expb = AL("expb", [128, 1], F32)
    lneps = AL("lneps", [128, 1], F32)
    st = AL("st", [NH, 16], F32)
    a23 = AL("a23", [NH, 2], F32)
    qkb_sb = AL("qkb_sb", [128, NJT], F32)
    b1_sb = AL("b1_sb", [128, NHT], F32)
    b2_sb = AL("b2_sb", [128, NJT], F32)

    dbufs = {}
    if "qkT" in debug:
        dbufs["dbq"] = AL("dbq", [128, L], F32)
    if "v" in debug:
        dbufs["dbv"] = AL("dbv", [128, D], F32)
    if "ctx" in debug:
        dbufs["dbc"] = AL("dbc", [128, NTT * 65], F32)

    dot1 = AL("dot1", [NH, L], F32)
    n2 = AL("n2", [NH, L], BF16)
    bits = AL("bits", [NH, L + NQ], BF16)

    qk_wT = AL("qk_wT", [128, DC * D], BF16, side="right")
    v_wT = AL("v_wT", [128, DC * D], BF16, side="right")
    vbb_sb = AL("vbb_sb", [128, D], BF16, side="right")
    a1sel = AL("a1sel", [128, NJT * NH], F32, side="right")
    n1sel = AL("n1sel", [128, NJT * NH], BF16, side="right")
    onesb = AL("onesb", [1, 128], BF16, side="right")
    vbrow = AL("vbrow", [1, D], BF16, side="right")
    onesf = AL("onesf", [1, NH], F32, side="right")

    psA = nc.alloc_psum_tensor("psA", [128, 1536], F32)
    psB = nc.alloc_psum_tensor("psB", [128, 1536], F32)
    psC = nc.alloc_psum_tensor("psC", [128, 512], F32)
    psT = nc.alloc_psum_tensor("psT", [128, 512], F32)


    def emit_phase_w():
        em.op("gp", lambda E: E.memset(ident[:, :], 0.0), inc="gp_i0")
        em.wait("gp", "gp_i0")
        em.op("gp", lambda E: E.affine_select(
            out=ident[:, :], in_=ident[:, :], compare_op=ALU.not_equal,
            fill=1.0, base=0, pattern=[[-1, 128]], channel_multiplier=1), inc="gp_init")
        em.op("sp", lambda E: E.dma_start(
            qkb_sb[:, 0:NJT], qkb_d.rearrange("(j p) -> p j", p=128),
            allow_slow_non_contiguous=True), inc="sp_w", n=16)
        em.op("sp", lambda E: E.dma_start(
            b1_sb[:, 0:NHT], b1_d.rearrange("(t p) -> p t", p=128),
            allow_slow_non_contiguous=True), inc="sp_w", n=16)
        em.op("sp", lambda E: E.dma_start(
            b2_sb[:, 0:NJT], b2_d.rearrange("(t p) -> p t", p=128),
            allow_slow_non_contiguous=True), inc="sp_w", n=16)
        em.op("ve", lambda E: E.memset(expb[:, :], EXP_BIAS), inc="ve_w")
        em.op("ve", lambda E: E.memset(lneps[:, :], LN_EPS), inc="ve_w")
        em.op("ve", lambda E: E.memset(a1sel[:, :], 0.0), inc="ve_w")
        em.op("ve", lambda E: E.memset(n1sel[:, :], 0.0), inc="ve_w")
        em.wait("ve", "ve_w", em.val("ve_w"))
        for jt in range(NJT):
            c0 = jt * NH + 2 * jt
            em.wait("sp", "ve_w", em.val("ve_w"))
            em.op("sp", lambda E, c0=c0: E.dma_start(
                a1sel[0:64, c0:c0 + 1],
                ha_d[0:64].rearrange("(p o) -> p o", o=1),
                allow_slow_non_contiguous=True), inc="sp_w", n=16)
            em.op("sp", lambda E, c0=c0: E.dma_start(
                a1sel[64:128, c0 + 1:c0 + 2],
                ha_d[0:64].rearrange("(p o) -> p o", o=1),
                allow_slow_non_contiguous=True), inc="sp_w", n=16)
            em.op("ve", lambda E, c0=c0: E.memset(n1sel[0:64, c0:c0 + 1], 1.0))
            em.op("ve", lambda E, c0=c0: E.memset(n1sel[64:128, c0 + 1:c0 + 2], 1.0),
                  inc="ve_w")
        em.op("ve", lambda E: E.memset(onesb[:, :], 1.0), inc="ve_w")
        em.op("sp", lambda E: E.dma_start(
            st[0:1, 0:2], ha_d[64:66].rearrange("(o c) -> o c", o=1)), inc="sp_w", n=16)
        em.op("gp", lambda E: E.dma_start(
            vbrow[0:1, :], vb_d.rearrange("(o c) -> o c", o=1)), inc="gp_w", n=16)
        em.wait("pe", "gp_init")
        em.wait("pe", "ve_w")
        em.wait("pe", "sp_w")
        em.wait("pe", "gp_w")
        em.op("ve", lambda E: E.memset(onesf[:, :], 1.0), inc="ve_w")
        em.wait("pe", "ve_w")
        em.op("pe", lambda E: E.matmul(psC[0:NH, 0:2], onesf[0:1, 0:NH],
                                       st[0:1, 0:2], start=True, stop=True), inc="pe_w")
        em.wait("ve", "pe_w", em.val("pe_w"))
        em.op("ve", lambda E: E.tensor_copy(a23[:, :], psC[0:NH, 0:2]), inc="ve_a23")
        for half in range(D // BCW):
            em.wait("pe", "ve_vbb", em.val("ve_vbb") if half > 0 else 0)
            em.wait("pe", "ve_a23", em.val("ve_a23"))
            em.op("pe", lambda E, h=half: E.matmul(
                psC[:, 0:BCW], onesb[0:1, 0:128],
                vbrow[0:1, h * BCW:(h + 1) * BCW], start=True, stop=True), inc="pe_w")
            em.wait("ve", "pe_w", em.val("pe_w"))
            em.op("ve", lambda E, h=half: E.tensor_copy(
                vbb_sb[:, h * BCW:(h + 1) * BCW], psC[:, 0:BCW]), inc="ve_vbb")
        em.op("gp", lambda E: E.dma_start(qkw_bf, qkw_d), inc="gp_wbf", n=16)
        em.op("gp", lambda E: E.dma_start(vw_bf, vw_d), inc="gp_wbf", n=16)
        for q in range(NQUARTER):
            em.op("gp", lambda E, q=q: E.dma_start(
                xbf_s[q * QL:(q + 1) * QL, :], x_d[q * QL:(q + 1) * QL, :]),
                inc="gp_xbf%d" % q, n=16)
        em.wait("sp", "gp_wbf", em.val("gp_wbf"))
        for wi, (src, dstT) in enumerate(((qkw_bf, qk_wT), (vw_bf, v_wT))):
            for dc in range(DC):
                em.op("sp", lambda E, s=src, d=dstT, dc=dc: E.dma_start(
                    d[:, dc * D:(dc + 1) * D], s[:, dc * 128:(dc + 1) * 128],
                    transpose=True), inc="sp_wt", n=16)

    xT = AL("xT", [128, 2 * DC * QL], BF16, side="right")
    qkf = AL("qkf", [128, 2 * 512], F32, side="right")
    sqb = AL("sqb", [128, 2 * 512], BF16, side="right")
    qstage = AL("qstage", [128, 2 * NJT * 512], BF16, side="right")
    vstage = AL("vstage", [128, 2 * D], BF16, side="right")

    def emit_phase_p():
        em.wait("act", "sp_w", em.val("sp_w"))
        em.wait("act", "ve_w", em.val("ve_w"))
        mm_slots = [psA[:, 1024:1536], psB[:, 0:512]]
        d1_slots = [psB[:, 512:1024], psB[:, 1024:1536]]
        n2_slots = [psC[:, 0:512], psT[:, 0:512]]
        em.wait("pe", "ve_vbb", em.val("ve_vbb"))
        def issue_xt(q):
            par = q % 2
            em.wait("sp", "gp_xbf%d" % q, em.val("gp_xbf%d" % q))
            for dc in range(DC):
                em.op("sp", lambda E, q=q, dc=dc, par=par: E.dma_start(
                    xT[:, par * DC * QL + dc * QL: par * DC * QL + (dc + 1) * QL],
                    xbf_s[q * QL:(q + 1) * QL, dc * 128:(dc + 1) * 128],
                    transpose=True), inc="sp_xt%d" % par, n=16)

        issue_xt(0)
        for q in range(NQUARTER):
            par = q % 2
            xTq = xT[:, par * DC * QL:(par + 1) * DC * QL]
            if q + 1 < NQUARTER:
                em.wait("sp", "pe_qk", em.val("pe_qk"))
                em.wait("sp", "pe_v", em.val("pe_v"))
                issue_xt(q + 1)
            em.wait("pe", "sp_xt%d" % par, em.val("sp_xt%d" % par))
            em.wait("pe", "sp_wt", em.val("sp_wt"))
            em.wait("pe", "ve_v", em.val("ve_v"))
            for tc in range(QTC):
                gi = q * QTC + tc
                d1b = d1_slots[gi % 2]
                n2b = n2_slots[gi % 2]
                def stats_mms(jt, d1b, n2b, qf, sq):
                    em.wait("pe", "act_qk", em.val("act_qk"))
                    em.op("pe", lambda E, d1b=d1b, jt=jt, qf=qf: E.matmul(
                        d1b[0:NH, :], a1sel[:, jt * NH:jt * NH + NH], qf,
                        start=(jt == 0), stop=(jt == NJT - 1)), inc="pe_d1")
                    em.wait("pe", "ve_sq", em.val("ve_sq"))
                    em.op("pe", lambda E, n2b=n2b, jt=jt, sq=sq: E.matmul(
                        n2b[0:NH, :], n1sel[:, jt * NH:jt * NH + NH], sq,
                        start=(jt == 0), stop=(jt == NJT - 1)), inc="pe_n2")

                pending = None
                for jt in range(NJT):
                    mi = gi * NJT + jt
                    mmb = mm_slots[mi % 2]
                    if mi >= 2:
                        em.wait("pe", "act_qk", em.val("act_qk") - 1)
                    if jt == 0 and gi >= 2:
                        em.wait("pe", "ve_d1", em.val("ve_d1") - 1)
                        em.wait("pe", "ve_n2", em.val("ve_n2") - 1)
                    for dc in range(DC):
                        em.op("pe", lambda E, m=mmb, dc=dc, jt=jt, tc=tc, xTq=xTq: E.matmul(
                            m, qk_wT[:, (dc * NJT + jt) * 128:(dc * NJT + jt) * 128 + 128],
                            xTq[:, dc * QL + tc * 512: dc * QL + tc * 512 + 512],
                            start=(dc == 0), stop=(dc == DC - 1)),
                            inc=("pe_qk" if dc == DC - 1 else None))
                    if pending is not None:
                        stats_mms(*pending)
                    qf = qkf[:, (mi % 2) * 512:(mi % 2) * 512 + 512]
                    sq = sqb[:, (mi % 2) * 512:(mi % 2) * 512 + 512]
                    em.wait("act", "pe_qk", em.val("pe_qk"))
                    if mi >= 2:
                        em.wait("act", "pe_d1", em.val("pe_d1"))
                    em.op("act", lambda E, m=mmb, qf=qf, jt=jt: E.activation(
                        qf, m, AF.Identity, bias=qkb_sb[:, jt:jt + 1], scale=1.0),
                        inc="act_qk")
                    em.wait("ve", "act_qk", em.val("act_qk"))
                    if mi >= 2:
                        em.wait("ve", "pe_n2", em.val("pe_n2"))
                    em.op("ve", lambda E, sq=sq, qf=qf: E.tensor_mul(sq, qf, qf),
                          inc="ve_sq")
                    stg = qstage[:, (gi % 2) * NJT * 512 + jt * 512:
                                 (gi % 2) * NJT * 512 + jt * 512 + 512]
                    if gi >= 2 and jt == 0:
                        em.wait("ve", "gp_qs%d" % (gi % 2), em.val("gp_qs%d" % (gi % 2)))
                    em.op("ve", lambda E, stg=stg, qf=qf: E.tensor_copy(stg, qf),
                          inc="ve_qs")
                    pending = (jt, d1b, n2b, qf, sq)
                stats_mms(*pending)
                em.wait("ve", "pe_d1", em.val("pe_d1"))
                em.wait("ve", "pe_n2", em.val("pe_n2"))
                cs = q * QL + tc * 512
                em.op("ve", lambda E, d1b=d1b, cs=cs: E.tensor_copy(
                    dot1[:, cs:cs + 512], d1b[0:NH, :]), inc="ve_d1")
                em.op("ve", lambda E, n2b=n2b, cs=cs: E.tensor_copy(
                    n2[:, cs:cs + 512], n2b[0:NH, :]), inc="ve_n2")
                em.wait("gp", "ve_qs", em.val("ve_qs"))
                src = qstage[:, (gi % 2) * NJT * 512:(gi % 2) * NJT * 512 + NJT * 512]
                dst = qkT_s.rearrange("(jt p) Lx -> p jt Lx", p=128)[:, :, cs:cs + 512]
                em.op("gp", lambda E, s=src, d=dst: E.dma_start(
                    d, s.rearrange("p (jt c) -> p jt c", c=512)),
                    inc="gp_qs%d" % (gi % 2), n=16)
            em.wait("pe", "act_qk", em.val("act_qk"))
            for t in range(QT_TILES):
                gt = q * QT_TILES + t
                vs = vstage[:, (gt % 2) * D:(gt % 2) * D + D]
                if gt >= 2:
                    em.wait("ve", "gp_vs%d" % (gt % 2), em.val("gp_vs%d" % (gt % 2)))
                for half in range(NVH):
                    vi = gt * NVH + half
                    mmb = mm_slots[vi % 2]
                    em.wait("pe", "ve_v", max(0, em.val("ve_v") - 1))
                    for dc in range(DC):
                        em.op("pe", lambda E, m=mmb, dc=dc, t=t, h=half, xTq=xTq: E.matmul(
                            m[:, 0:VHW],
                            xTq[:, dc * QL + t * 128: dc * QL + t * 128 + 128],
                            v_wT[:, dc * D + h * VHW: dc * D + h * VHW + VHW],
                            start=(dc == 0), stop=(dc == DC - 1)),
                            inc=("pe_v" if dc == DC - 1 else None))
                    em.wait("ve", "pe_v", em.val("pe_v"))
                    em.op("ve", lambda E, m=mmb, vs=vs, h=half: E.tensor_tensor(
                        vs[:, h * VHW:h * VHW + VHW], m[:, 0:VHW],
                        vbb_sb[:, h * VHW:h * VHW + VHW], ALU.add), inc="ve_v")
                em.wait("gp", "ve_v", em.val("ve_v"))
                em.op("gp", lambda E, vs=vs, q=q, t=t: E.dma_start(
                    v_s[q * QL + t * 128: q * QL + t * 128 + 128, :], vs),
                    inc="gp_vs%d" % (gt % 2), n=16)
        if "qkT" in debug:
            db = dbufs["dbq"]
            for jt in range(NJT):
                em.wait("gp", "gp_qs0", em.val("gp_qs0"))
                em.wait("gp", "gp_qs1", em.val("gp_qs1"))
                em.wait("gp", "sp_dbg", em.val("sp_dbg"))
                em.op("gp", lambda E, jt=jt: E.dma_start(
                    db[:, :], qkT_s[jt * 128:(jt + 1) * 128, :]), inc="sp_dbg", n=16)
                em.wait("gp", "sp_dbg")
                em.op("gp", lambda E, jt=jt: E.dma_start(
                    dbg["qkT"][jt * 128:(jt + 1) * 128, :], db[:, :]), inc="sp_dbg", n=16)
        if "v" in debug:
            dbv = dbufs["dbv"]
            for t in range(L // 128):
                em.wait("gp", "gp_vs0", em.val("gp_vs0"))
                em.wait("gp", "gp_vs1", em.val("gp_vs1"))
                em.wait("gp", "sp_dbgv", em.val("sp_dbgv"))
                em.op("gp", lambda E, t=t: E.dma_start(
                    dbv[:, :], v_s[t * 128:(t + 1) * 128, :]), inc="sp_dbgv", n=16)
                em.wait("gp", "sp_dbgv")
                em.op("gp", lambda E, t=t: E.dma_start(
                    dbg["v"][t * 128:(t + 1) * 128, :], dbv[:, :]), inc="sp_dbgv", n=16)

    FREE(["xT", "qkf", "sqb", "qstage", "vstage"])
    FREE(["onesf"])
    FREE(["qk_wT", "v_wT", "vbb_sb", "a1sel", "n1sel", "onesb", "vbrow"])
    hkv = AL("hkv", [NH, L], BF16)

    def emit_phase_s():
        V = "ve"
        for nm in ("pe_qk", "pe_v", "pe_d1", "pe_n2", "gp_qs0", "gp_qs1",
                   "gp_vs0", "gp_vs1", "act_qk", "ve_qs", "ve_v", "ve_sq"):
            em.wait(V, nm, em.val(nm))
        em.wait(V, "ve_a23", em.val("ve_a23"))
        c = lambda i: st[:, i:i + 1]
        em.wait(V, "ve_s", em.val("ve_s"))
        em.op(V, lambda E: E.reduce_max(c(0), n2[:, :], axis=AX), inc="ve_s")
        em.wait("act", "ve_s", em.val("ve_s"))
        em.op("act", lambda E: E.activation(c(1), c(0), AF.Sqrt), inc="act_s")
        em.wait(V, "act_s", em.val("act_s"))
        em.wait(V, "ve_s", em.val("ve_s"))
        em.op(V, lambda E: E.tensor_single_scalar(c(2), c(1), 1e-12, ALU.max), inc="ve_s")
        em.wait(V, "ve_s", em.val("ve_s"))
        em.op(V, lambda E: E.reciprocal(c(3), c(2)), inc="ve_s")
        em.wait(V, "ve_s", em.val("ve_s"))
        em.op(V, lambda E: E.tensor_single_scalar(c(2), c(3), 0.75, ALU.mult), inc="ve_s")
        em.wait(V, "ve_s", em.val("ve_s"))
        em.op(V, lambda E: E.tensor_mul(c(4), c(2), c(2)), inc="ve_s")
        em.wait(V, "ve_s", em.val("ve_s"))
        em.op(V, lambda E: E.tensor_copy(bits[:, 0:L], dot1[:, :]), inc="ve_s")
        em.wait(V, "ve_s", em.val("ve_s"))
        em.op(V, lambda E: E.tensor_scalar_mul(hkv[:, :], n2[:, :], c(4)), inc="ve_s")
        em.wait(V, "ve_s", em.val("ve_s"))
        em.op(V, lambda E: E.tensor_scalar(n2[:, :], hkv[:, :], -1.0, 0.5,
                                           ALU.mult, ALU.add), inc="ve_s")
        em.wait(V, "ve_s", em.val("ve_s"))
        em.op(V, lambda E: E.tensor_mul(hkv[:, :], hkv[:, :], hkv[:, :]), inc="ve_s")
        em.wait(V, "ve_s", em.val("ve_s"))
        em.op(V, lambda E: E.tensor_scalar(hkv[:, :], hkv[:, :], -1.0, 0.5,
                                           ALU.mult, ALU.add), inc="ve_s")
        em.wait(V, "ve_s", em.val("ve_s"))
        em.op(V, lambda E: E.tensor_scalar_mul(hkv[:, :], hkv[:, :], a23[:, 1:2]), inc="ve_s")
        em.wait(V, "ve_s", em.val("ve_s"))
        em.op(V, lambda E: E.scalar_tensor_tensor(
            hkv[:, :], n2[:, :], a23[:, 0:1], hkv[:, :], ALU.mult, ALU.add), inc="ve_s")
        em.wait(V, "ve_s", em.val("ve_s"))
        em.op(V, lambda E: E.scalar_tensor_tensor(
            hkv[:, :], bits[:, 0:L], c(2), hkv[:, :], ALU.mult, ALU.add), inc="ve_s")
        em.wait(V, "ve_s", em.val("ve_s"))
        em.op(V, lambda E: E.tensor_single_scalar(hkv[:, :], hkv[:, :], 0.0, ALU.is_ge),
              inc="ve_s")
        em.wait(V, "ve_s", em.val("ve_s"))
        em.op(V, lambda E: E.tensor_scalar(bits[:, 0:L], hkv[:, :], 128.0, -64.0,
                                           ALU.mult, ALU.add), inc="ve_s")
        em.wait(V, "ve_s", em.val("ve_s"))
        em.op(V, lambda E: E.reduce_max(c(5), hkv[:, :], axis=AX), inc="ve_s")
        em.wait(V, "ve_s", em.val("ve_s"))
        em.op(V, lambda E: E.tensor_reduce(c(6), hkv[:, :], axis=AX, op=ALU.min), inc="ve_s")
        em.wait(V, "ve_s", em.val("ve_s"))
        em.op(V, lambda E: E.tensor_sub(c(7), c(5), c(6)), inc="ve_s")
        em.wait(V, "ve_s", em.val("ve_s"))
        em.op(V, lambda E: E.tensor_add(c(8), c(5), c(6)), inc="ve_s")
        em.wait(V, "ve_s", em.val("ve_s"))
        em.op(V, lambda E: E.tensor_single_scalar(c(8), c(8), 1.0, ALU.subtract),
              inc="ve_s")
        em.wait(V, "ve_s", em.val("ve_s"))
        em.op(V, lambda E: E.tensor_single_scalar(c(9), c(7), 156.0, ALU.mult), inc="ve_s")
        em.wait(V, "ve_s", em.val("ve_s"))
        em.op(V, lambda E: E.tensor_sub(c(10), c(8), c(7)), inc="ve_s")
        em.wait(V, "ve_s", em.val("ve_s"))
        em.op(V, lambda E: E.tensor_single_scalar(c(10), c(10), 78.0, ALU.mult), inc="ve_s")
        em.wait(V, "ve_s", em.val("ve_s"))
        em.op(V, lambda E: E.tensor_single_scalar(dot1[:, 0:NQ], dot1[:, 0:NQ], 0.0,
                                                  ALU.is_ge), inc="ve_s")
        em.wait(V, "ve_s", em.val("ve_s"))
        em.op(V, lambda E: E.tensor_scalar(bits[:, L:L + NQ], dot1[:, 0:NQ], c(9), c(10),
                                           ALU.mult, ALU.add), inc="ve_s")
        em.wait("gp", "ve_s", em.val("ve_s"))
        em.op("gp", lambda E: E.dma_start(bits_s, bits[:, :]), inc="gp_bits", n=16)
        if "stats" in debug:
            em.wait("gp", "ve_s", em.val("ve_s"))
            em.op("gp", lambda E: E.dma_start(dbg["dot1"], dot1[:, :]), inc="sp_dbgs", n=16)
            em.op("gp", lambda E: E.dma_start(dbg["n2"], n2[:, :]), inc="sp_dbgs", n=16)
            em.op("gp", lambda E: E.dma_start(dbg["bits"], bits[:, :]), inc="sp_dbgs", n=16)

    FREE(["hkv"])
    FREE(["dot1", "n2", "bits"])
    w1_sb = AL("w1_sb", [128, DC * HF], BF16)
    w2_sb = AL("w2_sb", [128, NHT * D], BF16)
    KT = AL("KT", [65, 2 * L], BF16)
    QTb = AL("QTb", [65, 2 * NQ], BF16)
    vhb = AL("vhb", [128, 2 * NKT * 65], BF16)
    ctxT = AL("ctxT", [65, NQ], F32)
    pt4 = AL("pt4", [128, 4 * 1536], BF16)
    cstage = AL("cstage", [128, NTT * 65], BF16)

    groups = []
    k = 0
    while k < NKT:
        size = min(3, NKT - k)
        groups.append((k, size))
        k += size

    ctx_w = ctx_s.rearrange("(h t p) j -> h p t j", p=128, t=NTT)

    def emit_phase_a():
        em.wait("pe", "ve_d1", em.val("ve_d1"))
        em.wait("pe", "ve_n2", em.val("ve_n2"))
        em.wait("pe", "ve_v", em.val("ve_v"))
        em.wait("pe", "act_qk", em.val("act_qk"))
        em.wait("pe", "evac_xp", em.val("evac_xp"))

        em.wait("act", "ve_w", em.val("ve_w"))
        state = {"exp_at_bank": {0: 0, 1: 0}, "ctx_evac": 0,
                 "av_done": {0: 0, 1: 0, 2: 0, 3: 0}}

        def prefetch(h, into):
            kth = KT[:, into * L: into * L + L]
            em.op("sp", lambda E, h=h, kth=kth: E.dma_start(
                kth[0:64, :], qkT_s[h * 64:(h + 1) * 64, :]),
                inc="sp_kt%d" % into, n=16)
            em.op("gp", lambda E, h=h, kth=kth: E.dma_start(
                kth[64:65, :], bits_s[h:h + 1, 0:L]), inc="gp_kt%d" % into, n=16)
            vb = vhb[:, into * NKT * 65: into * NKT * 65 + NKT * 65]
            dstv = vb.rearrange("p (k j) -> p k j", j=65)[:, :, 0:64]
            srcv = v_s[:, h * 64:(h + 1) * 64].rearrange("(k p) j -> p k j", p=128)
            KH = NKT // 2
            em.op("sp", lambda E, d=dstv, s=srcv, KH=KH: E.dma_start(
                d[:, 0:KH, :], s[:, 0:KH, :]), inc="sp_vh%d" % into, n=16)
            em.op("gp", lambda E, d=dstv, s=srcv, KH=KH: E.dma_start(
                d[:, KH:NKT, :], s[:, KH:NKT, :]), inc="gp_vh%d" % into, n=16)

        em.wait("ve", "sp_dbgs", em.val("sp_dbgs"))
        em.wait("ve", "gp_bits", em.val("gp_bits"))
        em.op("ve", lambda E: E.memset(vhb[:, :], 1.0), inc="ve_ones")
        em.wait("gp", "ve_ones", em.val("ve_ones"))
        em.wait("sp", "ve_ones", em.val("ve_ones"))
        em.wait("gp", "gp_bits", em.val("gp_bits"))
        em.wait("sp", "ve_s", em.val("ve_s"))
        em.wait("sp", "gp_bits", em.val("gp_bits"))
        em.wait("sp", "sp_dbgs", em.val("sp_dbgs"))
        em.wait("gp", "sp_dbgs", em.val("sp_dbgs"))
        em.wait("sp", "gp_qs0", em.val("gp_qs0"))
        em.wait("sp", "gp_qs1", em.val("gp_qs1"))
        em.wait("gp", "gp_vs0", em.val("gp_vs0"))
        em.wait("gp", "gp_vs1", em.val("gp_vs1"))
        prefetch(0, 0)
        prefetch(1, 1)

        ct_at_qc = [0] * NQC

        def build_qt(h):
            par = h % 2
            kth = KT[:, par * L: par * L + L]
            qtb = QTb[:, par * NQ: par * NQ + NQ]
            em.wait("ve", "sp_kt%d" % par, em.val("sp_kt%d" % par))
            em.wait("ve", "gp_kt%d" % par, em.val("gp_kt%d" % par))
            em.wait("ve", "pe_s", em.val("pe_s"))
            em.op("ve", lambda E, kth=kth, q=qtb: E.tensor_scalar_mul(
                q[0:64, :], kth[0:64, 0:NQ], 0.125), inc="ve_qt")
            em.wait("gp", "pe_s", em.val("pe_s"))
            em.op("gp", lambda E, h=h, q=qtb: E.dma_start(
                q[64:65, :], bits_s[h:h + 1, L:L + NQ]), inc="gp_qt%d" % par, n=16)

        build_qt(0)
        for h in range(NH):
            par = h % 2
            kth = KT[:, par * L: par * L + L]
            vb = vhb[:, par * NKT * 65: par * NKT * 65 + NKT * 65]
            qtb = QTb[:, par * NQ: par * NQ + NQ]
            if h + 1 < NH:
                build_qt(h + 1)
            em.wait("pe", "ve_qt", em.val("ve_qt") - (1 if h + 1 < NH else 0))
            em.wait("pe", "gp_qt%d" % par, em.val("gp_qt%d" % par))
            em.wait("pe", "gp_vh%d" % par, em.val("gp_vh%d" % par))
            em.wait("pe", "sp_vh%d" % par, em.val("sp_vh%d" % par))
            em.wait("pe", "sp_kt%d" % par, em.val("sp_kt%d" % par))
            em.wait("pe", "gp_kt%d" % par, em.val("gp_kt%d" % par))

            evac_at_qc = [0] * NQC

            def epilogue_block(b, h):
                em.wait("pe", "ve_ctx", evac_at_qc[b])
                em.wait("pe", "ve_cs", em.val("ve_cs"))
                for j in range(4):
                    tt = b * 4 + j
                    em.op("pe", lambda E, j=j, tt=tt: E.transpose(
                        psT[:, j * 80:j * 80 + 65], ctxT[:, tt * 128:tt * 128 + 128],
                        ident[0:65, 0:65]), inc=("pe_ct" if j == 3 else None))
                ct_at_qc[b] = em.val("pe_ct")
                em.wait("ve", "pe_ct", em.val("pe_ct"))
                if b == 0:
                    em.wait("ve", "gp_cs", em.val("gp_cs"))
                em.op("ve", lambda E, b=b: E.tensor_copy(
                    cstage[:, b * 4 * 65:(b + 1) * 4 * 65].rearrange(
                        "p (t j) -> p t j", j=65),
                    psT[:, 0:320].rearrange("p (t j) -> p t j", j=80)[:, :, 0:65]),
                    inc="ve_cs")

            pending_tail = [None]

            def av_group(qc_ctx, j):
                exp_at_g, vbx = qc_ctx
                k0, size = groups[j]
                ptb = pt4[:, (j % 4) * 1536:(j % 4) * 1536 + 1536]
                em.wait("pe", "act_e", exp_at_g[j])
                if k0 == 0:
                    em.wait("pe", "ve_ctx", state["ctx_evac"])
                for j2 in range(size):
                    kt = k0 + j2
                    em.op("pe", lambda E, v=vbx, kt=kt, p=ptb, j2=j2: E.matmul(
                        psC[0:65, :], v[:, kt * 65:kt * 65 + 65],
                        p[:, j2 * 512:j2 * 512 + 512],
                        start=(kt == 0), stop=(kt == NKT - 1),
                        skip_group_check=True),
                        inc=("pe_avg" if j2 == size - 1 else None))
                state["av_done"][j % 4] = em.val("pe_avg")

            def flush_tail(stage):
                if pending_tail[0] is None:
                    return
                qc_ctx, tqc = pending_tail[0]
                if stage == 0:
                    av_group(qc_ctx, len(groups) - 2)
                    return
                pending_tail[0] = None
                av_group(qc_ctx, len(groups) - 1)
                em.wait("ve", "pe_avg", em.val("pe_avg"))
                em.wait("ve", "pe_ct", ct_at_qc[tqc])
                em.op("ve", lambda E, tqc=tqc: E.tensor_copy(
                    ctxT[:, tqc * 512:tqc * 512 + 512], psC[0:65, :]), inc="ve_ctx")
                state["ctx_evac"] = em.val("ve_ctx")
                evac_at_qc[tqc] = em.val("ve_ctx")

            for qc in range(NQC):
                exp_at_g = []
                qc_ctx = (exp_at_g, vb)
                for g, (k0, size) in enumerate(groups):
                    bank = psA if g % 2 == 0 else psB
                    bpar = g % 2
                    ptb = pt4[:, (g % 4) * 1536:(g % 4) * 1536 + 1536]
                    em.wait("pe", "act_e", state["exp_at_bank"][bpar])
                    for j in range(size):
                        kt = k0 + j
                        em.op("pe", lambda E, b=bank, j=j, kt=kt, kth=kth, qc=qc,
                              qtb=qtb: E.matmul(
                            b[:, j * 512:j * 512 + 512],
                            kth[:, kt * 128:kt * 128 + 128],
                            qtb[:, qc * 512:qc * 512 + 512],
                            start=True, stop=True),
                            inc=("pe_s" if j == size - 1 else None))
                    if g == 0:
                        flush_tail(0)
                    if g == 1:
                        flush_tail(1)
                    em.wait("act", "pe_avg", state["av_done"][g % 4])
                    em.wait("act", "pe_s", em.val("pe_s"))
                    em.op("act", lambda E, b=bank, p=ptb, w=size * 512: E.activation(
                        p[:, 0:w], b[:, 0:w], AF.Exp, bias=expb[:, 0:1], scale=1.0),
                        inc="act_e")
                    state["exp_at_bank"][bpar] = em.val("act_e")
                    exp_at_g.append(em.val("act_e"))
                    if g >= 2:
                        av_group(qc_ctx, g - 2)
                    if g == 1 and qc >= 1:
                        epilogue_block(qc - 1, h)
                pending_tail[0] = (qc_ctx, qc)
            flush_tail(0)
            flush_tail(1)
            epilogue_block(NQC - 1, h)
            em.wait("gp", "ve_cs", em.val("ve_cs"))
            em.op("gp", lambda E, h=h: E.dma_start(
                ctx_w[h], cstage[:, :].rearrange("p (t j) -> p t j", j=65)),
                inc="gp_cs", n=16)
            if h + 2 < NH:
                em.wait("sp", "pe_s", em.val("pe_s"))
                em.wait("gp", "pe_s", em.val("pe_s"))
                em.wait("gp", "pe_avg", em.val("pe_avg"))
                em.wait("sp", "pe_avg", em.val("pe_avg"))
                prefetch(h + 2, par)
            if h == 1:
                em.op("gp", lambda E: E.dma_start(
                    w1_sb[:, :].rearrange("p (dc h) -> p dc h", h=HF),
                    w1_d.rearrange("(dc p) h -> p dc h", p=128)), inc="gp_wffn", n=16)
                em.op("gp", lambda E: E.dma_start(
                    w2_sb[:, :].rearrange("p (ht d) -> p ht d", d=D),
                    w2_d.rearrange("(ht p) d -> p ht d", p=128)), inc="gp_wffn", n=16)
        if "ctx" in debug:
            dbc = dbufs["dbc"]
            dbg_w = dbg["ctx"].rearrange("(h t p) j -> h p t j", p=128, t=NTT)
            for h in range(NH):
                em.wait("gp", "gp_cs", em.val("gp_cs"))
                em.wait("gp", "sp_dbgc", em.val("sp_dbgc"))
                em.op("gp", lambda E, h=h: E.dma_start(
                    dbc[:, :].rearrange("p (t j) -> p t j", j=65), ctx_w[h]),
                    inc="sp_dbgc", n=16)
                em.wait("gp", "sp_dbgc")
                em.op("gp", lambda E, h=h: E.dma_start(
                    dbg_w[h], dbc[:, :].rearrange("p (t j) -> p t j", j=65)),
                    inc="sp_dbgc", n=16)

    FREE(["KT", "QTb", "vhb", "ctxT", "pt4", "cstage"])
    xres = AL("xres", [128, 2 * D], F32)
    cres = AL("cres", [128, 2 * NH * 65], BF16)
    cres2 = AL("cres2", [128, 2 * D], BF16)
    rcpt = AL("rcpt", [128, 2 * NH], F32)
    zbuf = AL("zbuf", [128, 4 * D], F32)
    zzbuf = AL("zzbuf", [128, 2 * D], F32)
    lns = AL("lns", [128, 40], F32)
    xnT = AL("xnT", [128, DC * NQ], BF16)
    hT = AL("hT", [128, NHT * 512], BF16)
    z2T = AL("z2T", [128, 2 * 512], F32)
    z2n = AL("z2n", [128, 4 * D], BF16)
    ostage = AL("ostage", [128, 2 * D], F32)

    ctx_r = ctx_s.rearrange("(h t p) j -> t p h j", p=128, t=NTT)

    def _ln_cs(zios, colbase):
        return [(lns[:, colbase + 4 * i:colbase + 4 * i + 1],
                 lns[:, colbase + 4 * i + 1:colbase + 4 * i + 2],
                 lns[:, colbase + 4 * i + 2:colbase + 4 * i + 3])
                for i in range(len(zios))]

    def emit_ln_stats(zios, colbase):
        W = lambda: em.wait("ve", "ve_ln", em.val("ve_ln"))
        cs = _ln_cs(zios, colbase)
        W()
        for i, (zin, _) in enumerate(zios):
            em.op("ve", lambda E, z=zin, c=cs[i]: E.reduce_sum(c[0], z, axis=AX),
                  inc="ve_ln")
        for i, (zin, _) in enumerate(zios):
            if i == 2:
                W()
            em.op("ve", lambda E, z=zin, c=cs[i], i=i: E.scalar_tensor_tensor(
                zzbuf[:, (i % 2) * D:(i % 2) * D + D], z, 1.0, z, ALU.mult, ALU.mult,
                accum_out=c[1]), inc="ve_ln")
        W()
        for i in range(len(zios)):
            em.op("ve", lambda E, c=cs[i]: E.tensor_single_scalar(
                c[0], c[0], 1.0 / D, ALU.mult), inc="ve_ln")
            em.op("ve", lambda E, c=cs[i]: E.tensor_single_scalar(
                c[1], c[1], 1.0 / D, ALU.mult), inc="ve_ln")
        W()
        for i in range(len(zios)):
            em.op("ve", lambda E, c=cs[i]: E.scalar_tensor_tensor(
                c[2], c[0], -1.0, c[0], ALU.mult, ALU.mult), inc="ve_ln")
        W()
        for i in range(len(zios)):
            em.op("ve", lambda E, c=cs[i]: E.tensor_add(c[1], c[1], c[2]), inc="ve_ln")

    def emit_ln_finish(zios, colbase):
        W = lambda: em.wait("ve", "ve_ln", em.val("ve_ln"))
        cs = _ln_cs(zios, colbase)
        em.wait("act", "ve_ln", em.val("ve_ln"))
        for i in range(len(zios)):
            em.op("act", lambda E, c=cs[i]: E.activation(
                c[2], c[1], AF.Sqrt, bias=lneps[0:NHX, 0:1]), inc="act_ln")
        em.wait("ve", "act_ln", em.val("act_ln"))
        for i in range(len(zios)):
            em.op("ve", lambda E, c=cs[i]: E.reciprocal(c[1], c[2]), inc="ve_ln")
        W()
        for i, (zin, outbuf) in enumerate(zios):
            if outbuf is not None:
                em.op("ve", lambda E, z=zin, o=outbuf, c=cs[i]: E.tensor_scalar(
                    o, z, c[0], c[1], ALU.subtract, ALU.mult), inc="ve_ln")

    def emit_ln_batch(zios, colbase):
        emit_ln_stats(zios, colbase)
        emit_ln_finish(zios, colbase)

    def emit_phase_f():
        em.wait("act", "ve_w", em.val("ve_w"))
        em.wait("act", "sp_w", em.val("sp_w"))
        em.wait("pe", "act_e", em.val("act_e"))
        for eng in ("sp", "gp", "ve", "act"):
            em.wait(eng, "pe_s", em.val("pe_s"))
            em.wait(eng, "pe_av", em.val("pe_av"))
            em.wait(eng, "pe_ct", em.val("pe_ct"))
            em.wait(eng, "ve_cs", em.val("ve_cs"))
            em.wait(eng, "ve_ct2", em.val("ve_ct2"))
            for nm in ("sp_kt0", "sp_kt1", "gp_kt0", "gp_kt1", "gp_vh0", "gp_vh1",
                       "gp_qt", "gp_cs"):
                em.wait(eng, nm, em.val(nm))
        em.wait("pe", "gp_wffn", em.val("gp_wffn"))
        xn_slots = [psA[:, 0:128], psA[:, 512:640], psB[:, 1024:1152], psC[:, 0:128]]
        for t0 in (0, 1):
            em.op("sp", lambda E, t=t0: E.dma_start(
                xres[:, (t % 2) * D:(t % 2) * D + D],
                x_d[t * 128:t * 128 + 128, :]), inc="sp_xr%d" % (t0 % 2), n=16)
            em.op("gp", lambda E, t=t0: E.dma_start(
                cres[:, (t % 2) * NH * 65:(t % 2) * NH * 65 + NH * 65].rearrange(
                    "p (h j) -> p h j", j=65), ctx_r[t]), inc="gp_cr%d" % (t0 % 2), n=16)

        def ln1_ve(t):
                xb = xres[:, (t % 2) * D:(t % 2) * D + D]
                cb = cres[:, (t % 2) * NH * 65:(t % 2) * NH * 65 + NH * 65]
                c2 = cres2[:, (t % 2) * D:(t % 2) * D + D]
                rc = rcpt[:, (t % 2) * NH:(t % 2) * NH + NH]
                if t >= 2:
                    em.wait("sp", "ve_z", em.val("ve_z") - 1)
                    em.op("sp", lambda E, xb=xb, t=t: E.dma_start(
                        xb, x_d[t * 128:t * 128 + 128, :]),
                        inc="sp_xr%d" % (t % 2), n=16)
                    em.wait("gp", "ve_rcp", em.val("ve_rcp") - 1)
                    em.op("gp", lambda E, cb=cb, t=t: E.dma_start(
                        cb.rearrange("p (h j) -> p h j", j=65), ctx_r[t]),
                        inc="gp_cr%d" % (t % 2), n=16)
                em.wait("ve", "sp_xr%d" % (t % 2), em.val("sp_xr%d" % (t % 2)))
                em.wait("ve", "gp_cr%d" % (t % 2), em.val("gp_cr%d" % (t % 2)))
                cbh = cb.rearrange("p (h j) -> p h j", j=65)
                if t >= 2:
                    em.wait("ve", "ve_z", em.val("ve_z") - 1)
                em.op("ve", lambda E, cbh=cbh, rc=rc: E.reciprocal(
                    rc.rearrange("p (h o) -> p h o", o=1), cbh[:, :, 64:65]),
                    inc="ve_rcp")
                em.wait("ve", "ve_rcp", em.val("ve_rcp"))
                rc_b = bass.AP(rcpt, (t % 2) * NH, [[2 * NH, 128], [1, NH], [0, 64]])
                c2h = c2.rearrange("p (h j) -> p h j", j=64)
                em.op("ve", lambda E, c2h=c2h, cbh=cbh, rb=rc_b: E.tensor_mul(
                    c2h, cbh[:, :, 0:64], rb), inc="ve_rcp")
                em.wait("ve", "ve_rcp", em.val("ve_rcp"))
                zo = zbuf[:, (t % 4) * D:(t % 4) * D + D]
                em.op("ve", lambda E, xb=xb, c2=c2, zo=zo: E.scalar_tensor_tensor(
                    zo, xb, 1.0, c2, ALU.mult, ALU.add), inc="ve_z")

        def ln1_pe(t):
                xo = zbuf[:, (t % 4) * D:(t % 4) * D + D]
                em.wait("pe", "ve_ln", lnval_at[t])
                for dc in range(DC):
                    k = t * DC + dc
                    sl = xn_slots[k % 4]
                    ecnt = "evac_xnv" if k % 2 == 0 else "evac_xna"
                    if k >= 4:
                        em.wait("pe", ecnt, em.val(ecnt) - 1)
                    em.op("pe", lambda E, s=sl, dc=dc, xo=xo: E.transpose(
                        s, xo[:, dc * 128:dc * 128 + 128], ident[:, :]), inc="pe_xn")
                    eng = "ve" if dc % 2 == 0 else "act"
                    em.wait(eng, "pe_xn", em.val("pe_xn"))
                    dst = xnT[:, dc * NQ + t * 128: dc * NQ + t * 128 + 128]
                    if eng == "ve":
                        em.op("ve", lambda E, s=sl, d=dst: E.tensor_copy(d, s),
                              inc=ecnt)
                    else:
                        em.op("act", lambda E, s=sl, d=dst: E.copy(d, s),
                              inc=ecnt)

        lnval_at = {}

        def ln1_zios(b):
            tiles = list(range(4 * b, min(4 * b + 4, NTT)))
            return tiles, [(zbuf[:, (t % 4) * D:(t % 4) * D + D],
                            zbuf[:, (t % 4) * D:(t % 4) * D + D]) for t in tiles]

        def ln1_batch_ve(b):
            tiles, zios = ln1_zios(b)
            em.wait("ve", "pe_xn", em.val("pe_xn"))
            for t in tiles:
                ln1_ve(t)
            em.wait("ve", "ve_z", em.val("ve_z"))
            emit_ln_stats(zios, 0)

        def ln1_batch_finish(b):
            tiles, zios = ln1_zios(b)
            emit_ln_finish(zios, 0)
            for t in tiles:
                lnval_at[t] = em.val("ve_ln")

        def ln1_batch_pe(b):
            for t in range(4 * b, min(4 * b + 4, NTT)):
                ln1_pe(t)

        NTC = NQ // 512
        mm_slots = [psB[:, 0:512], psB[:, 512:1024]]
        tslots = [psA[:, 1024:1152], psT[:, 0:128]]
        ln1_batch_ve(0)
        ln1_batch_finish(0)
        ln1_batch_pe(0)
        for tc in range(NTC):
            if tc + 1 < NTC:
                ln1_batch_ve(tc + 1)
            em.wait("pe", "evac_xnv", em.val("evac_xnv"))
            em.wait("pe", "evac_xna", em.val("evac_xna"))
            em.wait("pe", "act_y", em.val("act_y"))
            for ht in range(NHT):
                mi = tc * NHT + ht
                m = mm_slots[mi % 2]
                if mi >= 2:
                    em.wait("pe", "act_h", em.val("act_h") - 1)
                for dc in range(DC):
                    em.op("pe", lambda E, m=m, dc=dc, ht=ht, tc=tc: E.matmul(
                        m, w1_sb[:, dc * HF + ht * 128: dc * HF + ht * 128 + 128],
                        xnT[:, dc * NQ + tc * 512: dc * NQ + tc * 512 + 512],
                        start=(dc == 0), stop=(dc == DC - 1)),
                        inc=("pe_h" if dc == DC - 1 else None))
                em.wait("act", "pe_h", em.val("pe_h"))
                em.op("act", lambda E, m=m, ht=ht: E.activation(
                    hT[:, ht * 512:ht * 512 + 512], m, AF.Relu,
                    bias=b1_sb[:, ht:ht + 1], scale=1.0), inc="act_h")
            if tc + 1 < NTC:
                ln1_batch_finish(tc + 1)

            def z2_transposes(dt, zslot):
                em.wait("pe", "ve_y", em.val("ve_y"))
                for st4 in range(4):
                    k = dt * 4 + st4
                    sl = tslots[k % 2]
                    if True:
                        em.wait("pe", "ve_z2", max(0, em.val("ve_z2") - 1))
                    em.op("pe", lambda E, sl=sl, st4=st4, z=zslot: E.transpose(
                        sl, z[:, st4 * 128:st4 * 128 + 128], ident[:, :]), inc="pe_z2")
                    em.wait("ve", "pe_z2", em.val("pe_z2"))
                    em.op("ve", lambda E, sl=sl, st4=st4, dt=dt: E.tensor_copy(
                        z2n[:, st4 * D + dt * 128: st4 * D + dt * 128 + 128], sl),
                        inc="ve_z2")

            pending_z = None
            for dt in range(NJT):
                mi2 = tc * NJT + dt
                m = mm_slots[mi2 % 2]
                zslot = z2T[:, (dt % 2) * 512:(dt % 2) * 512 + 512]
                em.wait("pe", "act_h", em.val("act_h"))
                if mi2 >= 2:
                    em.wait("pe", "act_y", em.val("act_y") - 1)
                for ht in range(NHT):
                    em.op("pe", lambda E, m=m, ht=ht, dt=dt: E.matmul(
                        m, w2_sb[:, ht * D + dt * 128: ht * D + dt * 128 + 128],
                        hT[:, ht * 512:ht * 512 + 512],
                        start=(ht == 0), stop=(ht == NHT - 1)),
                        inc=("pe_y" if ht == NHT - 1 else None))
                if pending_z is not None:
                    z2_transposes(*pending_z)
                em.wait("act", "pe_y", em.val("pe_y"))
                em.wait("act", "pe_z2", max(0, em.val("pe_z2") - 4))
                em.op("act", lambda E, m=m, dt=dt, z=zslot: E.activation(
                    z, m, AF.Identity, bias=b2_sb[:, dt:dt + 1], scale=1.0),
                    inc="act_y")
                em.wait("ve", "act_y", em.val("act_y"))
                em.op("ve", lambda E, dt=dt, tc=tc, z=zslot: E.tensor_tensor(
                    z, z, xnT[:, dt * NQ + tc * 512: dt * NQ + tc * 512 + 512], ALU.add),
                    inc="ve_y")
                pending_z = (dt, zslot)
            z2_transposes(*pending_z)
            em.wait("ve", "ve_z2", em.val("ve_z2"))
            emit_ln_batch([(z2n[:, st4 * D:st4 * D + D], None) for st4 in range(4)], 16)
            for st4 in range(4):
                oi = tc * 4 + st4
                c0 = lns[:, 16 + 4 * st4:16 + 4 * st4 + 1]
                c1 = lns[:, 16 + 4 * st4 + 1:16 + 4 * st4 + 2]
                ob = ostage[:, (oi % 2) * D:(oi % 2) * D + D]
                if oi >= 2:
                    em.wait("ve", "sp_out%d" % (oi % 2), em.val("sp_out%d" % (oi % 2)))
                em.op("ve", lambda E, st4=st4, ob=ob, c0=c0, c1=c1: E.tensor_scalar(
                    ob, z2n[:, st4 * D:st4 * D + D], c0, c1,
                    ALU.subtract, ALU.mult), inc="ve_ln")
                em.wait("sp", "ve_ln", em.val("ve_ln"))
                em.op("sp", lambda E, ob=ob, tc=tc, st4=st4: E.dma_start(
                    out_d[tc * 512 + st4 * 128: tc * 512 + st4 * 128 + 128, :], ob),
                    inc="sp_out%d" % (oi % 2), n=16)
            if tc + 1 < NTC:
                ln1_batch_pe(tc + 1)
        if "xn" in debug:
            for dc in range(DC):
                em.wait("gp", "evac_xnv", em.val("evac_xnv"))
                em.wait("gp", "evac_xna", em.val("evac_xna"))
                em.op("gp", lambda E, dc=dc: E.dma_start(
                    dbg["xn"][dc * 128:(dc + 1) * 128, :],
                    xnT[:, dc * NQ:(dc + 1) * NQ]), inc="sp_dbgx", n=16)
        em.wait("sp", "sp_out0", em.val("sp_out0"))
        em.wait("sp", "sp_out1", em.val("sp_out1"))

    emit_phase_w()
    emit_phase_p()
    emit_phase_s()
    emit_phase_a()
    emit_phase_f()

    with nc.Block() as blk:
        em.run(blk)
    return nc


_NC_CACHE = {}


def _get_nc(key, cfg, debug=()):
    if key not in _NC_CACHE:
        _NC_CACHE[key] = build(cfg, debug=debug)
    return _NC_CACHE[key]


def kernel(L0_attn_hidden_states, L0_attn_qk_w, L0_attn_qk_b, L0_attn_v_w,
           L0_attn_v_b, L0_attn_hash_a, L0_ffn_w_gate, L0_ffn_w_noise,
           L0_ffn_W1, L0_ffn_b1, L0_ffn_W2, L0_ffn_b2):
    x = np.ascontiguousarray(np.asarray(L0_attn_hidden_states, dtype=np.float32))
    Bsz, L, D = x.shape
    cfg = CFG(L=L, NH=D // 64, HF=np.asarray(L0_ffn_W1).shape[2])
    nc = _get_nc(("full", L, D), cfg)

    com = {
        "qk_w": np.ascontiguousarray(L0_attn_qk_w, dtype=np.float32),
        "qk_b": np.ascontiguousarray(L0_attn_qk_b, dtype=np.float32),
        "v_w": np.ascontiguousarray(L0_attn_v_w, dtype=np.float32),
        "v_b": np.ascontiguousarray(L0_attn_v_b, dtype=np.float32),
        "hash_a": np.ascontiguousarray(L0_attn_hash_a, dtype=np.float32),
        "w1": np.ascontiguousarray(np.asarray(L0_ffn_W1)[1], dtype=np.float32),
        "b1": np.ascontiguousarray(np.asarray(L0_ffn_b1)[1], dtype=np.float32),
        "w2": np.ascontiguousarray(np.asarray(L0_ffn_W2)[1], dtype=np.float32),
        "b2": np.ascontiguousarray(np.asarray(L0_ffn_b2)[1], dtype=np.float32),
    }
    NQ = L // 2
    in_maps = []
    for c in range(8):
        b = c // 2
        if c % 2 == 0:
            xb = x[b]
        else:
            xb = np.concatenate([x[b, NQ:], x[b, :NQ]], axis=0)
        m = dict(com)
        m["x"] = np.ascontiguousarray(xb)
        in_maps.append(m)

    res = run_bass_kernel_spmd(nc, in_maps, core_ids=list(range(8)))
    out = np.empty((Bsz, L, D), dtype=np.float32)
    for c in range(8):
        b = c // 2
        r = res.results[c]["out"]
        if c % 2 == 0:
            out[b, :NQ] = r
        else:
            out[b, NQ:] = r
    return out


# version 10
# speedup vs baseline: 1.1504x; 1.0015x over previous
import numpy as np
import concourse.bass as bass
import concourse.mybir as mybir
from concourse.bass_utils import run_bass_kernel_spmd

F32 = mybir.dt.float32
BF16 = mybir.dt.bfloat16
AF = mybir.ActivationFunctionType
ALU = mybir.AluOpType
AX = mybir.AxisListType.X

CBIT = 4992.0
EXP_BIAS = -(CBIT + 5.0)
LN_EPS = 1e-5

ENGS = ("pe", "act", "ve", "sp", "gp")


class CFG:
    def __init__(self, L=8192, NH=16, HF=2048):
        self.L = L
        self.NH = NH
        self.D = NH * 64
        self.NQ = L // 2
        self.HF = HF
        self.DC = self.D // 128
        self.NJT = self.D // 128
        self.NKT = L // 128
        self.NQC = self.NQ // 512
        self.NTT = self.NQ // 128


class Emitter:
    def __init__(self, nc):
        self.nc = nc
        self.streams = {e: [] for e in ENGS}
        self.sems = {}
        self.cnt = {}

    def sem(self, name):
        if name not in self.sems:
            self.sems[name] = self.nc.alloc_semaphore(name)
            self.cnt[name] = 0
        return self.sems[name]

    def val(self, name):
        self.sem(name)
        return self.cnt[name]

    def op(self, eng, fn, inc=None, n=1):
        if inc is not None:
            h = self.sem(inc)
            self.cnt[inc] += n

            def wrapped(E, fn=fn, h=h, n=n):
                fn(E).then_inc(h, n)
            self.streams[eng].append(wrapped)
        else:
            self.streams[eng].append(fn)

    def wait(self, eng, name, value=None):
        if value is None:
            value = self.val(name)
        if value <= 0:
            return
        h = self.sem(name)
        self.streams[eng].append(lambda E, h=h, v=value: E.wait_ge(h, v))

    def run(self, block):
        for eng, meth in (("pe", block.tensor), ("act", block.scalar),
                          ("ve", block.vector), ("sp", block.sync),
                          ("gp", block.gpsimd)):
            fns = self.streams[eng]
            if fns:
                meth(lambda E, fns=fns: [f(E) for f in fns])
            self.streams[eng] = []


def build(cfg, debug=()):
    nc = bass.Bass("TRN2", target_bir_lowering=False)
    em = Emitter(nc)
    L, NH, D, NQ, HF = cfg.L, cfg.NH, cfg.D, cfg.NQ, cfg.HF
    DC, NJT, NKT, NQC, NTT = cfg.DC, cfg.NJT, cfg.NKT, cfg.NQC, cfg.NTT
    NHT = HF // 128
    NQUARTER = 4
    QL = L // NQUARTER
    QT_TILES = QL // 128
    QTC = QL // 512
    assert QTC >= 1 and NQC >= 1
    NHX = 128
    VHW = min(512, D)
    NVH = D // VHW
    BCW = min(512, D)

    x_d = nc.dram_tensor("x", [L, D], F32, kind="ExternalInput").ap()
    qkw_d = nc.dram_tensor("qk_w", [D, D], F32, kind="ExternalInput").ap()
    qkb_d = nc.dram_tensor("qk_b", [D], F32, kind="ExternalInput").ap()
    vw_d = nc.dram_tensor("v_w", [D, D], F32, kind="ExternalInput").ap()
    vb_d = nc.dram_tensor("v_b", [D], F32, kind="ExternalInput").ap()
    ha_d = nc.dram_tensor("hash_a", [66], F32, kind="ExternalInput").ap()
    w1_d = nc.dram_tensor("w1", [D, HF], F32, kind="ExternalInput").ap()
    b1_d = nc.dram_tensor("b1", [HF], F32, kind="ExternalInput").ap()
    w2_d = nc.dram_tensor("w2", [HF, D], F32, kind="ExternalInput").ap()
    b2_d = nc.dram_tensor("b2", [D], F32, kind="ExternalInput").ap()
    out_d = nc.dram_tensor("out", [NQ, D], F32, kind="ExternalOutput").ap()

    qkT_s = nc.dram_tensor("qkT_s", [D, L], BF16, kind="Internal").ap()
    xbf_s = nc.dram_tensor("xbf_s", [L, D], BF16, kind="Internal").ap()
    qkw_bf = nc.dram_tensor("qkw_bf", [D, D], BF16, kind="Internal").ap()
    vw_bf = nc.dram_tensor("vw_bf", [D, D], BF16, kind="Internal").ap()
    v_s = nc.dram_tensor("v_s", [L, D], BF16, kind="Internal").ap()
    ctx_s = nc.dram_tensor("ctx_s", [NH * NQ, 65], BF16, kind="Internal").ap()
    bits_s = nc.dram_tensor("bits_s", [NH, L + NQ], BF16, kind="Internal").ap()

    dbg = {}
    if "qkT" in debug:
        dbg["qkT"] = nc.dram_tensor("dbg_qkT", [D, L], F32, kind="ExternalOutput").ap()
    if "stats" in debug:
        dbg["dot1"] = nc.dram_tensor("dbg_dot1", [NH, L], F32, kind="ExternalOutput").ap()
        dbg["n2"] = nc.dram_tensor("dbg_n2", [NH, L], F32, kind="ExternalOutput").ap()
        dbg["bits"] = nc.dram_tensor("dbg_bits", [NH, L + NQ], F32, kind="ExternalOutput").ap()
    if "v" in debug:
        dbg["v"] = nc.dram_tensor("dbg_v", [L, D], F32, kind="ExternalOutput").ap()
    if "ctx" in debug:
        dbg["ctx"] = nc.dram_tensor("dbg_ctx", [NH * NQ, 65], F32, kind="ExternalOutput").ap()
    if "xn" in debug:
        dbg["xn"] = nc.dram_tensor("dbg_xn", [D, NQ], F32, kind="ExternalOutput").ap()

    guards = []

    def AL(name, shape, dtype, side="left"):
        g = nc.sbuf_tensor(name, shape, dtype, side=side)
        t = g.__enter__()
        guards.append((name, g))
        return t

    def FREE(names):
        for name in reversed(names):
            for i in range(len(guards) - 1, -1, -1):
                if guards[i][0] == name:
                    guards[i][1].__exit__(None, None, None)
                    guards.pop(i)
                    break

    ident = AL("ident", [128, 128], F32)
    expb = AL("expb", [128, 1], F32)
    lneps = AL("lneps", [128, 1], F32)
    st = AL("st", [NH, 16], F32)
    a23 = AL("a23", [NH, 2], F32)
    qkb_sb = AL("qkb_sb", [128, NJT], F32)
    b1_sb = AL("b1_sb", [128, NHT], F32)
    b2_sb = AL("b2_sb", [128, NJT], F32)

    dbufs = {}
    if "qkT" in debug:
        dbufs["dbq"] = AL("dbq", [128, L], F32)
    if "v" in debug:
        dbufs["dbv"] = AL("dbv", [128, D], F32)
    if "ctx" in debug:
        dbufs["dbc"] = AL("dbc", [128, NTT * 65], F32)

    dot1 = AL("dot1", [NH, L], F32)
    n2 = AL("n2", [NH, L], BF16)
    bits = AL("bits", [NH, L + NQ], BF16)

    qk_wT = AL("qk_wT", [128, DC * D], BF16, side="right")
    v_wT = AL("v_wT", [128, DC * D], BF16, side="right")
    vbb_sb = AL("vbb_sb", [128, D], BF16, side="right")
    a1sel = AL("a1sel", [128, NJT * NH], F32, side="right")
    n1sel = AL("n1sel", [128, NJT * NH], BF16, side="right")
    onesb = AL("onesb", [1, 128], BF16, side="right")
    vbrow = AL("vbrow", [1, D], BF16, side="right")
    onesf = AL("onesf", [1, NH], F32, side="right")

    psA = nc.alloc_psum_tensor("psA", [128, 1536], F32)
    psB = nc.alloc_psum_tensor("psB", [128, 1536], F32)
    psC = nc.alloc_psum_tensor("psC", [128, 512], F32)
    psT = nc.alloc_psum_tensor("psT", [128, 512], F32)


    def emit_phase_w():
        em.op("gp", lambda E: E.memset(ident[:, :], 0.0), inc="gp_i0")
        em.wait("gp", "gp_i0")
        em.op("gp", lambda E: E.affine_select(
            out=ident[:, :], in_=ident[:, :], compare_op=ALU.not_equal,
            fill=1.0, base=0, pattern=[[-1, 128]], channel_multiplier=1), inc="gp_init")
        em.op("sp", lambda E: E.dma_start(
            qkb_sb[:, 0:NJT], qkb_d.rearrange("(j p) -> p j", p=128),
            allow_slow_non_contiguous=True), inc="sp_w", n=16)
        em.op("sp", lambda E: E.dma_start(
            b1_sb[:, 0:NHT], b1_d.rearrange("(t p) -> p t", p=128),
            allow_slow_non_contiguous=True), inc="sp_w", n=16)
        em.op("sp", lambda E: E.dma_start(
            b2_sb[:, 0:NJT], b2_d.rearrange("(t p) -> p t", p=128),
            allow_slow_non_contiguous=True), inc="sp_w", n=16)
        em.op("ve", lambda E: E.memset(expb[:, :], EXP_BIAS), inc="ve_w")
        em.op("ve", lambda E: E.memset(lneps[:, :], LN_EPS), inc="ve_w")
        em.op("ve", lambda E: E.memset(a1sel[:, :], 0.0), inc="ve_w")
        em.op("ve", lambda E: E.memset(n1sel[:, :], 0.0), inc="ve_w")
        em.wait("ve", "ve_w", em.val("ve_w"))
        for jt in range(NJT):
            c0 = jt * NH + 2 * jt
            em.wait("sp", "ve_w", em.val("ve_w"))
            em.op("sp", lambda E, c0=c0: E.dma_start(
                a1sel[0:64, c0:c0 + 1],
                ha_d[0:64].rearrange("(p o) -> p o", o=1),
                allow_slow_non_contiguous=True), inc="sp_w", n=16)
            em.op("sp", lambda E, c0=c0: E.dma_start(
                a1sel[64:128, c0 + 1:c0 + 2],
                ha_d[0:64].rearrange("(p o) -> p o", o=1),
                allow_slow_non_contiguous=True), inc="sp_w", n=16)
            em.op("ve", lambda E, c0=c0: E.memset(n1sel[0:64, c0:c0 + 1], 1.0))
            em.op("ve", lambda E, c0=c0: E.memset(n1sel[64:128, c0 + 1:c0 + 2], 1.0),
                  inc="ve_w")
        em.op("ve", lambda E: E.memset(onesb[:, :], 1.0), inc="ve_w")
        em.op("sp", lambda E: E.dma_start(
            st[0:1, 0:2], ha_d[64:66].rearrange("(o c) -> o c", o=1)), inc="sp_w", n=16)
        em.op("gp", lambda E: E.dma_start(
            vbrow[0:1, :], vb_d.rearrange("(o c) -> o c", o=1)), inc="gp_w", n=16)
        em.wait("pe", "gp_init")
        em.wait("pe", "ve_w")
        em.wait("pe", "sp_w")
        em.wait("pe", "gp_w")
        em.op("ve", lambda E: E.memset(onesf[:, :], 1.0), inc="ve_w")
        em.wait("pe", "ve_w")
        em.op("pe", lambda E: E.matmul(psC[0:NH, 0:2], onesf[0:1, 0:NH],
                                       st[0:1, 0:2], start=True, stop=True), inc="pe_w")
        em.wait("ve", "pe_w", em.val("pe_w"))
        em.op("ve", lambda E: E.tensor_copy(a23[:, :], psC[0:NH, 0:2]), inc="ve_a23")
        for half in range(D // BCW):
            em.wait("pe", "ve_vbb", em.val("ve_vbb") if half > 0 else 0)
            em.wait("pe", "ve_a23", em.val("ve_a23"))
            em.op("pe", lambda E, h=half: E.matmul(
                psC[:, 0:BCW], onesb[0:1, 0:128],
                vbrow[0:1, h * BCW:(h + 1) * BCW], start=True, stop=True), inc="pe_w")
            em.wait("ve", "pe_w", em.val("pe_w"))
            em.op("ve", lambda E, h=half: E.tensor_copy(
                vbb_sb[:, h * BCW:(h + 1) * BCW], psC[:, 0:BCW]), inc="ve_vbb")
        em.op("gp", lambda E: E.dma_start(qkw_bf, qkw_d), inc="gp_wbf", n=16)
        em.op("gp", lambda E: E.dma_start(vw_bf, vw_d), inc="gp_wbf", n=16)
        for q in range(NQUARTER):
            em.op("gp", lambda E, q=q: E.dma_start(
                xbf_s[q * QL:(q + 1) * QL, :], x_d[q * QL:(q + 1) * QL, :]),
                inc="gp_xbf%d" % q, n=16)
        em.wait("sp", "gp_wbf", em.val("gp_wbf"))
        for wi, (src, dstT) in enumerate(((qkw_bf, qk_wT), (vw_bf, v_wT))):
            for dc in range(DC):
                em.op("sp", lambda E, s=src, d=dstT, dc=dc: E.dma_start(
                    d[:, dc * D:(dc + 1) * D], s[:, dc * 128:(dc + 1) * 128],
                    transpose=True), inc="sp_wt", n=16)

    xT = AL("xT", [128, 2 * DC * QL], BF16, side="right")
    qkf = AL("qkf", [128, 2 * 512], F32, side="right")
    sqb = AL("sqb", [128, 2 * 512], BF16, side="right")
    qstage = AL("qstage", [128, 2 * NJT * 512], BF16, side="right")
    vstage = AL("vstage", [128, 2 * D], BF16, side="right")

    def emit_phase_p():
        em.wait("act", "sp_w", em.val("sp_w"))
        em.wait("act", "ve_w", em.val("ve_w"))
        mm_slots = [psA[:, 1024:1536], psB[:, 0:512]]
        d1_slots = [psB[:, 512:1024], psB[:, 1024:1536]]
        n2_slots = [psC[:, 0:512], psT[:, 0:512]]
        em.wait("pe", "ve_vbb", em.val("ve_vbb"))
        def issue_xt(q):
            par = q % 2
            em.wait("sp", "gp_xbf%d" % q, em.val("gp_xbf%d" % q))
            for dc in range(DC):
                em.op("sp", lambda E, q=q, dc=dc, par=par: E.dma_start(
                    xT[:, par * DC * QL + dc * QL: par * DC * QL + (dc + 1) * QL],
                    xbf_s[q * QL:(q + 1) * QL, dc * 128:(dc + 1) * 128],
                    transpose=True), inc="sp_xt%d" % par, n=16)

        issue_xt(0)
        for q in range(NQUARTER):
            par = q % 2
            xTq = xT[:, par * DC * QL:(par + 1) * DC * QL]
            if q + 1 < NQUARTER:
                em.wait("sp", "pe_qk", em.val("pe_qk"))
                em.wait("sp", "pe_v", em.val("pe_v"))
                issue_xt(q + 1)
            em.wait("pe", "sp_xt%d" % par, em.val("sp_xt%d" % par))
            em.wait("pe", "sp_wt", em.val("sp_wt"))
            em.wait("pe", "ve_v", em.val("ve_v"))
            for tc in range(QTC):
                gi = q * QTC + tc
                d1b = d1_slots[gi % 2]
                n2b = n2_slots[gi % 2]
                def stats_mms(jt, d1b, n2b, qf, sq):
                    em.wait("pe", "act_qk", em.val("act_qk"))
                    em.op("pe", lambda E, d1b=d1b, jt=jt, qf=qf: E.matmul(
                        d1b[0:NH, :], a1sel[:, jt * NH:jt * NH + NH], qf,
                        start=(jt == 0), stop=(jt == NJT - 1)), inc="pe_d1")
                    em.wait("pe", "ve_sq", em.val("ve_sq"))
                    em.op("pe", lambda E, n2b=n2b, jt=jt, sq=sq: E.matmul(
                        n2b[0:NH, :], n1sel[:, jt * NH:jt * NH + NH], sq,
                        start=(jt == 0), stop=(jt == NJT - 1)), inc="pe_n2")

                pending = None
                for jt in range(NJT):
                    mi = gi * NJT + jt
                    mmb = mm_slots[mi % 2]
                    if mi >= 2:
                        em.wait("pe", "act_qk", em.val("act_qk") - 1)
                    if jt == 0 and gi >= 2:
                        em.wait("pe", "ve_d1", em.val("ve_d1") - 1)
                        em.wait("pe", "ve_n2", em.val("ve_n2") - 1)
                    for dc in range(DC):
                        em.op("pe", lambda E, m=mmb, dc=dc, jt=jt, tc=tc, xTq=xTq: E.matmul(
                            m, qk_wT[:, (dc * NJT + jt) * 128:(dc * NJT + jt) * 128 + 128],
                            xTq[:, dc * QL + tc * 512: dc * QL + tc * 512 + 512],
                            start=(dc == 0), stop=(dc == DC - 1)),
                            inc=("pe_qk" if dc == DC - 1 else None))
                    if pending is not None:
                        stats_mms(*pending)
                    qf = qkf[:, (mi % 2) * 512:(mi % 2) * 512 + 512]
                    sq = sqb[:, (mi % 2) * 512:(mi % 2) * 512 + 512]
                    em.wait("act", "pe_qk", em.val("pe_qk"))
                    if mi >= 2:
                        em.wait("act", "pe_d1", em.val("pe_d1"))
                    em.op("act", lambda E, m=mmb, qf=qf, jt=jt: E.activation(
                        qf, m, AF.Identity, bias=qkb_sb[:, jt:jt + 1], scale=1.0),
                        inc="act_qk")
                    em.wait("ve", "act_qk", em.val("act_qk"))
                    if mi >= 2:
                        em.wait("ve", "pe_n2", em.val("pe_n2"))
                    em.op("ve", lambda E, sq=sq, qf=qf: E.tensor_mul(sq, qf, qf),
                          inc="ve_sq")
                    stg = qstage[:, (gi % 2) * NJT * 512 + jt * 512:
                                 (gi % 2) * NJT * 512 + jt * 512 + 512]
                    if gi >= 2 and jt == 0:
                        em.wait("ve", "gp_qs%d" % (gi % 2), em.val("gp_qs%d" % (gi % 2)))
                    em.op("ve", lambda E, stg=stg, qf=qf: E.tensor_copy(stg, qf),
                          inc="ve_qs")
                    pending = (jt, d1b, n2b, qf, sq)
                stats_mms(*pending)
                em.wait("ve", "pe_d1", em.val("pe_d1"))
                em.wait("ve", "pe_n2", em.val("pe_n2"))
                cs = q * QL + tc * 512
                em.op("ve", lambda E, d1b=d1b, cs=cs: E.tensor_copy(
                    dot1[:, cs:cs + 512], d1b[0:NH, :]), inc="ve_d1")
                em.op("ve", lambda E, n2b=n2b, cs=cs: E.tensor_copy(
                    n2[:, cs:cs + 512], n2b[0:NH, :]), inc="ve_n2")
                em.wait("gp", "ve_qs", em.val("ve_qs"))
                src = qstage[:, (gi % 2) * NJT * 512:(gi % 2) * NJT * 512 + NJT * 512]
                dst = qkT_s.rearrange("(jt p) Lx -> p jt Lx", p=128)[:, :, cs:cs + 512]
                em.op("gp", lambda E, s=src, d=dst: E.dma_start(
                    d, s.rearrange("p (jt c) -> p jt c", c=512)),
                    inc="gp_qs%d" % (gi % 2), n=16)
            em.wait("pe", "act_qk", em.val("act_qk"))
            for t in range(QT_TILES):
                gt = q * QT_TILES + t
                vs = vstage[:, (gt % 2) * D:(gt % 2) * D + D]
                if gt >= 2:
                    em.wait("ve", "gp_vs%d" % (gt % 2), em.val("gp_vs%d" % (gt % 2)))
                for half in range(NVH):
                    vi = gt * NVH + half
                    mmb = mm_slots[vi % 2]
                    em.wait("pe", "ve_v", max(0, em.val("ve_v") - 1))
                    for dc in range(DC):
                        em.op("pe", lambda E, m=mmb, dc=dc, t=t, h=half, xTq=xTq: E.matmul(
                            m[:, 0:VHW],
                            xTq[:, dc * QL + t * 128: dc * QL + t * 128 + 128],
                            v_wT[:, dc * D + h * VHW: dc * D + h * VHW + VHW],
                            start=(dc == 0), stop=(dc == DC - 1)),
                            inc=("pe_v" if dc == DC - 1 else None))
                    em.wait("ve", "pe_v", em.val("pe_v"))
                    em.op("ve", lambda E, m=mmb, vs=vs, h=half: E.tensor_tensor(
                        vs[:, h * VHW:h * VHW + VHW], m[:, 0:VHW],
                        vbb_sb[:, h * VHW:h * VHW + VHW], ALU.add), inc="ve_v")
                em.wait("gp", "ve_v", em.val("ve_v"))
                em.op("gp", lambda E, vs=vs, q=q, t=t: E.dma_start(
                    v_s[q * QL + t * 128: q * QL + t * 128 + 128, :], vs),
                    inc="gp_vs%d" % (gt % 2), n=16)
        if "qkT" in debug:
            db = dbufs["dbq"]
            for jt in range(NJT):
                em.wait("gp", "gp_qs0", em.val("gp_qs0"))
                em.wait("gp", "gp_qs1", em.val("gp_qs1"))
                em.wait("gp", "sp_dbg", em.val("sp_dbg"))
                em.op("gp", lambda E, jt=jt: E.dma_start(
                    db[:, :], qkT_s[jt * 128:(jt + 1) * 128, :]), inc="sp_dbg", n=16)
                em.wait("gp", "sp_dbg")
                em.op("gp", lambda E, jt=jt: E.dma_start(
                    dbg["qkT"][jt * 128:(jt + 1) * 128, :], db[:, :]), inc="sp_dbg", n=16)
        if "v" in debug:
            dbv = dbufs["dbv"]
            for t in range(L // 128):
                em.wait("gp", "gp_vs0", em.val("gp_vs0"))
                em.wait("gp", "gp_vs1", em.val("gp_vs1"))
                em.wait("gp", "sp_dbgv", em.val("sp_dbgv"))
                em.op("gp", lambda E, t=t: E.dma_start(
                    dbv[:, :], v_s[t * 128:(t + 1) * 128, :]), inc="sp_dbgv", n=16)
                em.wait("gp", "sp_dbgv")
                em.op("gp", lambda E, t=t: E.dma_start(
                    dbg["v"][t * 128:(t + 1) * 128, :], dbv[:, :]), inc="sp_dbgv", n=16)

    FREE(["xT", "qkf", "sqb", "qstage", "vstage"])
    FREE(["onesf"])
    FREE(["qk_wT", "v_wT", "vbb_sb", "a1sel", "n1sel", "onesb", "vbrow"])
    hkv = AL("hkv", [NH, L], BF16)

    def emit_phase_s():
        V = "ve"
        for nm in ("pe_qk", "pe_v", "pe_d1", "pe_n2", "gp_qs0", "gp_qs1",
                   "gp_vs0", "gp_vs1", "act_qk", "ve_qs", "ve_v", "ve_sq"):
            em.wait(V, nm, em.val(nm))
        em.wait(V, "ve_a23", em.val("ve_a23"))
        c = lambda i: st[:, i:i + 1]
        em.wait(V, "ve_s", em.val("ve_s"))
        em.op(V, lambda E: E.reduce_max(c(0), n2[:, :], axis=AX), inc="ve_s")
        em.wait("act", "ve_s", em.val("ve_s"))
        em.op("act", lambda E: E.activation(c(1), c(0), AF.Sqrt), inc="act_s")
        em.wait(V, "act_s", em.val("act_s"))
        em.wait(V, "ve_s", em.val("ve_s"))
        em.op(V, lambda E: E.tensor_single_scalar(c(2), c(1), 1e-12, ALU.max), inc="ve_s")
        em.wait(V, "ve_s", em.val("ve_s"))
        em.op(V, lambda E: E.reciprocal(c(3), c(2)), inc="ve_s")
        em.wait(V, "ve_s", em.val("ve_s"))
        em.op(V, lambda E: E.tensor_single_scalar(c(2), c(3), 0.75, ALU.mult), inc="ve_s")
        em.wait(V, "ve_s", em.val("ve_s"))
        em.op(V, lambda E: E.tensor_mul(c(4), c(2), c(2)), inc="ve_s")
        em.wait(V, "ve_s", em.val("ve_s"))
        em.op(V, lambda E: E.tensor_copy(bits[:, 0:L], dot1[:, :]), inc="ve_s")
        em.wait(V, "ve_s", em.val("ve_s"))
        em.op(V, lambda E: E.tensor_scalar_mul(hkv[:, :], n2[:, :], c(4)), inc="ve_s")
        em.wait(V, "ve_s", em.val("ve_s"))
        em.op(V, lambda E: E.tensor_scalar(n2[:, :], hkv[:, :], -1.0, 0.5,
                                           ALU.mult, ALU.add), inc="ve_s")
        em.wait(V, "ve_s", em.val("ve_s"))
        em.op(V, lambda E: E.tensor_mul(hkv[:, :], hkv[:, :], hkv[:, :]), inc="ve_s")
        em.wait(V, "ve_s", em.val("ve_s"))
        em.op(V, lambda E: E.tensor_scalar(hkv[:, :], hkv[:, :], -1.0, 0.5,
                                           ALU.mult, ALU.add), inc="ve_s")
        em.wait(V, "ve_s", em.val("ve_s"))
        em.op(V, lambda E: E.tensor_scalar_mul(hkv[:, :], hkv[:, :], a23[:, 1:2]), inc="ve_s")
        em.wait(V, "ve_s", em.val("ve_s"))
        em.op(V, lambda E: E.scalar_tensor_tensor(
            hkv[:, :], n2[:, :], a23[:, 0:1], hkv[:, :], ALU.mult, ALU.add), inc="ve_s")
        em.wait(V, "ve_s", em.val("ve_s"))
        em.op(V, lambda E: E.scalar_tensor_tensor(
            hkv[:, :], bits[:, 0:L], c(2), hkv[:, :], ALU.mult, ALU.add), inc="ve_s")
        em.wait(V, "ve_s", em.val("ve_s"))
        em.op(V, lambda E: E.tensor_single_scalar(hkv[:, :], hkv[:, :], 0.0, ALU.is_ge),
              inc="ve_s")
        em.wait(V, "ve_s", em.val("ve_s"))
        em.op(V, lambda E: E.tensor_scalar(bits[:, 0:L], hkv[:, :], 128.0, -64.0,
                                           ALU.mult, ALU.add), inc="ve_s")
        em.wait(V, "ve_s", em.val("ve_s"))
        em.op(V, lambda E: E.reduce_max(c(5), hkv[:, :], axis=AX), inc="ve_s")
        em.wait(V, "ve_s", em.val("ve_s"))
        em.op(V, lambda E: E.tensor_reduce(c(6), hkv[:, :], axis=AX, op=ALU.min), inc="ve_s")
        em.wait(V, "ve_s", em.val("ve_s"))
        em.op(V, lambda E: E.tensor_sub(c(7), c(5), c(6)), inc="ve_s")
        em.wait(V, "ve_s", em.val("ve_s"))
        em.op(V, lambda E: E.tensor_add(c(8), c(5), c(6)), inc="ve_s")
        em.wait(V, "ve_s", em.val("ve_s"))
        em.op(V, lambda E: E.tensor_single_scalar(c(8), c(8), 1.0, ALU.subtract),
              inc="ve_s")
        em.wait(V, "ve_s", em.val("ve_s"))
        em.op(V, lambda E: E.tensor_single_scalar(c(9), c(7), 156.0, ALU.mult), inc="ve_s")
        em.wait(V, "ve_s", em.val("ve_s"))
        em.op(V, lambda E: E.tensor_sub(c(10), c(8), c(7)), inc="ve_s")
        em.wait(V, "ve_s", em.val("ve_s"))
        em.op(V, lambda E: E.tensor_single_scalar(c(10), c(10), 78.0, ALU.mult), inc="ve_s")
        em.wait(V, "ve_s", em.val("ve_s"))
        em.op(V, lambda E: E.tensor_single_scalar(dot1[:, 0:NQ], dot1[:, 0:NQ], 0.0,
                                                  ALU.is_ge), inc="ve_s")
        em.wait(V, "ve_s", em.val("ve_s"))
        em.op(V, lambda E: E.tensor_scalar(bits[:, L:L + NQ], dot1[:, 0:NQ], c(9), c(10),
                                           ALU.mult, ALU.add), inc="ve_s")
        em.wait("gp", "ve_s", em.val("ve_s"))
        em.op("gp", lambda E: E.dma_start(bits_s, bits[:, :]), inc="gp_bits", n=16)
        if "stats" in debug:
            em.wait("gp", "ve_s", em.val("ve_s"))
            em.op("gp", lambda E: E.dma_start(dbg["dot1"], dot1[:, :]), inc="sp_dbgs", n=16)
            em.op("gp", lambda E: E.dma_start(dbg["n2"], n2[:, :]), inc="sp_dbgs", n=16)
            em.op("gp", lambda E: E.dma_start(dbg["bits"], bits[:, :]), inc="sp_dbgs", n=16)

    FREE(["hkv"])
    FREE(["dot1", "n2", "bits"])
    w1_sb = AL("w1_sb", [128, DC * HF], BF16)
    w2_sb = AL("w2_sb", [128, NHT * D], BF16)
    KT = AL("KT", [65, 2 * L], BF16)
    QTb = AL("QTb", [65, 2 * NQ], BF16)
    vhb = AL("vhb", [128, 2 * NKT * 65], BF16)
    ctxT = AL("ctxT", [65, NQ], F32)
    pt4 = AL("pt4", [128, 4 * 1536], BF16)
    cstage = AL("cstage", [128, NTT * 65], BF16)

    groups = []
    k = 0
    while k < NKT:
        size = min(3, NKT - k)
        groups.append((k, size))
        k += size

    ctx_w = ctx_s.rearrange("(h t p) j -> h p t j", p=128, t=NTT)

    def emit_phase_a():
        em.wait("pe", "ve_d1", em.val("ve_d1"))
        em.wait("pe", "ve_n2", em.val("ve_n2"))
        em.wait("pe", "ve_v", em.val("ve_v"))
        em.wait("pe", "act_qk", em.val("act_qk"))
        em.wait("pe", "evac_xp", em.val("evac_xp"))

        em.wait("act", "ve_w", em.val("ve_w"))
        state = {"exp_at_bank": {0: 0, 1: 0}, "ctx_evac": 0,
                 "av_done": {0: 0, 1: 0, 2: 0, 3: 0}}

        def prefetch(h, into):
            kth = KT[:, into * L: into * L + L]
            em.op("sp", lambda E, h=h, kth=kth: E.dma_start(
                kth[0:64, :], qkT_s[h * 64:(h + 1) * 64, :]),
                inc="sp_kt%d" % into, n=16)
            em.op("gp", lambda E, h=h, kth=kth: E.dma_start(
                kth[64:65, :], bits_s[h:h + 1, 0:L]), inc="gp_kt%d" % into, n=16)
            vb = vhb[:, into * NKT * 65: into * NKT * 65 + NKT * 65]
            dstv = vb.rearrange("p (k j) -> p k j", j=65)[:, :, 0:64]
            srcv = v_s[:, h * 64:(h + 1) * 64].rearrange("(k p) j -> p k j", p=128)
            KH = NKT // 2
            em.op("sp", lambda E, d=dstv, s=srcv, KH=KH: E.dma_start(
                d[:, 0:KH, :], s[:, 0:KH, :]), inc="sp_vh%d" % into, n=16)
            em.op("gp", lambda E, d=dstv, s=srcv, KH=KH: E.dma_start(
                d[:, KH:NKT, :], s[:, KH:NKT, :]), inc="gp_vh%d" % into, n=16)

        em.wait("ve", "sp_dbgs", em.val("sp_dbgs"))
        em.wait("ve", "gp_bits", em.val("gp_bits"))
        em.op("ve", lambda E: E.memset(vhb[:, :], 1.0), inc="ve_ones")
        em.wait("gp", "ve_ones", em.val("ve_ones"))
        em.wait("sp", "ve_ones", em.val("ve_ones"))
        em.wait("gp", "gp_bits", em.val("gp_bits"))
        em.wait("sp", "ve_s", em.val("ve_s"))
        em.wait("sp", "gp_bits", em.val("gp_bits"))
        em.wait("sp", "sp_dbgs", em.val("sp_dbgs"))
        em.wait("gp", "sp_dbgs", em.val("sp_dbgs"))
        em.wait("sp", "gp_qs0", em.val("gp_qs0"))
        em.wait("sp", "gp_qs1", em.val("gp_qs1"))
        em.wait("gp", "gp_vs0", em.val("gp_vs0"))
        em.wait("gp", "gp_vs1", em.val("gp_vs1"))
        prefetch(0, 0)
        prefetch(1, 1)

        ct_at_qc = [0] * NQC

        def build_qt(h):
            par = h % 2
            kth = KT[:, par * L: par * L + L]
            qtb = QTb[:, par * NQ: par * NQ + NQ]
            em.wait("ve", "sp_kt%d" % par, em.val("sp_kt%d" % par))
            em.wait("ve", "gp_kt%d" % par, em.val("gp_kt%d" % par))
            em.wait("ve", "pe_s", em.val("pe_s"))
            em.op("ve", lambda E, kth=kth, q=qtb: E.tensor_scalar_mul(
                q[0:64, :], kth[0:64, 0:NQ], 0.125), inc="ve_qt")
            em.wait("gp", "pe_s", em.val("pe_s"))
            em.op("gp", lambda E, h=h, q=qtb: E.dma_start(
                q[64:65, :], bits_s[h:h + 1, L:L + NQ]), inc="gp_qt%d" % par, n=16)

        build_qt(0)
        for h in range(NH):
            par = h % 2
            kth = KT[:, par * L: par * L + L]
            vb = vhb[:, par * NKT * 65: par * NKT * 65 + NKT * 65]
            qtb = QTb[:, par * NQ: par * NQ + NQ]
            if h + 1 < NH:
                build_qt(h + 1)
            em.wait("pe", "ve_qt", em.val("ve_qt") - (1 if h + 1 < NH else 0))
            em.wait("pe", "gp_qt%d" % par, em.val("gp_qt%d" % par))
            em.wait("pe", "gp_vh%d" % par, em.val("gp_vh%d" % par))
            em.wait("pe", "sp_vh%d" % par, em.val("sp_vh%d" % par))
            em.wait("pe", "sp_kt%d" % par, em.val("sp_kt%d" % par))
            em.wait("pe", "gp_kt%d" % par, em.val("gp_kt%d" % par))

            evac_at_qc = [0] * NQC

            def epilogue_block(b, h):
                em.wait("pe", "ve_ctx", evac_at_qc[b])
                em.wait("pe", "ve_cs", em.val("ve_cs"))
                for j in range(4):
                    tt = b * 4 + j
                    em.op("pe", lambda E, j=j, tt=tt: E.transpose(
                        psT[:, j * 80:j * 80 + 65], ctxT[:, tt * 128:tt * 128 + 128],
                        ident[0:65, 0:65]), inc=("pe_ct" if j == 3 else None))
                ct_at_qc[b] = em.val("pe_ct")
                em.wait("ve", "pe_ct", em.val("pe_ct"))
                if b == 0:
                    em.wait("ve", "gp_cs", em.val("gp_cs"))
                em.op("ve", lambda E, b=b: E.tensor_copy(
                    cstage[:, b * 4 * 65:(b + 1) * 4 * 65].rearrange(
                        "p (t j) -> p t j", j=65),
                    psT[:, 0:320].rearrange("p (t j) -> p t j", j=80)[:, :, 0:65]),
                    inc="ve_cs")

            pending_tail = [None]

            def av_group(qc_ctx, j):
                exp_at_g, vbx = qc_ctx
                k0, size = groups[j]
                ptb = pt4[:, (j % 4) * 1536:(j % 4) * 1536 + 1536]
                em.wait("pe", "act_e", exp_at_g[j])
                if k0 == 0:
                    em.wait("pe", "ve_ctx", state["ctx_evac"])
                for j2 in range(size):
                    kt = k0 + j2
                    em.op("pe", lambda E, v=vbx, kt=kt, p=ptb, j2=j2: E.matmul(
                        psC[0:65, :], v[:, kt * 65:kt * 65 + 65],
                        p[:, j2 * 512:j2 * 512 + 512],
                        start=(kt == 0), stop=(kt == NKT - 1),
                        skip_group_check=True),
                        inc=("pe_avg" if j2 == size - 1 else None))
                state["av_done"][j % 4] = em.val("pe_avg")

            def flush_tail(stage):
                if pending_tail[0] is None:
                    return
                qc_ctx, tqc = pending_tail[0]
                if stage == 0:
                    av_group(qc_ctx, len(groups) - 2)
                    return
                pending_tail[0] = None
                av_group(qc_ctx, len(groups) - 1)
                em.wait("ve", "pe_avg", em.val("pe_avg"))
                em.wait("ve", "pe_ct", ct_at_qc[tqc])
                em.op("ve", lambda E, tqc=tqc: E.tensor_copy(
                    ctxT[:, tqc * 512:tqc * 512 + 512], psC[0:65, :]), inc="ve_ctx")
                state["ctx_evac"] = em.val("ve_ctx")
                evac_at_qc[tqc] = em.val("ve_ctx")

            for qc in range(NQC):
                exp_at_g = []
                qc_ctx = (exp_at_g, vb)
                for g, (k0, size) in enumerate(groups):
                    bank = psA if g % 2 == 0 else psB
                    bpar = g % 2
                    ptb = pt4[:, (g % 4) * 1536:(g % 4) * 1536 + 1536]
                    em.wait("pe", "act_e", state["exp_at_bank"][bpar])
                    for j in range(size):
                        kt = k0 + j
                        em.op("pe", lambda E, b=bank, j=j, kt=kt, kth=kth, qc=qc,
                              qtb=qtb: E.matmul(
                            b[:, j * 512:j * 512 + 512],
                            kth[:, kt * 128:kt * 128 + 128],
                            qtb[:, qc * 512:qc * 512 + 512],
                            start=True, stop=True),
                            inc=("pe_s" if j == size - 1 else None))
                    if g == 0:
                        flush_tail(0)
                    if g == 1:
                        flush_tail(1)
                    em.wait("act", "pe_avg", state["av_done"][g % 4])
                    em.wait("act", "pe_s", em.val("pe_s"))
                    em.op("act", lambda E, b=bank, p=ptb, w=size * 512: E.activation(
                        p[:, 0:w], b[:, 0:w], AF.Exp, bias=expb[:, 0:1], scale=1.0),
                        inc="act_e")
                    state["exp_at_bank"][bpar] = em.val("act_e")
                    exp_at_g.append(em.val("act_e"))
                    if g >= 2:
                        av_group(qc_ctx, g - 2)
                    if g == 1 and qc >= 1:
                        epilogue_block(qc - 1, h)
                pending_tail[0] = (qc_ctx, qc)
            flush_tail(0)
            flush_tail(1)
            epilogue_block(NQC - 1, h)
            em.wait("gp", "ve_cs", em.val("ve_cs"))
            em.op("gp", lambda E, h=h: E.dma_start(
                ctx_w[h], cstage[:, :].rearrange("p (t j) -> p t j", j=65)),
                inc="gp_cs", n=16)
            if h + 2 < NH:
                em.wait("sp", "pe_s", em.val("pe_s"))
                em.wait("gp", "pe_s", em.val("pe_s"))
                em.wait("gp", "pe_avg", em.val("pe_avg"))
                em.wait("sp", "pe_avg", em.val("pe_avg"))
                prefetch(h + 2, par)
            if h == 1:
                em.op("gp", lambda E: E.dma_start(
                    w1_sb[:, :].rearrange("p (dc h) -> p dc h", h=HF),
                    w1_d.rearrange("(dc p) h -> p dc h", p=128)), inc="gp_wffn", n=16)
                em.op("gp", lambda E: E.dma_start(
                    w2_sb[:, :].rearrange("p (ht d) -> p ht d", d=D),
                    w2_d.rearrange("(ht p) d -> p ht d", p=128)), inc="gp_wffn", n=16)
        if "ctx" in debug:
            dbc = dbufs["dbc"]
            dbg_w = dbg["ctx"].rearrange("(h t p) j -> h p t j", p=128, t=NTT)
            for h in range(NH):
                em.wait("gp", "gp_cs", em.val("gp_cs"))
                em.wait("gp", "sp_dbgc", em.val("sp_dbgc"))
                em.op("gp", lambda E, h=h: E.dma_start(
                    dbc[:, :].rearrange("p (t j) -> p t j", j=65), ctx_w[h]),
                    inc="sp_dbgc", n=16)
                em.wait("gp", "sp_dbgc")
                em.op("gp", lambda E, h=h: E.dma_start(
                    dbg_w[h], dbc[:, :].rearrange("p (t j) -> p t j", j=65)),
                    inc="sp_dbgc", n=16)

    FREE(["KT", "QTb", "vhb", "ctxT", "pt4", "cstage"])
    xres = AL("xres", [128, 2 * D], F32)
    cres = AL("cres", [128, 2 * NH * 65], BF16)
    cres2 = AL("cres2", [128, 2 * D], BF16)
    rcpt = AL("rcpt", [128, 2 * NH], F32)
    zbuf = AL("zbuf", [128, 4 * D], F32)
    zzbuf = AL("zzbuf", [128, 2 * D], F32)
    lns = AL("lns", [128, 40], F32)
    xnT = AL("xnT", [128, DC * NQ], BF16)
    hT = AL("hT", [128, NHT * 512], BF16)
    z2T = AL("z2T", [128, 2 * 512], F32)
    z2n = AL("z2n", [128, 4 * D], BF16)
    ostage = AL("ostage", [128, 2 * D], F32)

    ctx_r = ctx_s.rearrange("(h t p) j -> t p h j", p=128, t=NTT)

    def _ln_cs(zios, colbase):
        return [(lns[:, colbase + 4 * i:colbase + 4 * i + 1],
                 lns[:, colbase + 4 * i + 1:colbase + 4 * i + 2],
                 lns[:, colbase + 4 * i + 2:colbase + 4 * i + 3])
                for i in range(len(zios))]

    def emit_ln_stats(zios, colbase):
        W = lambda: em.wait("ve", "ve_ln", em.val("ve_ln"))
        cs = _ln_cs(zios, colbase)
        W()
        for i, (zin, _) in enumerate(zios):
            em.op("ve", lambda E, z=zin, c=cs[i]: E.reduce_sum(c[0], z, axis=AX),
                  inc="ve_ln")
        for i, (zin, _) in enumerate(zios):
            if i == 2:
                W()
            em.op("ve", lambda E, z=zin, c=cs[i], i=i: E.scalar_tensor_tensor(
                zzbuf[:, (i % 2) * D:(i % 2) * D + D], z, 1.0, z, ALU.mult, ALU.mult,
                accum_out=c[1]), inc="ve_ln")
        W()
        for i in range(len(zios)):
            em.op("ve", lambda E, c=cs[i]: E.tensor_single_scalar(
                c[0], c[0], 1.0 / D, ALU.mult), inc="ve_ln")
            em.op("ve", lambda E, c=cs[i]: E.tensor_single_scalar(
                c[1], c[1], 1.0 / D, ALU.mult), inc="ve_ln")
        W()
        for i in range(len(zios)):
            em.op("ve", lambda E, c=cs[i]: E.scalar_tensor_tensor(
                c[2], c[0], -1.0, c[0], ALU.mult, ALU.mult), inc="ve_ln")
        W()
        for i in range(len(zios)):
            em.op("ve", lambda E, c=cs[i]: E.tensor_add(c[1], c[1], c[2]), inc="ve_ln")

    def emit_ln_finish(zios, colbase):
        W = lambda: em.wait("ve", "ve_ln", em.val("ve_ln"))
        cs = _ln_cs(zios, colbase)
        em.wait("act", "ve_ln", em.val("ve_ln"))
        for i in range(len(zios)):
            em.op("act", lambda E, c=cs[i]: E.activation(
                c[2], c[1], AF.Sqrt, bias=lneps[0:NHX, 0:1]), inc="act_ln")
        em.wait("ve", "act_ln", em.val("act_ln"))
        for i in range(len(zios)):
            em.op("ve", lambda E, c=cs[i]: E.reciprocal(c[1], c[2]), inc="ve_ln")
        W()
        for i, (zin, outbuf) in enumerate(zios):
            if outbuf is not None:
                em.op("ve", lambda E, z=zin, o=outbuf, c=cs[i]: E.tensor_scalar(
                    o, z, c[0], c[1], ALU.subtract, ALU.mult), inc="ve_ln")

    def emit_ln_batch(zios, colbase):
        emit_ln_stats(zios, colbase)
        emit_ln_finish(zios, colbase)

    def emit_phase_f():
        em.wait("act", "ve_w", em.val("ve_w"))
        em.wait("act", "sp_w", em.val("sp_w"))
        em.wait("pe", "act_e", em.val("act_e"))
        for eng in ("sp", "gp", "ve", "act"):
            em.wait(eng, "pe_s", em.val("pe_s"))
            em.wait(eng, "pe_av", em.val("pe_av"))
            em.wait(eng, "pe_ct", em.val("pe_ct"))
            em.wait(eng, "ve_cs", em.val("ve_cs"))
            em.wait(eng, "ve_ct2", em.val("ve_ct2"))
            for nm in ("sp_kt0", "sp_kt1", "gp_kt0", "gp_kt1", "gp_vh0", "gp_vh1",
                       "gp_qt", "gp_cs"):
                em.wait(eng, nm, em.val(nm))
        em.wait("pe", "gp_wffn", em.val("gp_wffn"))
        xn_slots = [psA[:, 0:128], psA[:, 512:640], psB[:, 1024:1152], psC[:, 0:128]]
        for t0 in (0, 1):
            em.op("sp", lambda E, t=t0: E.dma_start(
                xres[:, (t % 2) * D:(t % 2) * D + D],
                x_d[t * 128:t * 128 + 128, :]), inc="sp_xr%d" % (t0 % 2), n=16)
            em.op("gp", lambda E, t=t0: E.dma_start(
                cres[:, (t % 2) * NH * 65:(t % 2) * NH * 65 + NH * 65].rearrange(
                    "p (h j) -> p h j", j=65), ctx_r[t]), inc="gp_cr%d" % (t0 % 2), n=16)

        def ln1_ve(t):
                xb = xres[:, (t % 2) * D:(t % 2) * D + D]
                cb = cres[:, (t % 2) * NH * 65:(t % 2) * NH * 65 + NH * 65]
                c2 = cres2[:, (t % 2) * D:(t % 2) * D + D]
                rc = rcpt[:, (t % 2) * NH:(t % 2) * NH + NH]
                if t >= 2:
                    em.wait("sp", "ve_z", em.val("ve_z") - 1)
                    em.op("sp", lambda E, xb=xb, t=t: E.dma_start(
                        xb, x_d[t * 128:t * 128 + 128, :]),
                        inc="sp_xr%d" % (t % 2), n=16)
                    em.wait("gp", "ve_rcp", em.val("ve_rcp") - 1)
                    em.op("gp", lambda E, cb=cb, t=t: E.dma_start(
                        cb.rearrange("p (h j) -> p h j", j=65), ctx_r[t]),
                        inc="gp_cr%d" % (t % 2), n=16)
                em.wait("ve", "sp_xr%d" % (t % 2), em.val("sp_xr%d" % (t % 2)))
                em.wait("ve", "gp_cr%d" % (t % 2), em.val("gp_cr%d" % (t % 2)))
                cbh = cb.rearrange("p (h j) -> p h j", j=65)
                if t >= 2:
                    em.wait("ve", "ve_z", em.val("ve_z") - 1)
                em.op("ve", lambda E, cbh=cbh, rc=rc: E.reciprocal(
                    rc.rearrange("p (h o) -> p h o", o=1), cbh[:, :, 64:65]),
                    inc="ve_rcp")
                em.wait("ve", "ve_rcp", em.val("ve_rcp"))
                rc_b = bass.AP(rcpt, (t % 2) * NH, [[2 * NH, 128], [1, NH], [0, 64]])
                c2h = c2.rearrange("p (h j) -> p h j", j=64)
                em.op("ve", lambda E, c2h=c2h, cbh=cbh, rb=rc_b: E.tensor_mul(
                    c2h, cbh[:, :, 0:64], rb), inc="ve_rcp")
                em.wait("ve", "ve_rcp", em.val("ve_rcp"))
                zo = zbuf[:, (t % 4) * D:(t % 4) * D + D]
                em.op("ve", lambda E, xb=xb, c2=c2, zo=zo: E.scalar_tensor_tensor(
                    zo, xb, 1.0, c2, ALU.mult, ALU.add), inc="ve_z")

        def ln1_pe(t):
                xo = zbuf[:, (t % 4) * D:(t % 4) * D + D]
                em.wait("pe", "ve_ln", lnval_at[t])
                for dc in range(DC):
                    k = t * DC + dc
                    sl = xn_slots[k % 4]
                    ecnt = "evac_xnv" if k % 2 == 0 else "evac_xna"
                    if k >= 4:
                        em.wait("pe", ecnt, em.val(ecnt) - 1)
                    em.op("pe", lambda E, s=sl, dc=dc, xo=xo: E.transpose(
                        s, xo[:, dc * 128:dc * 128 + 128], ident[:, :]), inc="pe_xn")
                    eng = "ve" if dc % 2 == 0 else "act"
                    em.wait(eng, "pe_xn", em.val("pe_xn"))
                    dst = xnT[:, dc * NQ + t * 128: dc * NQ + t * 128 + 128]
                    if eng == "ve":
                        em.op("ve", lambda E, s=sl, d=dst: E.tensor_copy(d, s),
                              inc=ecnt)
                    else:
                        em.op("act", lambda E, s=sl, d=dst: E.copy(d, s),
                              inc=ecnt)

        lnval_at = {}

        def ln1_zios(b):
            tiles = list(range(4 * b, min(4 * b + 4, NTT)))
            return tiles, [(zbuf[:, (t % 4) * D:(t % 4) * D + D],
                            zbuf[:, (t % 4) * D:(t % 4) * D + D]) for t in tiles]

        def ln1_batch_ve(b):
            tiles, zios = ln1_zios(b)
            em.wait("ve", "pe_xn", em.val("pe_xn"))
            for t in tiles:
                ln1_ve(t)
            em.wait("ve", "ve_z", em.val("ve_z"))
            emit_ln_stats(zios, 0)

        def ln1_batch_finish(b):
            tiles, zios = ln1_zios(b)
            emit_ln_finish(zios, 0)
            for t in tiles:
                lnval_at[t] = em.val("ve_ln")

        def ln1_batch_pe(b):
            em.wait("pe", "ve_z2", em.val("ve_z2"))
            for t in range(4 * b, min(4 * b + 4, NTT)):
                ln1_pe(t)

        NTC = NQ // 512
        mm_slots = [psB[:, 0:512], psB[:, 512:1024]]
        tslots = [psA[:, 1024:1152], psT[:, 0:128], psB[:, 1024:1152], psC[:, 0:128]]
        ln1_batch_ve(0)
        ln1_batch_finish(0)
        ln1_batch_pe(0)
        for tc in range(NTC):
            if tc + 1 < NTC:
                ln1_batch_ve(tc + 1)
            em.wait("pe", "evac_xnv", em.val("evac_xnv"))
            em.wait("pe", "evac_xna", em.val("evac_xna"))
            em.wait("pe", "act_y", em.val("act_y"))
            for ht in range(NHT):
                mi = tc * NHT + ht
                m = mm_slots[mi % 2]
                if mi >= 2:
                    em.wait("pe", "act_h", em.val("act_h") - 1)
                for dc in range(DC):
                    em.op("pe", lambda E, m=m, dc=dc, ht=ht, tc=tc: E.matmul(
                        m, w1_sb[:, dc * HF + ht * 128: dc * HF + ht * 128 + 128],
                        xnT[:, dc * NQ + tc * 512: dc * NQ + tc * 512 + 512],
                        start=(dc == 0), stop=(dc == DC - 1)),
                        inc=("pe_h" if dc == DC - 1 else None))
                em.wait("act", "pe_h", em.val("pe_h"))
                em.op("act", lambda E, m=m, ht=ht: E.activation(
                    hT[:, ht * 512:ht * 512 + 512], m, AF.Relu,
                    bias=b1_sb[:, ht:ht + 1], scale=1.0), inc="act_h")
            if tc + 1 < NTC:
                ln1_batch_finish(tc + 1)

            def z2_transposes(dt, zslot):
                em.wait("pe", "ve_y", em.val("ve_y"))
                for st4 in range(4):
                    k = dt * 4 + st4
                    sl = tslots[k % 4]
                    if True:
                        em.wait("pe", "ve_z2", max(0, em.val("ve_z2") - 3))
                    em.op("pe", lambda E, sl=sl, st4=st4, z=zslot: E.transpose(
                        sl, z[:, st4 * 128:st4 * 128 + 128], ident[:, :]), inc="pe_z2")
                    em.wait("ve", "pe_z2", em.val("pe_z2"))
                    em.op("ve", lambda E, sl=sl, st4=st4, dt=dt: E.tensor_copy(
                        z2n[:, st4 * D + dt * 128: st4 * D + dt * 128 + 128], sl),
                        inc="ve_z2")

            em.wait("pe", "evac_xnv", em.val("evac_xnv"))
            em.wait("pe", "evac_xna", em.val("evac_xna"))
            pending_z = None
            for dt in range(NJT):
                mi2 = tc * NJT + dt
                m = mm_slots[mi2 % 2]
                zslot = z2T[:, (dt % 2) * 512:(dt % 2) * 512 + 512]
                em.wait("pe", "act_h", em.val("act_h"))
                if mi2 >= 2:
                    em.wait("pe", "act_y", em.val("act_y") - 1)
                for ht in range(NHT):
                    em.op("pe", lambda E, m=m, ht=ht, dt=dt: E.matmul(
                        m, w2_sb[:, ht * D + dt * 128: ht * D + dt * 128 + 128],
                        hT[:, ht * 512:ht * 512 + 512],
                        start=(ht == 0), stop=(ht == NHT - 1)),
                        inc=("pe_y" if ht == NHT - 1 else None))
                if pending_z is not None:
                    z2_transposes(*pending_z)
                em.wait("act", "pe_y", em.val("pe_y"))
                em.wait("act", "pe_z2", max(0, em.val("pe_z2") - 4))
                em.op("act", lambda E, m=m, dt=dt, z=zslot: E.activation(
                    z, m, AF.Identity, bias=b2_sb[:, dt:dt + 1], scale=1.0),
                    inc="act_y")
                em.wait("ve", "act_y", em.val("act_y"))
                em.op("ve", lambda E, dt=dt, tc=tc, z=zslot: E.tensor_tensor(
                    z, z, xnT[:, dt * NQ + tc * 512: dt * NQ + tc * 512 + 512], ALU.add),
                    inc="ve_y")
                pending_z = (dt, zslot)
            z2_transposes(*pending_z)
            em.wait("ve", "ve_z2", em.val("ve_z2"))
            emit_ln_batch([(z2n[:, st4 * D:st4 * D + D], None) for st4 in range(4)], 16)
            for st4 in range(4):
                oi = tc * 4 + st4
                c0 = lns[:, 16 + 4 * st4:16 + 4 * st4 + 1]
                c1 = lns[:, 16 + 4 * st4 + 1:16 + 4 * st4 + 2]
                ob = ostage[:, (oi % 2) * D:(oi % 2) * D + D]
                if oi >= 2:
                    em.wait("ve", "sp_out%d" % (oi % 2), em.val("sp_out%d" % (oi % 2)))
                em.op("ve", lambda E, st4=st4, ob=ob, c0=c0, c1=c1: E.tensor_scalar(
                    ob, z2n[:, st4 * D:st4 * D + D], c0, c1,
                    ALU.subtract, ALU.mult), inc="ve_ln")
                em.wait("sp", "ve_ln", em.val("ve_ln"))
                em.op("sp", lambda E, ob=ob, tc=tc, st4=st4: E.dma_start(
                    out_d[tc * 512 + st4 * 128: tc * 512 + st4 * 128 + 128, :], ob),
                    inc="sp_out%d" % (oi % 2), n=16)
            if tc + 1 < NTC:
                ln1_batch_pe(tc + 1)
        if "xn" in debug:
            for dc in range(DC):
                em.wait("gp", "evac_xnv", em.val("evac_xnv"))
                em.wait("gp", "evac_xna", em.val("evac_xna"))
                em.op("gp", lambda E, dc=dc: E.dma_start(
                    dbg["xn"][dc * 128:(dc + 1) * 128, :],
                    xnT[:, dc * NQ:(dc + 1) * NQ]), inc="sp_dbgx", n=16)
        em.wait("sp", "sp_out0", em.val("sp_out0"))
        em.wait("sp", "sp_out1", em.val("sp_out1"))

    emit_phase_w()
    emit_phase_p()
    emit_phase_s()
    emit_phase_a()
    emit_phase_f()

    with nc.Block() as blk:
        em.run(blk)
    return nc


_NC_CACHE = {}


def _get_nc(key, cfg, debug=()):
    if key not in _NC_CACHE:
        _NC_CACHE[key] = build(cfg, debug=debug)
    return _NC_CACHE[key]


def kernel(L0_attn_hidden_states, L0_attn_qk_w, L0_attn_qk_b, L0_attn_v_w,
           L0_attn_v_b, L0_attn_hash_a, L0_ffn_w_gate, L0_ffn_w_noise,
           L0_ffn_W1, L0_ffn_b1, L0_ffn_W2, L0_ffn_b2):
    x = np.ascontiguousarray(np.asarray(L0_attn_hidden_states, dtype=np.float32))
    Bsz, L, D = x.shape
    cfg = CFG(L=L, NH=D // 64, HF=np.asarray(L0_ffn_W1).shape[2])
    nc = _get_nc(("full", L, D), cfg)

    com = {
        "qk_w": np.ascontiguousarray(L0_attn_qk_w, dtype=np.float32),
        "qk_b": np.ascontiguousarray(L0_attn_qk_b, dtype=np.float32),
        "v_w": np.ascontiguousarray(L0_attn_v_w, dtype=np.float32),
        "v_b": np.ascontiguousarray(L0_attn_v_b, dtype=np.float32),
        "hash_a": np.ascontiguousarray(L0_attn_hash_a, dtype=np.float32),
        "w1": np.ascontiguousarray(np.asarray(L0_ffn_W1)[1], dtype=np.float32),
        "b1": np.ascontiguousarray(np.asarray(L0_ffn_b1)[1], dtype=np.float32),
        "w2": np.ascontiguousarray(np.asarray(L0_ffn_W2)[1], dtype=np.float32),
        "b2": np.ascontiguousarray(np.asarray(L0_ffn_b2)[1], dtype=np.float32),
    }
    NQ = L // 2
    in_maps = []
    for c in range(8):
        b = c // 2
        if c % 2 == 0:
            xb = x[b]
        else:
            xb = np.concatenate([x[b, NQ:], x[b, :NQ]], axis=0)
        m = dict(com)
        m["x"] = np.ascontiguousarray(xb)
        in_maps.append(m)

    res = run_bass_kernel_spmd(nc, in_maps, core_ids=list(range(8)))
    out = np.empty((Bsz, L, D), dtype=np.float32)
    for c in range(8):
        b = c // 2
        r = res.results[c]["out"]
        if c % 2 == 0:
            out[b, :NQ] = r
        else:
            out[b, NQ:] = r
    return out
